# version 01
# baseline (speedup 1.0000x reference)
import numpy as np
import ml_dtypes
from contextlib import ExitStack

import concourse.bass as bass
import concourse.mybir as mybir
from concourse.bass_utils import run_bass_kernel_spmd

F32 = mybir.dt.float32
BF16 = mybir.dt.bfloat16
AF = mybir.ActivationFunctionType
ALU = mybir.AluOpType
AX = mybir.AxisListType

D = 1024
FC = 8
SEQ = 2048
NSEQ = 2
NTOK = SEQ * NSEQ
TT = 256
NTILE = NTOK // TT
TPS = SEQ // TT
NHEAD = 16
HD = 64
QS = 0.125

ENGS = ("pe", "dve", "act", "pool", "sp")
CHAINS = ("pe", "dve", "act", "pool", "qs", "qp", "qa")

WTS = {
    "w_in": ("L0_attn_w_in", 3072, 1024, 512),
    "w_out": ("L0_attn_w_out", 1024, 1024, 1024),
    "w1": ("L0_ffn_w1_w", 4096, 1024, 512),
    "w2": ("L0_ffn_w2_w", 4096, 1024, 512),
    "h1": ("L0_ffn_h1_w", 4096, 4096, 256),
    "h2": ("L0_ffn_h2_w", 4096, 4096, 256),
    "w3": ("L0_ffn_w3_w", 1024, 4096, 256),
    "ew1": ("L1_ffn_w1", 8192, 1024, 512),
    "ew3": ("L1_ffn_w3", 8192, 1024, 512),
    "ew2": ("L1_ffn_w2", 8192, 1024, 512),
    "sw1": ("L1_ffn_sw1", 2048, 1024, 512),
    "sw3": ("L1_ffn_sw3", 2048, 1024, 512),
    "sw2": ("L1_ffn_sw2", 1024, 2048, 512),
}

COLVECS = {
    "b_in": ("L0_attn_b_in", 3072),
    "b_out": ("L0_attn_b_out", 1024),
    "w1s": ("L0_ffn_w1_s", 4096),
    "w2s": ("L0_ffn_w2_s", 4096),
    "h1s": ("L0_ffn_h1_s", 4096),
    "h2s": ("L0_ffn_h2_s", 4096),
    "w3s": ("L0_ffn_w3_s", 1024),
}


class B:

    def __init__(self, nc, block, sems):
        self.nc = nc
        self.block = block
        self.h = sems
        self.c = {k: 0 for k in CHAINS}
        self.ops = {e: [] for e in ENGS}
        self.pend = {e: [] for e in ENGS}

    def wait(self, eng, sem, val):
        if val > 0:
            self.pend[eng].append((sem, val))

    def wt(self, eng, tok):
        if tok is not None:
            self.wait(eng, tok[0], tok[1])

    def _emit(self, eng, fn, inc=None, by=1):
        waits = self.pend[eng]
        self.pend[eng] = []
        tok = None
        if inc is not None:
            self.c[inc] += by
            tok = (inc, self.c[inc])
        self.ops[eng].append((waits, fn, inc, by))
        return tok

    def pe(self, fn, inc=False):
        return self._emit("pe", fn, "pe" if inc else None, 1)

    def dve(self, fn):
        self.wait("dve", "dve", self.c["dve"])
        return self._emit("dve", fn, "dve", 1)

    def act(self, fn):
        self.wait("act", "act", self.c["act"])
        return self._emit("act", fn, "act", 1)

    def pool(self, fn):
        self.wait("pool", "pool", self.c["pool"])
        return self._emit("pool", fn, "pool", 1)

    def dma(self, eng, fn):
        sem = {"sp": "qs", "pool": "qp", "act": "qa"}[eng]
        return self._emit(eng, fn, sem, 16)

    def raw(self, eng, fn):
        return self._emit(eng, fn, None)

    def tok(self, sem):
        return (sem, self.c[sem])

    def barrier(self, engines=ENGS, sems=CHAINS):
        snap = [(s, self.c[s]) for s in sems if self.c[s] > 0]
        for e in engines:
            for s, v in snap:
                self.wait(e, s, v)

    def n_ops(self):
        return sum(len(v) for v in self.ops.values())

    def finalize(self):
        h = self.h

        def mk(eng):
            def body(e):
                for waits, fn, inc, by in self.ops[eng]:
                    for s, v in waits:
                        e.wait_ge(h[s], v)
                    inst = fn(e)
                    if inc is not None:
                        inst.then_inc(h[inc], by)
            return body
        self.block.tensor(mk("pe"))
        self.block.vector(mk("dve"))
        self.block.scalar(mk("act"))
        self.block.gpsimd(mk("pool"))
        self.block.sync(mk("sp"))


class Arena:
    def __init__(self, f32_ap, bf16_ap, nbytes):
        self.vf = f32_ap
        self.vb = bf16_ap
        self.n = nbytes
        self.off = 0
        self.stack = []
        self.peak = 0

    def push(self):
        self.stack.append(self.off)

    def pop(self):
        self.off = self.stack.pop()

    def _take(self, nbytes):
        off = (self.off + 31) & ~31
        self.off = off + nbytes
        self.peak = max(self.peak, self.off)
        assert self.off <= self.n, f"SBUF arena overflow: {self.off} > {self.n}"
        return off

    def f32(self, cols):
        off = self._take(cols * 4)
        return self.vf[:, off // 4: off // 4 + cols]

    def bf16(self, cols):
        off = self._take(cols * 2)
        return self.vb[:, off // 2: off // 2 + cols]


def build_program(phases=("l0a", "l0f", "l1a", "l1f")):
    nc = bass.Bass("TRN2", target_bir_lowering=False)

    ishapes = {
        "x": [NSEQ, SEQ, D],
        "L0_attn_w_in": [3072, 1024], "L0_attn_b_in": [3072],
        "L0_attn_w_out": [1024, 1024], "L0_attn_b_out": [1024],
        "L0_ffn_w1_w": [4096, 1024], "L0_ffn_w1_s": [4096],
        "L0_ffn_w2_w": [4096, 1024], "L0_ffn_w2_s": [4096],
        "L0_ffn_h1_w": [4096, 4096], "L0_ffn_h1_s": [4096],
        "L0_ffn_h2_w": [4096, 4096], "L0_ffn_h2_s": [4096],
        "L0_ffn_w3_w": [1024, 4096], "L0_ffn_w3_s": [1024],
        "L1_attn_g": [512, 1024],
        "L1_ffn_gate_w": [8, 1024], "L1_ffn_gate_bias": [8],
        "L1_ffn_w1": [8192, 1024], "L1_ffn_w3": [8192, 1024],
        "L1_ffn_w2": [8192, 1024],
        "L1_ffn_sw1": [2048, 1024], "L1_ffn_sw3": [2048, 1024],
        "L1_ffn_sw2": [1024, 2048],
        "msk": [128, 3 * 384],
        "pwc": [128, 3 * 128],
        "onehot": [8, 8 * 128],
    }
    inp = {k: nc.dram_tensor(k, s, F32, kind="ExternalInput").ap()
           for k, s in ishapes.items()}
    out = nc.dram_tensor("out", [NSEQ, SEQ, D], F32, kind="ExternalOutput").ap()

    cast_bf = {w: nc.dram_tensor("cb_" + w, [v[1], v[2]], BF16,
                                 kind="Internal").ap()
               for w, v in WTS.items()}
    cast_bf["g"] = nc.dram_tensor("cb_g", [512, 1024], BF16, kind="Internal").ap()
    wT = {w: nc.dram_tensor("wT_" + w, [v[1] // v[3], 128, v[2] // 128, v[3]],
                            BF16, kind="Internal").ap()
          for w, v in WTS.items()}

    es = ExitStack()
    ARENA_F32 = 52800
    arena_t = es.enter_context(nc.sbuf_tensor("arena", [128, ARENA_F32], F32))
    A = Arena(arena_t.ap(), arena_t.bitcast(BF16).ap(), ARENA_F32 * 4)
    banks = [es.enter_context(nc.psum_tensor(f"pbank{i}", [128, 512], F32))
             for i in range(8)]

    def bk(i, lo=0, hi=512):
        return banks[i][:, lo:hi]

    def bkb(i, lo=0, hi=1024):
        return banks[i].bitcast(BF16)[:, lo:hi]

    sems = {s: es.enter_context(nc.semaphore("sem_" + s)) for s in CHAINS}
    block = es.enter_context(nc.Block())
    b = B(nc, block, sems)

    with es:
        ident_f = A.f32(128)
        ident_b = A.bf16(128)
        ones_cf = A.f32(1)
        ones_cb = A.bf16(16)
        ones_row = A.f32(128)
        ccols = A.f32(8)
        cols = {"b_in": A.f32(24), "b_inq": A.f32(8), "b_out": A.f32(8),
                "w1s": A.f32(32), "w2s": A.f32(32), "h1s": A.f32(32),
                "h2s": A.f32(32), "w3s": A.f32(8)}
        stream = A.f32(FC * NTOK)
        st3 = stream.rearrange("p (fc t) -> p fc t", fc=FC)

        def ST(fc, t0, n):
            return st3[:, fc, t0:t0 + n]

        nc.const_aps.register(F32, 1e-5, ccols[:, 5:6])

        b.pool(lambda e: e.memset(ident_f, 0.0))
        b.pool(lambda e: e.affine_select(
            out=ident_f, in_=ident_f, compare_op=ALU.not_equal, fill=1.0,
            base=0, pattern=[[-1, 128]], channel_multiplier=1))
        b.pool(lambda e: e.memset(ident_b, 0.0))
        b.pool(lambda e: e.affine_select(
            out=ident_b, in_=ident_b, compare_op=ALU.not_equal, fill=1.0,
            base=0, pattern=[[-1, 128]], channel_multiplier=1))
        b.pool(lambda e: e.memset(ones_cf, 1.0))
        b.pool(lambda e: e.memset(ones_cb[:, 0:1], 1.0))
        b.pool(lambda e: e.memset(ones_row, 0.0))
        b.pool(lambda e: e.memset(ones_row[0:1, :], 1.0))
        for i, v in enumerate([QS, 0.5, 4.0, 0.4, -0.5, 1e-5, 1.0, 0.0]):
            b.pool(lambda e, i=i, v=v: e.memset(ccols[:, i:i + 1], v))
        t_consts = b.tok("pool")

        cast_toks = {}
        for w, (src, O, I, OW) in WTS.items():
            cast_toks[w] = b.dma(
                "pool", lambda e, w=w, src=src:
                e.dma_start(out=cast_bf[w], in_=inp[src]))
        cast_toks["g"] = b.dma(
            "pool", lambda e: e.dma_start(out=cast_bf["g"],
                                          in_=inp["L1_attn_g"]))

        A.push()
        tstage = A.bf16(32 * 256)
        wT_tok = {}
        for w, (src, O, I, OW) in WTS.items():
            IC = I // 128
            b.wt("sp", cast_toks[w])
            for og in range(O // OW):
                last = None
                for ic in range(IC):
                    last = b.dma("sp", lambda e, w=w, og=og, ic=ic, OW=OW:
                                 e.dma_start(
                                     out=tstage[:, ic * OW:(ic + 1) * OW],
                                     in_=cast_bf[w][og * OW:(og + 1) * OW,
                                                    ic * 128:(ic + 1) * 128],
                                     transpose=True))
                b.wt("sp", last)
                st_t = b.dma("sp", lambda e, w=w, og=og, IC=IC, OW=OW:
                             e.dma_start(
                                 out=wT[w][og].rearrange("p ic ow -> p (ic ow)"),
                                 in_=tstage[:, 0:IC * OW]))
                b.wt("sp", st_t)
            wT_tok[w] = b.tok("qs")
        A.pop()

        def load_wt(dst, w, og):
            _, O, I, OW = WTS[w]
            IC = I // 128
            b.wt("sp", wT_tok[w])
            return b.dma("sp", lambda e, w=w, og=og, IC=IC, OW=OW, dst=dst:
                         e.dma_start(
                             out=dst[:, 0:IC * OW],
                             in_=wT[w][og].rearrange("p ic ow -> p (ic ow)")))

        A.push()
        vstage = A.f32(128)
        for name, (src, n) in COLVECS.items():
            nch = n // 128
            ld = b.dma("sp", lambda e, src=src, nch=nch:
                       e.dma_start(out=vstage[0:nch, :],
                                   in_=inp[src].rearrange("(c p) -> c p", p=128)))
            b.wt("pe", ld)
            b.wt("pe", t_consts)
            tk = b.pe(lambda e, nch=nch: e.transpose(
                bk(0, 0, nch), vstage[0:nch, :], ident_f[0:nch, 0:nch]),
                inc=True)
            b.wt("dve", tk)
            cp = b.dve(lambda e, name=name, nch=nch:
                       e.tensor_copy(cols[name][:, 0:nch], bk(0, 0, nch)))
            b.wt("sp", tk)
            b.wt("pe", cp)
        b.dve(lambda e: e.tensor_scalar_mul(cols["b_inq"], cols["b_in"][:, 0:8],
                                            QS))
        A.pop()

        A.push()
        xst = [A.f32(1024), A.f32(1024)]
        last_rd = [None, None]
        NCH = NTOK // 128
        for ch in range(NCH):
            bb, t0 = divmod(ch * 128, SEQ)
            buf = ch % 2
            b.wt("sp", last_rd[buf])
            ld = b.dma("sp", lambda e, bb=bb, t0=t0, buf=buf:
                       e.dma_start(out=xst[buf], in_=inp["x"][bb, t0:t0 + 128, :]))
            b.wt("pe", ld)
            b.wt("pe", t_consts)
            for half in range(2):
                bkid = half
                tk = None
                for j in range(4):
                    fc = half * 4 + j
                    tk = b.pe(lambda e, buf=buf, fc=fc, bkid=bkid, j=j:
                              e.transpose(bk(bkid, j * 128, (j + 1) * 128),
                                          xst[buf][:, fc * 128:(fc + 1) * 128],
                                          ident_f), inc=(j == 3))
                b.wt("dve", tk)
                ev = b.dve(lambda e, ch=ch, half=half, bkid=bkid: e.tensor_copy(
                    st3[:, half * 4:(half + 1) * 4, ch * 128:ch * 128 + 128],
                    bk(bkid).rearrange("p (f t) -> p f t", f=4)))
                b.wt("pe", ev)
            last_rd[buf] = b.tok("pe")
        A.pop()
        b.barrier()

        def cast_tile(dst_bf, t0, n=TT):
            src = st3[:, :, t0:t0 + n]
            d3 = dst_bf.rearrange("p (fc t) -> p fc t", fc=FC)[:, :, 0:n]
            return b.dve(lambda e: e.tensor_copy(d3, src))

        def ln_sweep():
            A.push()
            sqt = A.f32(TT)
            rows = A.f32(512)
            mt = None
            for ti in range(NTILE):
                t0 = ti * TT
                for fc in range(FC):
                    if mt is not None:
                        b.wt("dve", mt)
                    sq = b.dve(lambda e, fc=fc, t0=t0: e.tensor_mul(
                        sqt, ST(fc, t0, TT), ST(fc, t0, TT)))
                    b.wt("pe", sq)
                    b.pe(lambda e, fc=fc, t0=t0: e.matmul(
                        bk(0, 0, TT)[0:1, :], ones_cf, ST(fc, t0, TT),
                        start=(fc == 0), stop=(fc == FC - 1)))
                    mt = b.pe(lambda e, fc=fc: e.matmul(
                        bk(1, 0, TT)[0:1, :], ones_cf, sqt,
                        start=(fc == 0), stop=(fc == FC - 1)), inc=True)
                b.wt("dve", mt)
                b.dve(lambda e: e.tensor_scalar_mul(
                    rows[0:1, 0:TT], bk(0, 0, TT)[0:1, :], 1.0 / D))
                b.dve(lambda e: e.tensor_mul(
                    rows[0:1, 256:512], rows[0:1, 0:TT], rows[0:1, 0:TT]))
                vv = b.dve(lambda e: e.scalar_tensor_tensor(
                    rows[0:1, 256:512], bk(1, 0, TT)[0:1, :], 1.0 / D,
                    rows[0:1, 256:512], op0=ALU.mult, op1=ALU.subtract))
                b.wt("act", vv)
                b.act(lambda e: e.activation(rows[0:1, 256:512],
                                             rows[0:1, 256:512], AF.Ln,
                                             bias=1e-5))
                l2 = b.act(lambda e: e.activation(rows[0:1, 256:512],
                                                  rows[0:1, 256:512], AF.Exp,
                                                  scale=-0.5))
                b.wt("pe", l2)
                bc = b.pe(lambda e: e.matmul(bk(2), ones_row[0:1, :],
                                             rows[0:1, 0:512],
                                             start=True, stop=True), inc=True)
                b.wt("dve", bc)
                lt = None
                for fc in range(FC):
                    b.dve(lambda e, fc=fc, t0=t0: e.tensor_sub(
                        sqt, ST(fc, t0, TT), bk(2, 0, TT)))
                    lt = b.dve(lambda e, fc=fc, t0=t0: e.tensor_mul(
                        ST(fc, t0, TT), sqt, bk(2, 256, 512)))
                b.wt("pe", lt)
            A.pop()
            b.barrier()

        def _ph_l0a():
            A.push()
            msk_b = A.bf16(3 * 384)
            w_inT = A.bf16(8 * 512)
            w_outT = A.bf16(8 * 1024)
            k_ring = A.bf16(3 * FC * TT)
            v_ring = A.bf16(3 * 2 * NHEAD * HD)
            q_ring = A.bf16(2 * FC * TT)
            x_bf = A.bf16(FC * TT)
            ctx_bf = A.bf16(FC * TT)
            s_sb = A.f32(384)
            p_bf = A.bf16(384)
            pT_bf = A.bf16(384)
            vstg = A.bf16(TT)
            sumc = A.f32(1)
            rinv = A.f32(1)
            otmp = A.f32(TT)

            b.dma("pool", lambda e: e.dma_start(out=msk_b[:, :], in_=inp["msk"]))
            b.pool(lambda e: e.memset(k_ring, 0.0))
            b.pool(lambda e: e.memset(v_ring, 0.0))
            b.pool(lambda e: e.memset(q_ring, 0.0))
            wo_ld = load_wt(w_outT, "w_out", 0)
            b.barrier()

            def qsl(sl, fc, lo, n):
                base = sl * 2048 + fc * TT + lo
                return q_ring[:, base:base + n]

            def ksl(sl, fc, lo, n):
                base = sl * 2048 + fc * TT + lo
                return k_ring[:, base:base + n]

            def vsl(sl, th, h):
                base = sl * 2048 + th * 1024 + h * HD
                return v_ring[:, base:base + HD]

            for s in range(NSEQ):
                def qkv(i, s=s):
                    sl = i % 3
                    qid = i % 2
                    t0 = s * SEQ + i * TT
                    b.wait("dve", "pe", b.c["pe"])
                    ct = cast_tile(x_bf, t0)
                    wl = [None]
                    for oc in range(24):
                        og, ol = divmod(oc, 4)
                        if ol == 0:
                            b.wait("sp", "pe", b.c["pe"])
                            wl[0] = load_wt(w_inT, "w_in", og)
                        b.wt("pe", wl[0])
                        b.wt("pe", ct)
                        mmt = None
                        for fc in range(FC):
                            mmt = b.pe(lambda e, ol=ol, fc=fc: e.matmul(
                                bk(0, 0, TT),
                                w_inT[:, fc * 512 + ol * 128:
                                      fc * 512 + ol * 128 + 128],
                                x_bf[:, fc * TT:(fc + 1) * TT],
                                start=(fc == 0), stop=(fc == FC - 1)),
                                inc=(fc == FC - 1))
                        b.wt("act", mmt)
                        b.wait("act", "pe", b.c["pe"])
                        if oc < 8:
                            ev = b.act(lambda e, oc=oc, qid=qid: e.activation(
                                qsl(qid, oc, 0, TT), bk(0, 0, TT), AF.Identity,
                                bias=cols["b_inq"][:, oc:oc + 1], scale=QS))
                        elif oc < 16:
                            ev = b.act(lambda e, oc=oc, sl=sl: e.activation(
                                ksl(sl, oc - 8, 0, TT), bk(0, 0, TT),
                                AF.Identity, bias=cols["b_in"][:, oc:oc + 1]))
                        else:
                            ev = b.act(lambda e, oc=oc: e.activation(
                                vstg, bk(0, 0, TT), AF.Identity,
                                bias=cols["b_in"][:, oc:oc + 1]))
                            c = oc - 16
                            for th in range(2):
                                b.wt("pe", ev)
                                tr = b.pe(lambda e, th=th: e.transpose(
                                    bkb(1, th * 128, (th + 1) * 128),
                                    vstg[:, th * 128:(th + 1) * 128],
                                    ident_b), inc=True)
                                b.wt("dve", tr)
                                b.wait("dve", "pe", b.c["pe"])
                                ev2 = b.dve(lambda e, th=th, sl=sl, c=c:
                                            e.tensor_copy(
                                    v_ring[:, sl * 2048 + th * 1024 + c * 128:
                                           sl * 2048 + th * 1024 + c * 128 + 128],
                                    bkb(1, th * 128, (th + 1) * 128)))
                                b.wt("pe", ev2)
                        b.wt("pe", ev)

                def attn(i, s=s):
                    slp, slc, sln = (i - 1) % 3, i % 3, (i + 1) % 3
                    qid = i % 2
                    b.wait("pe", "act", b.c["act"])
                    b.wait("pe", "dve", b.c["dve"])
                    for qb in range(2):
                        qg = i * 2 + qb
                        mv = 0 if qg == 0 else (2 if qg == 2 * TPS - 1 else 1)
                        for h in range(NHEAD):
                            fcq = h // 2
                            r0 = 64 * (h % 2)
                            if qb == 0:
                                kp = [(slp, 128, 128), (slc, 0, 256)]
                                vp = [(slp, 1), (slc, 0), (slc, 1)]
                            else:
                                kp = [(slc, 0, 128), (slc, 128, 256 - 128),
                                      (sln, 0, 128)]
                                vp = [(slc, 0), (slc, 1), (sln, 0)]
                            qAP = q_ring[r0:r0 + 64,
                                         qid * 2048 + fcq * TT + qb * 128:
                                         qid * 2048 + fcq * TT + qb * 128 + 128]
                            co = 0
                            mmt = None
                            for (ksl_, klo, kn) in kp:
                                mmt = b.pe(lambda e, ksl_=ksl_, klo=klo, kn=kn,
                                           co=co, qAP=qAP, r0=r0, fcq=fcq:
                                           e.matmul(
                                    bk(0, co, co + kn), qAP,
                                    k_ring[r0:r0 + 64,
                                           ksl_ * 2048 + fcq * TT + klo:
                                           ksl_ * 2048 + fcq * TT + klo + kn],
                                    start=True, stop=True),
                                    inc=(co + kn == 384))
                                co += kn
                            b.wt("dve", mmt)
                            ma = b.dve(lambda e, mv=mv: e.tensor_add(
                                s_sb, bk(0, 0, 384),
                                msk_b[:, mv * 384:(mv + 1) * 384]))
                            b.wt("pe", ma)
                            b.wt("act", ma)
                            ex = b.act(lambda e: e.activation(
                                s_sb, s_sb, AF.Exp, accum_out=sumc))
                            b.wt("dve", ex)
                            b.dve(lambda e: e.reciprocal(rinv, sumc))
                            pb = b.dve(lambda e: e.tensor_scalar_mul(
                                p_bf, s_sb, rinv[:, 0:1]))
                            b.wt("pe", pb)
                            trs = None
                            for j in range(3):
                                trs = b.pe(lambda e, j=j: e.transpose(
                                    bkb(1, j * 128, (j + 1) * 128),
                                    p_bf[:, j * 128:(j + 1) * 128], ident_b),
                                    inc=(j == 2))
                            b.wt("dve", trs)
                            pt = b.dve(lambda e: e.tensor_copy(pT_bf,
                                                               bkb(1, 0, 384)))
                            b.wt("pe", pt)
                            cm = None
                            for j, (vsl_, vth) in enumerate(vp):
                                cm = b.pe(lambda e, vsl_=vsl_, vth=vth, h=h,
                                          j=j: e.matmul(
                                    bk(2, 0, 128)[0:64, :], vsl(vsl_, vth, h),
                                    pT_bf[:, j * 128:(j + 1) * 128],
                                    start=(j == 0), stop=(j == 2)),
                                    inc=(j == 2))
                            b.wt("dve", cm)
                            cev = b.dve(lambda e, fcq=fcq, r0=r0, qb=qb:
                                        e.tensor_copy(
                                ctx_bf[r0:r0 + 64, fcq * TT + qb * 128:
                                       fcq * TT + qb * 128 + 128],
                                bk(2, 0, 128)[0:64, :]))
                            b.wt("pe", cev)

                def outproj(i, s=s):
                    t0 = s * SEQ + i * TT
                    b.wait("pe", "dve", b.c["dve"])
                    b.wt("pe", wo_ld)
                    b.wait("pe", "qs", min(b.c["qs"], wo_ld[1] + 16 * 200))
                    for oc in range(FC):
                        mmt = None
                        for fc in range(FC):
                            mmt = b.pe(lambda e, oc=oc, fc=fc: e.matmul(
                                bk(0, 0, TT),
                                w_outT[:, fc * 1024 + oc * 128:
                                       fc * 1024 + oc * 128 + 128],
                                ctx_bf[:, fc * TT:(fc + 1) * TT],
                                start=(fc == 0), stop=(fc == FC - 1)),
                                inc=(fc == FC - 1))
                        b.wt("dve", mmt)
                        b.dve(lambda e, oc=oc, t0=t0: e.scalar_tensor_tensor(
                            otmp, bk(0, 0, TT), cols["b_out"][:, oc:oc + 1],
                            ST(oc, t0, TT), op0=ALU.add, op1=ALU.add))
                        ad = b.dve(lambda e, oc=oc, t0=t0: e.tensor_add(
                            ST(oc, t0, TT), ST(oc, t0, TT), otmp))
                        b.wt("pe", ad)

                qkv(0)
                qkv(1)
                for i in range(TPS):
                    attn(i)
                    outproj(i)
                    if i + 2 < TPS:
                        qkv(i + 2)
            A.pop()
            b.barrier()
            ln_sweep()

        def _ph_l0f():
            A.push()
            h_buf = A.bf16(32 * TT)
            u_buf = A.bf16(32 * TT)
            x_bf = A.bf16(FC * TT)
            wbuf = A.bf16(32 * 256)
            silu_t = A.f32(TT)
            for ti in range(NTILE):
                t0 = ti * TT
                b.wait("dve", "pe", b.c["pe"])
                ct = cast_tile(x_bf, t0)
                for og in range(8):
                    b.wait("sp", "pe", b.c["pe"])
                    lw1 = load_wt(wbuf[:, 0:4096], "w1", og)
                    lw2 = load_wt(wbuf[:, 4096:8192], "w2", og)
                    for ol in range(4):
                        oc = og * 4 + ol
                        b.wt("pe", lw1)
                        b.wt("pe", lw2)
                        b.wt("pe", ct)
                        ta = None
                        for fc in range(FC):
                            ta = b.pe(lambda e, ol=ol, fc=fc: e.matmul(
                                bk(0, 0, TT),
                                wbuf[:, fc * 512 + ol * 128:
                                     fc * 512 + ol * 128 + 128],
                                x_bf[:, fc * TT:(fc + 1) * TT],
                                start=(fc == 0), stop=(fc == FC - 1)),
                                inc=(fc == FC - 1))
                        tb = None
                        for fc in range(FC):
                            tb = b.pe(lambda e, ol=ol, fc=fc: e.matmul(
                                bk(1, 0, TT),
                                wbuf[:, 4096 + fc * 512 + ol * 128:
                                     4096 + fc * 512 + ol * 128 + 128],
                                x_bf[:, fc * TT:(fc + 1) * TT],
                                start=(fc == 0), stop=(fc == FC - 1)),
                                inc=(fc == FC - 1))
                        b.wt("act", ta)
                        sa = b.act(lambda e, oc=oc: e.activation(
                            silu_t, bk(0, 0, TT), AF.Silu,
                            scale=cols["w1s"][:, oc:oc + 1]))
                        b.wt("dve", tb)
                        b.wt("dve", sa)
                        hv = b.dve(lambda e, oc=oc: e.scalar_tensor_tensor(
                            h_buf[:, oc * TT:(oc + 1) * TT], bk(1, 0, TT),
                            cols["w2s"][:, oc:oc + 1], silu_t,
                            op0=ALU.mult, op1=ALU.mult))
                        b.wt("pe", sa)
                        b.wt("pe", hv)
                b.wait("pe", "dve", b.c["dve"])
                b.wait("pe", "act", b.c["act"])
                for og in range(16):
                    b.wait("sp", "pe", b.c["pe"])
                    lh = load_wt(wbuf, "h1", og)
                    for ol in range(2):
                        oc = og * 2 + ol
                        b.wt("pe", lh)
                        tu = None
                        for ic in range(32):
                            tu = b.pe(lambda e, ol=ol, ic=ic: e.matmul(
                                bk(0, 0, TT),
                                wbuf[:, ic * 256 + ol * 128:
                                     ic * 256 + ol * 128 + 128],
                                h_buf[:, ic * TT:(ic + 1) * TT],
                                start=(ic == 0), stop=(ic == 31)),
                                inc=(ic == 31))
                        b.wt("act", tu)
                        ug = b.act(lambda e, oc=oc: e.activation(
                            u_buf[:, oc * TT:(oc + 1) * TT], bk(0, 0, TT),
                            AF.Gelu, scale=cols["h1s"][:, oc:oc + 1]))
                        b.wt("pe", ug)
                b.wait("pe", "act", b.c["act"])
                for og in range(16):
                    b.wait("sp", "pe", b.c["pe"])
                    lh = load_wt(wbuf, "h2", og)
                    for ol in range(2):
                        oc = og * 2 + ol
                        b.wt("pe", lh)
                        tv = None
                        for ic in range(32):
                            tv = b.pe(lambda e, ol=ol, ic=ic: e.matmul(
                                bk(0, 0, TT),
                                wbuf[:, ic * 256 + ol * 128:
                                     ic * 256 + ol * 128 + 128],
                                u_buf[:, ic * TT:(ic + 1) * TT],
                                start=(ic == 0), stop=(ic == 31)),
                                inc=(ic == 31))
                        b.wt("act", tv)
                        vg = b.act(lambda e, oc=oc: e.activation(
                            h_buf[:, oc * TT:(oc + 1) * TT], bk(0, 0, TT),
                            AF.Gelu, scale=cols["h2s"][:, oc:oc + 1]))
                        b.wt("pe", vg)
                b.wait("pe", "act", b.c["act"])
                for og in range(4):
                    b.wait("sp", "pe", b.c["pe"])
                    lw3 = load_wt(wbuf, "w3", og)
                    for ol in range(2):
                        oc = og * 2 + ol
                        b.wt("pe", lw3)
                        ty = None
                        for ic in range(32):
                            ty = b.pe(lambda e, ol=ol, ic=ic: e.matmul(
                                bk(0, 0, TT),
                                wbuf[:, ic * 256 + ol * 128:
                                     ic * 256 + ol * 128 + 128],
                                h_buf[:, ic * TT:(ic + 1) * TT],
                                start=(ic == 0), stop=(ic == 31)),
                                inc=(ic == 31))
                        b.wt("dve", ty)
                        yd = b.dve(lambda e, oc=oc, t0=t0:
                                   e.scalar_tensor_tensor(
                            ST(oc, t0, TT), bk(0, 0, TT),
                            cols["w3s"][:, oc:oc + 1], ST(oc, t0, TT),
                            op0=ALU.mult, op1=ALU.add))
                        b.wt("pe", yd)
            A.pop()
            b.barrier()
            ln_sweep()

        def _ph_l1a():
            A.push()
            pwc_f = A.f32(3 * 128)
            G_tm = A.bf16(4 * 1024)
            GhatT = A.bf16(8 * 512)
            Gh3 = GhatT.rearrange("p (fc g) -> p fc g", fc=FC)
            scr4 = A.f32(1024)
            gh_tmp = A.bf16(1024)
            d_bf = A.bf16(8 * 128)
            s_sb = A.f32(512)
            gw_bf = A.bf16(512)
            gwT_sb = A.bf16(512)
            infl_bf = A.bf16(8 * 128)
            nrm_t = A.bf16(128)
            bl_ring = A.bf16(2 * 1024)
            C_cur = A.bf16(128)
            C_prev = A.bf16(128)
            stg = A.f32(128)
            traj = A.f32(1024)
            cb = A.f32(16)
            (C_MAG, C_E4, C_T, C_MW0, C_MW1, C_MX, C_REC, C_I2, C_I03,
             C_SUM, C_RI, C_W, C_IW, C_S2, C_GN) = range(15)

            def col(i):
                return cb[:, i:i + 1]

            pld = b.dma("pool", lambda e: e.dma_start(out=pwc_f[:, :],
                                                      in_=inp["pwc"]))
            b.pool(lambda e: e.memset(stg, 0.0))
            b.wt("sp", cast_toks["g"])
            gld = b.dma("sp", lambda e: e.dma_start(
                out=G_tm.rearrange("p (gc f) -> p gc f", gc=4),
                in_=cast_bf["g"].rearrange("(gc p) f -> p gc f", p=128)))
            b.wt("act", gld)
            for gc in range(4):
                sq = b.act(lambda e, gc=gc: e.activation(
                    scr4, G_tm[:, gc * 1024:(gc + 1) * 1024], AF.Square,
                    accum_out=col(C_S2)))
                b.wt("act", sq)
                b.act(lambda e: e.activation(col(C_GN), col(C_S2), AF.Ln))
                gn = b.act(lambda e: e.activation(col(C_GN), col(C_GN),
                                                  AF.Exp, scale=0.5))
                b.wt("dve", gn)
                b.dve(lambda e: e.tensor_scalar_max(col(C_MX), col(C_GN), 1e-8))
                rc = b.dve(lambda e: e.reciprocal(col(C_REC), col(C_MX)))
                gh = b.dve(lambda e, gc=gc: e.tensor_scalar_mul(
                    gh_tmp, G_tm[:, gc * 1024:(gc + 1) * 1024],
                    col(C_REC)[:, 0:1]))
                b.wt("pe", gh)
                tk = None
                for fc in range(FC):
                    tk = b.pe(lambda e, fc=fc: e.transpose(
                        bkb(0, fc * 128, (fc + 1) * 128),
                        gh_tmp[:, fc * 128:(fc + 1) * 128], ident_b),
                        inc=(fc == FC - 1))
                b.wt("dve", tk)
                ev = b.dve(lambda e, gc=gc: e.tensor_copy(
                    Gh3[:, :, gc * 128:(gc + 1) * 128],
                    bkb(0).rearrange("p (f g) -> p f g", f=FC)))
                b.wt("pe", ev)
                b.wt("act", ev)
            b.barrier()

            for s in range(NSEQ):
                for ci in range(16):
                    t0 = s * SEQ + ci * 128
                    par = ci % 2
                    nd = 127 if ci == 15 else 128
                    b.wait("dve", "pe", b.c["pe"])
                    dts = []
                    for fc in range(FC):
                        dts.append(b.dve(lambda e, fc=fc, t0=t0, nd=nd:
                                         e.tensor_sub(
                            d_bf[:, fc * 128:fc * 128 + nd],
                            ST(fc, t0 + 1, nd), ST(fc, t0, nd))))
                    if ci == 15:
                        b.wait("pool", "dve", b.c["dve"])
                        for fc in range(FC):
                            b.pool(lambda e, fc=fc: e.memset(
                                d_bf[:, fc * 128 + 127:fc * 128 + 128], 0.0))
                    t_pool = b.tok("pool")
                    b.wt("dve", t_pool)
                    mt = None
                    for fc in range(FC):
                        if mt is not None:
                            b.wt("dve", mt)
                        sq = b.dve(lambda e, fc=fc: e.tensor_mul(
                            scr4[:, 0:128], d_bf[:, fc * 128:(fc + 1) * 128],
                            d_bf[:, fc * 128:(fc + 1) * 128]))
                        b.wt("pe", sq)
                        mt = b.pe(lambda e, fc=fc: e.matmul(
                            bk(0, 0, 128)[0:1, :], ones_cf, scr4[:, 0:128],
                            start=(fc == 0), stop=(fc == FC - 1)), inc=True)
                    b.wt("dve", mt)
                    cp = b.dve(lambda e: e.tensor_copy(stg[0:1, :],
                                                       bk(0, 0, 128)[0:1, :]))
                    b.wt("pe", cp)
                    tr1 = b.pe(lambda e: e.transpose(bk(1, 0, 128), stg,
                                                     ident_f), inc=True)
                    b.wt("act", tr1)
                    b.act(lambda e: e.activation(col(C_MAG), bk(1, 0, 128)[:, 0:1],
                                                 AF.Ln))
                    mg = b.act(lambda e: e.activation(col(C_MAG), col(C_MAG),
                                                      AF.Exp, scale=0.5))
                    e4 = b.act(lambda e: e.activation(col(C_E4), col(C_MAG),
                                                      AF.Exp, scale=4.0))
                    b.wt("dve", e4)
                    b.dve(lambda e: e.tensor_scalar_add(col(C_T), col(C_E4), 1.0))
                    b.dve(lambda e: e.reciprocal(col(C_T), col(C_T)))
                    mwc = col(C_MW0) if par == 0 else col(C_MW1)
                    b.dve(lambda e, mwc=mwc: e.tensor_scalar(
                        mwc, col(C_T), -2.0, 1.0, op0=ALU.mult, op1=ALU.add))
                    b.dve(lambda e: e.tensor_scalar_max(col(C_MX), col(C_MAG),
                                                        1e-6))
                    b.dve(lambda e: e.reciprocal(col(C_REC), col(C_MX)))
                    b.dve(lambda e: e.tensor_scalar_mul(col(C_I2), col(C_REC),
                                                        2.0))
                    iv3 = b.dve(lambda e: e.tensor_scalar_mul(
                        col(C_I03), col(C_REC), 0.6))
                    b.wt("pe", iv3)
                    b.wt("pe", t_pool)
                    dd = None
                    for fc in range(FC):
                        dd = b.pe(lambda e, fc=fc: e.matmul(
                            bk(2, 0, 512), d_bf[:, fc * 128:(fc + 1) * 128],
                            GhatT[:, fc * 512:(fc + 1) * 512],
                            start=(fc == 0), stop=(fc == FC - 1)),
                            inc=(fc == FC - 1))
                    b.wt("dve", dd)
                    sm = b.dve(lambda e: e.tensor_scalar_mul(
                        s_sb, bk(2, 0, 512), col(C_I2)[:, 0:1]))
                    b.wt("act", sm)
                    ex = b.act(lambda e: e.activation(s_sb, s_sb, AF.Exp,
                                                      accum_out=col(C_SUM)))
                    b.wt("dve", ex)
                    b.dve(lambda e: e.reciprocal(col(C_RI), col(C_SUM)))
                    gw = b.dve(lambda e: e.tensor_scalar_mul(
                        gw_bf, s_sb, col(C_RI)[:, 0:1]))
                    b.wt("pe", gw)
                    tk = None
                    for j in range(4):
                        tk = b.pe(lambda e, j=j: e.transpose(
                            bkb(3, j * 128, (j + 1) * 128),
                            gw_bf[:, j * 128:(j + 1) * 128], ident_b),
                            inc=(j == 3))
                    b.wt("dve", tk)
                    gt = b.dve(lambda e: e.tensor_copy(gwT_sb, bkb(3, 0, 512)))
                    b.wt("pe", gt)
                    ift = None
                    for fc in range(FC):
                        for gc in range(4):
                            ift = b.pe(lambda e, fc=fc, gc=gc: e.matmul(
                                bk(4, 0, 128),
                                G_tm[:, gc * 1024 + fc * 128:
                                     gc * 1024 + fc * 128 + 128],
                                gwT_sb[:, gc * 128:(gc + 1) * 128],
                                start=(gc == 0), stop=(gc == 3)),
                                inc=(gc == 3))
                        b.wt("act", ift)
                        fe = b.act(lambda e, fc=fc: e.activation(
                            infl_bf[:, fc * 128:(fc + 1) * 128], bk(4, 0, 128),
                            AF.Identity, scale=0.4))
                        b.wt("pe", fe)
                    for fc in range(FC):
                        trd = b.pe(lambda e, fc=fc: e.transpose(
                            bkb(5, 0, 128), d_bf[:, fc * 128:(fc + 1) * 128],
                            ident_b), inc=True)
                        b.wt("dve", trd)
                        nt = b.dve(lambda e: e.tensor_scalar_mul(
                            nrm_t, bkb(5, 0, 128), col(C_I03)[:, 0:1]))
                        b.wt("pe", nt)
                        tri = b.pe(lambda e, fc=fc: e.transpose(
                            bkb(5, 128, 256), infl_bf[:, fc * 128:(fc + 1) * 128],
                            ident_b), inc=True)
                        b.wt("dve", tri)
                        bl = b.dve(lambda e, fc=fc, par=par: e.tensor_add(
                            bl_ring[:, par * 1024 + fc * 128:
                                    par * 1024 + fc * 128 + 128],
                            nrm_t, bkb(5, 128, 256)))
                        b.wt("pe", bl)
                    b.wt("dve", pld)
                    pv = 0 if ci == 0 else 1
                    cc = b.dve(lambda e, pv=pv, mwc=mwc: e.tensor_scalar_mul(
                        C_cur, pwc_f[:, pv * 128:(pv + 1) * 128], mwc[:, 0:1]))
                    cpv = None
                    if ci > 0:
                        mwp = col(C_MW1) if par == 0 else col(C_MW0)
                        cpv = b.dve(lambda e, mwp=mwp: e.tensor_scalar_mul(
                            C_prev, pwc_f[:, 256:384], mwp[:, 0:1]))
                    b.wt("pe", cc)
                    b.wt("pe", cpv)
                    ws = b.pe(lambda e, ci=ci: e.matmul(
                        bk(0, 0, 128)[0:1, :], ones_cb[:, 0:1], C_cur,
                        start=True, stop=(ci == 0)), inc=(ci == 0))
                    if ci > 0:
                        ws = b.pe(lambda e: e.matmul(
                            bk(0, 0, 128)[0:1, :], ones_cb[:, 0:1], C_prev,
                            start=False, stop=True), inc=True)
                    b.wt("dve", ws)
                    cp2 = b.dve(lambda e: e.tensor_copy(stg[0:1, :],
                                                        bk(0, 0, 128)[0:1, :]))
                    b.wt("pe", cp2)
                    tr2 = b.pe(lambda e: e.transpose(bk(1, 0, 128), stg,
                                                     ident_f), inc=True)
                    b.wt("dve", tr2)
                    b.dve(lambda e: e.tensor_scalar_max(col(C_W),
                                                        bk(1, 0, 128)[:, 0:1],
                                                        1e-8))
                    iw = b.dve(lambda e: e.reciprocal(col(C_IW), col(C_W)))
                    for fh in range(2):
                        tj = None
                        if ci > 0:
                            tj = b.pe(lambda e, fh=fh, par=par: e.matmul(
                                bk(6, 0, 512), C_prev,
                                bl_ring[:, (1 - par) * 1024 + fh * 512:
                                        (1 - par) * 1024 + fh * 512 + 512],
                                start=True, stop=False))
                        tj = b.pe(lambda e, fh=fh, par=par, ci=ci: e.matmul(
                            bk(6, 0, 512), C_cur,
                            bl_ring[:, par * 1024 + fh * 512:
                                    par * 1024 + fh * 512 + 512],
                            start=(ci == 0), stop=True), inc=True)
                        b.wt("dve", tj)
                        b.wt("dve", iw)
                        tv = b.dve(lambda e, fh=fh: e.tensor_scalar_mul(
                            traj[:, fh * 512:(fh + 1) * 512], bk(6, 0, 512),
                            col(C_IW)[:, 0:1]))
                        b.wt("pe", tv)
                    for fc in range(FC):
                        tt = b.pe(lambda e, fc=fc: e.transpose(
                            bk(7, 0, 128), traj[:, fc * 128:(fc + 1) * 128],
                            ident_f), inc=True)
                        b.wt("dve", tt)
                        ad = b.dve(lambda e, fc=fc, t0=t0: e.tensor_add(
                            ST(fc, t0, 128), ST(fc, t0, 128), bk(7, 0, 128)))
                        b.wt("pe", ad)
            A.pop()
            b.barrier()
            ln_sweep()

        def _ph_l1f():
            A.push()
            oneh = A.f32(1024)
            gwTv = A.f32(64)
            gbias = A.f32(8)
            gb_row = A.f32(8)
            gw_nat = A.f32(1024)
            x_bf = A.bf16(FC * TT)
            wpool = A.bf16(16 * 512)
            h_bf = A.bf16(16 * TT)
            g_bc = A.f32(8 * TT)
            y_sb = A.f32(8 * TT)
            gat = A.f32(NTILE * 2 * 8)
            gT_sb = A.f32(128)
            silu_t = A.f32(TT)
            t2 = A.f32(TT)
            s8 = A.f32(8)
            sb8 = A.f32(8)
            i1 = A.f32(8)
            i2 = A.f32(8)
            mk8 = A.f32(8)
            c8 = A.f32(4)
            z_tm = A.f32(1024)
            obuf = A.f32(1024)

            old = b.dma("pool", lambda e: e.dma_start(out=oneh[0:8, :],
                                                      in_=inp["onehot"]))
            l1 = b.dma("sp", lambda e: e.dma_start(
                out=gb_row[0:1, :],
                in_=inp["L1_ffn_gate_bias"].rearrange("(a b) -> a b", a=1)))
            l2 = b.dma("sp", lambda e: e.dma_start(out=gw_nat[0:8, :],
                                                   in_=inp["L1_ffn_gate_w"]))
            b.wt("pe", l1)
            b.wt("pe", l2)
            tk = b.pe(lambda e: e.matmul(bk(1, 0, 8), ones_row[0:1, :],
                                         gb_row[0:1, :], start=True, stop=True),
                      inc=True)
            b.wt("dve", tk)
            cp = b.dve(lambda e: e.tensor_copy(gbias, bk(1, 0, 8)))
            b.wt("pe", l2)
            b.wt("pe", cp)
            for fc in range(FC):
                tk = b.pe(lambda e, fc=fc: e.transpose(
                    bk(2, fc * 8, fc * 8 + 8),
                    gw_nat[0:8, fc * 128:(fc + 1) * 128], ident_f[0:8, 0:8]),
                    inc=(fc == FC - 1))
            b.wt("dve", tk)
            cp2 = b.dve(lambda e: e.tensor_copy(gwTv, bk(2, 0, 64)))
            b.wt("pe", cp2)
            b.barrier()

            for tch in range(NTILE * 2):
                t0 = tch * 128
                lg = None
                for fc in range(FC):
                    lg = b.pe(lambda e, fc=fc, t0=t0: e.matmul(
                        bk(0, 0, 8), ST(fc, t0, 128),
                        gwTv[:, fc * 8:(fc + 1) * 8],
                        start=(fc == 0), stop=(fc == FC - 1)),
                        inc=(fc == FC - 1))
                b.wt("act", lg)
                b.wait("act", "dve", b.c["dve"])
                ex = b.act(lambda e: e.activation(s8, bk(0, 0, 8), AF.Exp,
                                                  accum_out=c8[:, 0:1]))
                b.wt("pe", ex)
                b.wt("dve", ex)
                b.dve(lambda e: e.reciprocal(c8[:, 1:2], c8[:, 0:1]))
                b.dve(lambda e: e.tensor_scalar_mul(s8, s8, c8[:, 1:2]))
                b.dve(lambda e: e.tensor_add(sb8, s8, gbias))
                b.dve(lambda e: e.reduce_max(c8[:, 2:3], sb8, axis=AX.X))
                b.dve(lambda e: e.tensor_scalar(i1, sb8, c8[:, 2:3], None,
                                                op0=ALU.is_ge))
                b.dve(lambda e: e.scalar_tensor_tensor(
                    mk8, i1, -1e9, sb8, op0=ALU.mult, op1=ALU.add))
                b.dve(lambda e: e.reduce_max(c8[:, 3:4], mk8, axis=AX.X))
                b.dve(lambda e: e.tensor_scalar(i2, mk8, c8[:, 3:4], None,
                                                op0=ALU.is_ge))
                b.dve(lambda e: e.tensor_add(i1, i1, i2))
                b.dve(lambda e, tch=tch: e.tensor_mul(
                    gat[:, tch * 8:(tch + 1) * 8], s8, i1))

            prev_store = None
            for ti in range(NTILE):
                t0 = ti * TT
                b.wait("dve", "pe", b.c["pe"])
                ct = cast_tile(x_bf, t0)
                b.wait("pool", "dve", b.c["dve"])
                ym = b.pool(lambda e: e.memset(y_sb, 0.0))
                for tc in range(2):
                    tch = ti * 2 + tc
                    b.wait("pe", "dve", b.c["dve"])
                    tk = b.pe(lambda e, tch=tch: e.transpose(
                        bk(1, 0, 128)[0:8, :], gat[:, tch * 8:(tch + 1) * 8],
                        ident_f), inc=True)
                    b.wt("dve", tk)
                    cp = b.dve(lambda e: e.tensor_copy(gT_sb[0:8, :],
                                                       bk(1, 0, 128)[0:8, :]))
                    b.wt("pe", cp)
                    for e_ in range(8):
                        bc = b.pe(lambda e, e_=e_: e.matmul(
                            bk(2, 0, 128), oneh[0:8, e_ * 128:(e_ + 1) * 128],
                            gT_sb[0:8, :], start=True, stop=True), inc=True)
                        b.wt("dve", bc)
                        ev = b.dve(lambda e, e_=e_, tc=tc: e.tensor_copy(
                            g_bc[:, e_ * TT + tc * 128:e_ * TT + tc * 128 + 128],
                            bk(2, 0, 128)))
                        b.wt("pe", ev)
                for e_ in range(8):
                    for half in range(2):
                        og = e_ * 2 + half
                        b.wait("sp", "pe", b.c["pe"])
                        lw1 = load_wt(wpool[:, 0:4096], "ew1", og)
                        lw3 = load_wt(wpool[:, 4096:8192], "ew3", og)
                        for ol in range(4):
                            oc = half * 4 + ol
                            b.wt("pe", lw1)
                            b.wt("pe", lw3)
                            b.wt("pe", ct)
                            ta = None
                            for fc in range(FC):
                                ta = b.pe(lambda e, ol=ol, fc=fc: e.matmul(
                                    bk(0, 0, TT),
                                    wpool[:, fc * 512 + ol * 128:
                                          fc * 512 + ol * 128 + 128],
                                    x_bf[:, fc * TT:(fc + 1) * TT],
                                    start=(fc == 0), stop=(fc == FC - 1)),
                                    inc=(fc == FC - 1))
                            tb = None
                            for fc in range(FC):
                                tb = b.pe(lambda e, ol=ol, fc=fc: e.matmul(
                                    bk(1, 0, TT),
                                    wpool[:, 4096 + fc * 512 + ol * 128:
                                          4096 + fc * 512 + ol * 128 + 128],
                                    x_bf[:, fc * TT:(fc + 1) * TT],
                                    start=(fc == 0), stop=(fc == FC - 1)),
                                    inc=(fc == FC - 1))
                            b.wt("act", ta)
                            sa = b.act(lambda e: e.activation(silu_t,
                                                              bk(0, 0, TT),
                                                              AF.Silu))
                            b.wt("dve", tb)
                            b.wt("dve", sa)
                            b.dve(lambda e: e.tensor_mul(t2, silu_t,
                                                         bk(1, 0, TT)))
                            hv = b.dve(lambda e, oc=oc, e_=e_: e.tensor_mul(
                                h_bf[:, oc * TT:(oc + 1) * TT], t2,
                                g_bc[:, e_ * TT:(e_ + 1) * TT]))
                            b.wt("pe", sa)
                            b.wt("pe", hv)
                    b.wait("pe", "dve", b.c["dve"])
                    for half in range(2):
                        og = e_ * 2 + half
                        b.wait("sp", "pe", b.c["pe"])
                        lw2 = load_wt(wpool[:, 0:4096], "ew2", og)
                        for ol in range(4):
                            oc = half * 4 + ol
                            b.wt("pe", lw2)
                            ty = None
                            for ic in range(FC):
                                ty = b.pe(lambda e, ol=ol, ic=ic: e.matmul(
                                    bk(0, 0, TT),
                                    wpool[:, ic * 512 + ol * 128:
                                          ic * 512 + ol * 128 + 128],
                                    h_bf[:, ic * TT:(ic + 1) * TT],
                                    start=(ic == 0), stop=(ic == FC - 1)),
                                    inc=(ic == FC - 1))
                            b.wt("dve", ty)
                            b.wt("dve", ym)
                            ya = b.dve(lambda e, oc=oc: e.tensor_add(
                                y_sb[:, oc * TT:(oc + 1) * TT],
                                y_sb[:, oc * TT:(oc + 1) * TT], bk(0, 0, TT)))
                            b.wt("pe", ya)
                for og in range(4):
                    b.wait("sp", "pe", b.c["pe"])
                    lw1 = load_wt(wpool[:, 0:4096], "sw1", og)
                    lw3 = load_wt(wpool[:, 4096:8192], "sw3", og)
                    for ol in range(4):
                        oc = og * 4 + ol
                        b.wt("pe", lw1)
                        b.wt("pe", lw3)
                        b.wt("pe", ct)
                        ta = None
                        for fc in range(FC):
                            ta = b.pe(lambda e, ol=ol, fc=fc: e.matmul(
                                bk(0, 0, TT),
                                wpool[:, fc * 512 + ol * 128:
                                      fc * 512 + ol * 128 + 128],
                                x_bf[:, fc * TT:(fc + 1) * TT],
                                start=(fc == 0), stop=(fc == FC - 1)),
                                inc=(fc == FC - 1))
                        tb = None
                        for fc in range(FC):
                            tb = b.pe(lambda e, ol=ol, fc=fc: e.matmul(
                                bk(1, 0, TT),
                                wpool[:, 4096 + fc * 512 + ol * 128:
                                      4096 + fc * 512 + ol * 128 + 128],
                                x_bf[:, fc * TT:(fc + 1) * TT],
                                start=(fc == 0), stop=(fc == FC - 1)),
                                inc=(fc == FC - 1))
                        b.wt("act", ta)
                        sa = b.act(lambda e: e.activation(silu_t, bk(0, 0, TT),
                                                          AF.Silu))
                        b.wt("dve", tb)
                        b.wt("dve", sa)
                        hv = b.dve(lambda e, oc=oc: e.tensor_mul(
                            h_bf[:, oc * TT:(oc + 1) * TT], silu_t,
                            bk(1, 0, TT)))
                        b.wt("pe", sa)
                        b.wt("pe", hv)
                b.wait("pe", "dve", b.c["dve"])
                for og in range(2):
                    b.wait("sp", "pe", b.c["pe"])
                    lw2 = load_wt(wpool, "sw2", og)
                    for ol in range(4):
                        oc = og * 4 + ol
                        b.wt("pe", lw2)
                        ty = None
                        for ic in range(16):
                            ty = b.pe(lambda e, ol=ol, ic=ic: e.matmul(
                                bk(0, 0, TT),
                                wpool[:, ic * 512 + ol * 128:
                                      ic * 512 + ol * 128 + 128],
                                h_bf[:, ic * TT:(ic + 1) * TT],
                                start=(ic == 0), stop=(ic == 15)),
                                inc=(ic == 15))
                        b.wt("dve", ty)
                        ya = b.dve(lambda e, oc=oc: e.tensor_add(
                            y_sb[:, oc * TT:(oc + 1) * TT],
                            y_sb[:, oc * TT:(oc + 1) * TT], bk(0, 0, TT)))
                        b.wt("pe", ya)
                zt = None
                for fc in range(FC):
                    zt = b.dve(lambda e, fc=fc, t0=t0: e.tensor_add(
                        ST(fc, t0, TT), ST(fc, t0, TT),
                        y_sb[:, fc * TT:(fc + 1) * TT]))
                for tc in range(2):
                    tt0 = t0 + tc * 128
                    bb, pos = divmod(tt0, SEQ)
                    b.wait("pe", "dve", b.c["dve"])
                    b.wt("pe", prev_store)
                    for half in range(2):
                        tk = None
                        for j in range(4):
                            fc = half * 4 + j
                            tk = b.pe(lambda e, fc=fc, tt0=tt0, half=half, j=j:
                                      e.transpose(
                                bk(half, j * 128, (j + 1) * 128),
                                ST(fc, tt0, 128), ident_f), inc=(j == 3))
                        b.wt("dve", tk)
                        ev = b.dve(lambda e, half=half: e.tensor_copy(
                            z_tm.rearrange("p (f t) -> p f t", f=FC)
                            [:, half * 4:(half + 1) * 4, :],
                            bk(half).rearrange("p (f t) -> p f t", f=4)))
                        b.wt("pe", ev)
                    b.dve(lambda e: e.reduce_sum(c8[:, 0:1], z_tm, axis=AX.X))
                    b.dve(lambda e: e.tensor_scalar_mul(c8[:, 0:1], c8[:, 0:1],
                                                        1.0 / D))
                    b.dve(lambda e: e.tensor_scalar_sub(z_tm, z_tm,
                                                        c8[:, 0:1]))
                    b.wt("dve", prev_store)
                    b.dve(lambda e: e.tensor_mul(obuf, z_tm, z_tm))
                    b.dve(lambda e: e.reduce_sum(c8[:, 1:2], obuf, axis=AX.X))
                    vr = b.dve(lambda e: e.tensor_scalar_mul(
                        c8[:, 1:2], c8[:, 1:2], 1.0 / D))
                    b.wt("act", vr)
                    b.act(lambda e: e.activation(c8[:, 1:2], c8[:, 1:2],
                                                 AF.Ln, bias=1e-5))
                    rs = b.act(lambda e: e.activation(c8[:, 1:2], c8[:, 1:2],
                                                      AF.Exp, scale=-0.5))
                    b.wt("dve", rs)
                    b.wt("dve", prev_store)
                    ob = b.dve(lambda e: e.tensor_scalar_mul(obuf, z_tm,
                                                             c8[:, 1:2]))
                    b.wait("pool", "dve", b.c["dve"])
                    prev_store = b.dma("pool", lambda e, bb=bb, pos=pos:
                                       e.dma_start(out=out[bb, pos:pos + 128, :],
                                                   in_=obuf))
            A.pop()

        def _ph_store():
            A.push()
            obuf = A.f32(1024)
            prev_store = None
            for ch in range(NTOK // 128):
                bb, pos = divmod(ch * 128, SEQ)
                b.wt("pe", prev_store)
                for half in range(2):
                    tk = None
                    for j in range(4):
                        fc = half * 4 + j
                        tk = b.pe(lambda e, fc=fc, ch=ch, half=half, j=j:
                                  e.transpose(
                            bk(half, j * 128, (j + 1) * 128),
                            ST(fc, ch * 128, 128), ident_f), inc=(j == 3))
                    b.wt("dve", tk)
                    b.wt("dve", prev_store)
                    ev = b.dve(lambda e, half=half: e.tensor_copy(
                        obuf.rearrange("p (f t) -> p f t", f=FC)
                        [:, half * 4:(half + 1) * 4, :],
                        bk(half).rearrange("p (f t) -> p f t", f=4)))
                    b.wt("pe", ev)
                b.wait("pool", "dve", b.c["dve"])
                prev_store = b.dma("pool", lambda e, bb=bb, pos=pos:
                                   e.dma_start(out=out[bb, pos:pos + 128, :],
                                               in_=obuf))
            A.pop()

        if "l0a" in phases:
            _ph_l0a()
        if "l0f" in phases:
            _ph_l0f()
        if "l1a" in phases:
            _ph_l1a()
        if "l1f" in phases:
            _ph_l1f()
        else:
            _ph_store()

        final_qp = b.c["qp"]
        b.raw("pool", lambda e: e.wait_ge(sems["qp"], final_qp))
        b.finalize()
    return nc, A.peak


def _host_consts():
    msk = np.full((128, 3, 384), -1e9, np.float32)
    qi = np.arange(128)[:, None]
    j = np.arange(384)[None, :]
    band = (j >= qi) & (j <= qi + 256)
    msk[:, 0][band & (j >= 128)] = 0.0
    msk[:, 1][band] = 0.0
    msk[:, 2][band & (j < 256)] = 0.0

    S, W = SEQ, 8
    pos = np.arange(S)[:, None]
    k = np.arange(W)[None, :]
    idx = pos - W + k
    valid = (idx >= 0) & (pos >= 1)
    w = np.minimum(pos, W)
    jj = k - (W - w)
    lin = np.where(w > 1, 0.1 + 0.9 * jj / np.maximum(w - 1, 1), 0.1)
    pw = np.where(valid, lin, 0.0).astype(np.float32)

    pwc = np.zeros((128, 3, 128), np.float32)
    i_ = np.arange(128)[:, None]
    j_ = np.arange(128)[None, :]
    kk = i_ - j_ + 8
    m = (kk >= 0) & (kk <= 7)
    kkc = np.clip(kk, 0, 7)
    jb = np.broadcast_to(j_, (128, 128))
    pwc[:, 0] = np.where(m, pw[jb, kkc], 0.0)
    pwc[:, 1] = np.where(m, pw[128 + jb, kkc], 0.0)
    kp = i_ - j_ - 120
    mp = (kp >= 0) & (kp <= 7)
    kpc = np.clip(kp, 0, 7)
    pwc[:, 2] = np.where(mp, pw[128 + jb, kpc], 0.0)

    oneh = np.zeros((8, 8, 128), np.float32)
    for e in range(8):
        oneh[e, e, :] = 1.0
    return msk.reshape(128, -1), pwc.reshape(128, -1), oneh.reshape(8, -1)


KPHASES = ("l0a", "l0f", "l1a", "l1f")
NCORES = 8
_CACHE = {}


def kernel(**inputs):
    key = tuple(KPHASES)
    if key not in _CACHE:
        _CACHE[key] = build_program(KPHASES)[0]
    nc = _CACHE[key]
    msk, pwc, oneh = _host_consts()
    in_maps = []
    for c in range(NCORES):
        m = {"x": np.ascontiguousarray(inputs["L0_attn_x"][2 * c:2 * c + 2])}
        for k in ("L0_attn_w_in", "L0_attn_b_in", "L0_attn_w_out",
                  "L0_attn_b_out", "L0_ffn_w1_w", "L0_ffn_w1_s",
                  "L0_ffn_w2_w", "L0_ffn_w2_s", "L0_ffn_h1_w", "L0_ffn_h1_s",
                  "L0_ffn_h2_w", "L0_ffn_h2_s", "L0_ffn_w3_w", "L0_ffn_w3_s",
                  "L1_ffn_gate_w", "L1_ffn_gate_bias",
                  "L1_ffn_sw1", "L1_ffn_sw3", "L1_ffn_sw2"):
            m[k] = np.ascontiguousarray(inputs[k])
        m["L1_attn_g"] = np.ascontiguousarray(
            inputs["L1_attn_guidance_trajectories"])
        m["L1_ffn_w1"] = np.ascontiguousarray(
            inputs["L1_ffn_w1"].reshape(8192, 1024))
        m["L1_ffn_w3"] = np.ascontiguousarray(
            inputs["L1_ffn_w3"].reshape(8192, 1024))
        m["L1_ffn_w2"] = np.ascontiguousarray(
            inputs["L1_ffn_w2"].reshape(8192, 1024))
        m["msk"] = msk
        m["pwc"] = pwc
        m["onehot"] = oneh
        in_maps.append(m)
    res = run_bass_kernel_spmd(nc, in_maps, core_ids=list(range(NCORES)))
    return np.concatenate([np.asarray(r["out"], np.float32)[None]
                           for r in res.results], axis=0).reshape(
                               2 * NCORES, SEQ, D)


# version 02
# speedup vs baseline: 1.5162x; 1.5162x over previous
import numpy as np
import ml_dtypes
from contextlib import ExitStack

import concourse.bass as bass
import concourse.mybir as mybir
from concourse.bass_utils import run_bass_kernel_spmd

F32 = mybir.dt.float32
BF16 = mybir.dt.bfloat16
AF = mybir.ActivationFunctionType
ALU = mybir.AluOpType
AX = mybir.AxisListType

D = 1024
FC = 8
SEQ = 2048
NSEQ = 2
NTOK = SEQ * NSEQ
TT = 256
NTILE = NTOK // TT
TPS = SEQ // TT
NHEAD = 16
HD = 64
QS = 0.125

ENGS = ("pe", "dve", "act", "pool", "sp")
CHAINS = ("pe", "dve", "act", "pool", "qs", "qp", "qa")

WTS = {
    "w_in": ("L0_attn_w_in", 3072, 1024, 512),
    "w_out": ("L0_attn_w_out", 1024, 1024, 1024),
    "w1": ("L0_ffn_w1_w", 4096, 1024, 512),
    "w2": ("L0_ffn_w2_w", 4096, 1024, 512),
    "h1": ("L0_ffn_h1_w", 4096, 4096, 256),
    "h2": ("L0_ffn_h2_w", 4096, 4096, 256),
    "w3": ("L0_ffn_w3_w", 1024, 4096, 256),
    "ew1": ("L1_ffn_w1", 8192, 1024, 512),
    "ew3": ("L1_ffn_w3", 8192, 1024, 512),
    "ew2": ("L1_ffn_w2", 8192, 1024, 512),
    "sw1": ("L1_ffn_sw1", 2048, 1024, 512),
    "sw3": ("L1_ffn_sw3", 2048, 1024, 512),
    "sw2": ("L1_ffn_sw2", 1024, 2048, 512),
}

COLVECS = {
    "b_in": ("L0_attn_b_in", 3072),
    "b_out": ("L0_attn_b_out", 1024),
    "w1s": ("L0_ffn_w1_s", 4096),
    "w2s": ("L0_ffn_w2_s", 4096),
    "h1s": ("L0_ffn_h1_s", 4096),
    "h2s": ("L0_ffn_h2_s", 4096),
    "w3s": ("L0_ffn_w3_s", 1024),
}


class B:

    def __init__(self, nc, block, sems):
        self.nc = nc
        self.block = block
        self.h = sems
        self.c = {k: 0 for k in CHAINS}
        self.ops = {e: [] for e in ENGS}
        self.pend = {e: [] for e in ENGS}

    def wait(self, eng, sem, val):
        if val > 0:
            self.pend[eng].append((sem, val))

    def wt(self, eng, tok):
        if tok is None:
            return
        if isinstance(tok, (list, tuple)) and tok and \
                not isinstance(tok[0], str):
            for t in tok:
                self.wt(eng, t)
            return
        self.wait(eng, tok[0], tok[1])

    def _emit(self, eng, fn, inc=None, by=1):
        waits = self.pend[eng]
        self.pend[eng] = []
        tok = None
        if inc is not None:
            self.c[inc] += by
            tok = (inc, self.c[inc])
        self.ops[eng].append((waits, fn, inc, by))
        return tok

    def pe(self, fn, inc=False):
        return self._emit("pe", fn, "pe" if inc else None, 1)

    def dve(self, fn):
        self.wait("dve", "dve", self.c["dve"])
        return self._emit("dve", fn, "dve", 1)

    def act(self, fn):
        self.wait("act", "act", self.c["act"])
        return self._emit("act", fn, "act", 1)

    def pool(self, fn):
        self.wait("pool", "pool", self.c["pool"])
        return self._emit("pool", fn, "pool", 1)

    def dma(self, eng, fn, sem=None):
        if sem is None:
            sem = {"sp": "qs", "pool": "qp", "act": "qa"}[eng]
        return self._emit(eng, fn, sem, 16)

    def raw(self, eng, fn):
        return self._emit(eng, fn, None)

    def tok(self, sem):
        return (sem, self.c[sem])

    def barrier(self, engines=ENGS, sems=CHAINS):
        snap = [(s, self.c[s]) for s in sems if self.c[s] > 0]
        for e in engines:
            for s, v in snap:
                self.wait(e, s, v)

    def n_ops(self):
        return sum(len(v) for v in self.ops.values())

    def finalize(self):
        h = self.h

        def mk(eng):
            def body(e):
                for waits, fn, inc, by in self.ops[eng]:
                    for s, v in waits:
                        e.wait_ge(h[s], v)
                    inst = fn(e)
                    if inc is not None:
                        inst.then_inc(h[inc], by)
            return body
        self.block.tensor(mk("pe"))
        self.block.vector(mk("dve"))
        self.block.scalar(mk("act"))
        self.block.gpsimd(mk("pool"))
        self.block.sync(mk("sp"))


class Arena:
    def __init__(self, f32_ap, bf16_ap, nbytes):
        self.vf = f32_ap
        self.vb = bf16_ap
        self.n = nbytes
        self.off = 0
        self.stack = []
        self.peak = 0

    def push(self):
        self.stack.append(self.off)

    def pop(self):
        self.off = self.stack.pop()

    def _take(self, nbytes):
        off = (self.off + 31) & ~31
        self.off = off + nbytes
        self.peak = max(self.peak, self.off)
        assert self.off <= self.n, f"SBUF arena overflow: {self.off} > {self.n}"
        return off

    def f32(self, cols):
        off = self._take(cols * 4)
        return self.vf[:, off // 4: off // 4 + cols]

    def bf16(self, cols):
        off = self._take(cols * 2)
        return self.vb[:, off // 2: off // 2 + cols]


def build_program(phases=("l0a", "l0f", "l1a", "l1f")):
    nc = bass.Bass("TRN2", target_bir_lowering=False)

    ishapes = {
        "x": [NSEQ, SEQ, D],
        "L0_attn_w_in": [3072, 1024], "L0_attn_b_in": [3072],
        "L0_attn_w_out": [1024, 1024], "L0_attn_b_out": [1024],
        "L0_ffn_w1_w": [4096, 1024], "L0_ffn_w1_s": [4096],
        "L0_ffn_w2_w": [4096, 1024], "L0_ffn_w2_s": [4096],
        "L0_ffn_h1_w": [4096, 4096], "L0_ffn_h1_s": [4096],
        "L0_ffn_h2_w": [4096, 4096], "L0_ffn_h2_s": [4096],
        "L0_ffn_w3_w": [1024, 4096], "L0_ffn_w3_s": [1024],
        "L1_attn_g": [512, 1024],
        "L1_ffn_gate_w": [8, 1024], "L1_ffn_gate_bias": [8],
        "L1_ffn_w1": [8192, 1024], "L1_ffn_w3": [8192, 1024],
        "L1_ffn_w2": [8192, 1024],
        "L1_ffn_sw1": [2048, 1024], "L1_ffn_sw3": [2048, 1024],
        "L1_ffn_sw2": [1024, 2048],
        "msk": [128, 384],
        "pwc": [128, 3 * 128],
        "onehot": [8, 8 * 128],
    }
    inp = {k: nc.dram_tensor(k, s, F32, kind="ExternalInput").ap()
           for k, s in ishapes.items()}
    out = nc.dram_tensor("out", [NSEQ, SEQ, D], F32, kind="ExternalOutput").ap()

    cast_bf = {w: nc.dram_tensor("cb_" + w, [v[1], v[2]], BF16,
                                 kind="Internal").ap()
               for w, v in WTS.items()}
    cast_bf["g"] = nc.dram_tensor("cb_g", [512, 1024], BF16, kind="Internal").ap()
    wT = {w: nc.dram_tensor("wT_" + w, [v[1] // v[3], 128, v[2] // 128, v[3]],
                            BF16, kind="Internal").ap()
          for w, v in WTS.items()}

    es = ExitStack()
    ARENA_F32 = 53100
    arena_t = es.enter_context(nc.sbuf_tensor("arena", [128, ARENA_F32], F32))
    A = Arena(arena_t.ap(), arena_t.bitcast(BF16).ap(), ARENA_F32 * 4)
    banks = [es.enter_context(nc.psum_tensor(f"pbank{i}", [128, 512], F32))
             for i in range(8)]

    def bk(i, lo=0, hi=512):
        return banks[i][:, lo:hi]

    def bkb(i, lo=0, hi=1024):
        return banks[i].bitcast(BF16)[:, lo:hi]

    sems = {s: es.enter_context(nc.semaphore("sem_" + s)) for s in CHAINS}
    block = es.enter_context(nc.Block())
    b = B(nc, block, sems)

    with es:
        ident_f = A.f32(128)
        ident_b = A.bf16(128)
        ones_cf = A.f32(1)
        ones_cb = A.bf16(16)
        ones_row = A.f32(128)
        ccols = A.f32(8)
        cols = {"b_in": A.f32(24), "b_inq": A.f32(8), "b_out": A.f32(8),
                "w1s": A.f32(32), "w2s": A.f32(32), "h1s": A.f32(32),
                "h2s": A.f32(32), "w3s": A.f32(8)}
        stream = A.f32(FC * NTOK)
        st3 = stream.rearrange("p (fc t) -> p fc t", fc=FC)

        def ST(fc, t0, n):
            return st3[:, fc, t0:t0 + n]

        nc.const_aps.register(F32, 1e-5, ccols[:, 5:6])

        b.pool(lambda e: e.memset(ident_f, 0.0))
        b.pool(lambda e: e.affine_select(
            out=ident_f, in_=ident_f, compare_op=ALU.not_equal, fill=1.0,
            base=0, pattern=[[-1, 128]], channel_multiplier=1))
        b.pool(lambda e: e.memset(ident_b, 0.0))
        b.pool(lambda e: e.affine_select(
            out=ident_b, in_=ident_b, compare_op=ALU.not_equal, fill=1.0,
            base=0, pattern=[[-1, 128]], channel_multiplier=1))
        b.pool(lambda e: e.memset(ones_cf, 1.0))
        b.pool(lambda e: e.memset(ones_cb[:, 0:1], 1.0))
        b.pool(lambda e: e.memset(ones_row, 0.0))
        b.pool(lambda e: e.memset(ones_row[0:1, :], 1.0))
        for i, v in enumerate([QS, 0.5, 4.0, 0.4, -0.5, 1e-5, 1.0, 0.0]):
            b.pool(lambda e, i=i, v=v: e.memset(ccols[:, i:i + 1], v))
        t_consts = b.tok("pool")

        cast_toks = {}
        for w, (src, O, I, OW) in WTS.items():
            cast_toks[w] = b.dma(
                "pool", lambda e, w=w, src=src:
                e.dma_start(out=cast_bf[w], in_=inp[src]))
        cast_toks["g"] = b.dma(
            "pool", lambda e: e.dma_start(out=cast_bf["g"],
                                          in_=inp["L1_attn_g"]))

        A.push()
        tstage = A.bf16(32 * 256)
        wT_tok = {}
        for w, (src, O, I, OW) in WTS.items():
            IC = I // 128
            b.wt("sp", cast_toks[w])
            for og in range(O // OW):
                last = None
                for ic in range(IC):
                    last = b.dma("sp", lambda e, w=w, og=og, ic=ic, OW=OW:
                                 e.dma_start(
                                     out=tstage[:, ic * OW:(ic + 1) * OW],
                                     in_=cast_bf[w][og * OW:(og + 1) * OW,
                                                    ic * 128:(ic + 1) * 128],
                                     transpose=True))
                b.wt("sp", last)
                st_t = b.dma("sp", lambda e, w=w, og=og, IC=IC, OW=OW:
                             e.dma_start(
                                 out=wT[w][og].rearrange("p ic ow -> p (ic ow)"),
                                 in_=tstage[:, 0:IC * OW]))
                b.wt("sp", st_t)
            wT_tok[w] = b.tok("qs")
        A.pop()

        def load_wt(dst, w, og, sem=None):
            _, O, I, OW = WTS[w]
            IC = I // 128
            b.wt("sp", wT_tok[w])
            return b.dma("sp", lambda e, w=w, og=og, IC=IC, OW=OW, dst=dst:
                         e.dma_start(
                             out=dst[:, 0:IC * OW],
                             in_=wT[w][og].rearrange("p ic ow -> p (ic ow)")),
                         sem=sem)

        A.push()
        vstage = A.f32(128)
        for name, (src, n) in COLVECS.items():
            nch = n // 128
            ld = b.dma("sp", lambda e, src=src, nch=nch:
                       e.dma_start(out=vstage[0:nch, :],
                                   in_=inp[src].rearrange("(c p) -> c p", p=128)))
            b.wt("pe", ld)
            b.wt("pe", t_consts)
            tk = b.pe(lambda e, nch=nch: e.transpose(
                bk(0, 0, nch), vstage[0:nch, :], ident_f[0:nch, 0:nch]),
                inc=True)
            b.wt("dve", tk)
            cp = b.dve(lambda e, name=name, nch=nch:
                       e.tensor_copy(cols[name][:, 0:nch], bk(0, 0, nch)))
            b.wt("sp", tk)
            b.wt("pe", cp)
        b.dve(lambda e: e.tensor_scalar_mul(cols["b_inq"], cols["b_in"][:, 0:8],
                                            QS))
        A.pop()

        A.push()
        xst = [A.f32(1024), A.f32(1024)]
        last_rd = [None, None]
        NCH = NTOK // 128
        for ch in range(NCH):
            bb, t0 = divmod(ch * 128, SEQ)
            buf = ch % 2
            b.wt("sp", last_rd[buf])
            ld = b.dma("sp", lambda e, bb=bb, t0=t0, buf=buf:
                       e.dma_start(out=xst[buf], in_=inp["x"][bb, t0:t0 + 128, :]))
            b.wt("pe", ld)
            b.wt("pe", t_consts)
            for half in range(2):
                bkid = half
                tk = None
                for j in range(4):
                    fc = half * 4 + j
                    tk = b.pe(lambda e, buf=buf, fc=fc, bkid=bkid, j=j:
                              e.transpose(bk(bkid, j * 128, (j + 1) * 128),
                                          xst[buf][:, fc * 128:(fc + 1) * 128],
                                          ident_f), inc=(j == 3))
                b.wt("dve", tk)
                ev = b.dve(lambda e, ch=ch, half=half, bkid=bkid: e.tensor_copy(
                    st3[:, half * 4:(half + 1) * 4, ch * 128:ch * 128 + 128],
                    bk(bkid).rearrange("p (f t) -> p f t", f=4)))
                b.wt("pe", ev)
            last_rd[buf] = b.tok("pe")
        A.pop()
        b.barrier()

        def cast_tile(dst_bf, t0, n=TT):
            src = st3[:, :, t0:t0 + n]
            d3 = dst_bf.rearrange("p (fc t) -> p fc t", fc=FC)[:, :, 0:n]
            return b.dve(lambda e: e.tensor_copy(d3, src))

        def ln_sweep():
            A.push()
            sqt = A.f32(TT)
            rows = A.f32(512)
            mt = None
            for ti in range(NTILE):
                t0 = ti * TT
                for fc in range(FC):
                    if mt is not None:
                        b.wt("dve", mt)
                    sq = b.dve(lambda e, fc=fc, t0=t0: e.tensor_mul(
                        sqt, ST(fc, t0, TT), ST(fc, t0, TT)))
                    b.wt("pe", sq)
                    b.pe(lambda e, fc=fc, t0=t0: e.matmul(
                        bk(0, 0, TT)[0:1, :], ones_cf, ST(fc, t0, TT),
                        start=(fc == 0), stop=(fc == FC - 1)))
                    mt = b.pe(lambda e, fc=fc: e.matmul(
                        bk(1, 0, TT)[0:1, :], ones_cf, sqt,
                        start=(fc == 0), stop=(fc == FC - 1)), inc=True)
                b.wt("dve", mt)
                b.dve(lambda e: e.tensor_scalar_mul(
                    rows[0:1, 0:TT], bk(0, 0, TT)[0:1, :], 1.0 / D))
                b.dve(lambda e: e.tensor_mul(
                    rows[0:1, 256:512], rows[0:1, 0:TT], rows[0:1, 0:TT]))
                vv = b.dve(lambda e: e.scalar_tensor_tensor(
                    rows[0:1, 256:512], bk(1, 0, TT)[0:1, :], 1.0 / D,
                    rows[0:1, 256:512], op0=ALU.mult, op1=ALU.subtract))
                b.wt("act", vv)
                b.act(lambda e: e.activation(rows[0:1, 256:512],
                                             rows[0:1, 256:512], AF.Ln,
                                             bias=1e-5))
                l2 = b.act(lambda e: e.activation(rows[0:1, 256:512],
                                                  rows[0:1, 256:512], AF.Exp,
                                                  scale=-0.5))
                b.wt("pe", l2)
                bc = b.pe(lambda e: e.matmul(bk(2), ones_row[0:1, :],
                                             rows[0:1, 0:512],
                                             start=True, stop=True), inc=True)
                b.wt("dve", bc)
                lt = None
                for fc in range(FC):
                    b.dve(lambda e, fc=fc, t0=t0: e.tensor_sub(
                        sqt, ST(fc, t0, TT), bk(2, 0, TT)))
                    lt = b.dve(lambda e, fc=fc, t0=t0: e.tensor_mul(
                        ST(fc, t0, TT), sqt, bk(2, 256, 512)))
                b.wt("pe", lt)
            A.pop()
            b.barrier()

        def _ph_l0a():
            A.push()
            msk_b = A.bf16(384)
            w_inT = [A.bf16(8 * 512), A.bf16(8 * 512)]
            w_outT = A.bf16(8 * 1024)
            k_ring = A.bf16(3 * FC * TT)
            v_ring = A.bf16(3 * 2 * NHEAD * HD)
            q_ring = A.bf16(2 * FC * TT)
            x_bf = A.bf16(FC * TT)
            ctx_bf = A.bf16(FC * TT)
            s_sb = A.f32(384)
            p_bf = A.bf16(384)
            pT_bf = A.bf16(384)
            vstg = A.bf16(TT)
            sumc = A.f32(1)
            rinv = A.f32(1)
            otmp = A.f32(TT)

            b.dma("pool", lambda e: e.dma_start(out=msk_b[:, :], in_=inp["msk"]))
            b.pool(lambda e: e.memset(k_ring, 0.0))
            b.pool(lambda e: e.memset(v_ring, 0.0))
            b.pool(lambda e: e.memset(q_ring, 0.0))
            wo_ld = load_wt(w_outT, "w_out", 0)
            b.barrier()

            def qsl(sl, fc, lo, n):
                base = sl * 2048 + fc * TT + lo
                return q_ring[:, base:base + n]

            def ksl(sl, fc, lo, n):
                base = sl * 2048 + fc * TT + lo
                return k_ring[:, base:base + n]

            def vsl(sl, th, h):
                base = sl * 2048 + th * 1024 + h * HD
                return v_ring[:, base:base + HD]

            for s in range(NSEQ):
                def qkv(i, s=s):
                    sl = i % 3
                    qid = i % 2
                    t0 = s * SEQ + i * TT
                    b.wait("dve", "pe", b.c["pe"])
                    ct = cast_tile(x_bf, t0)
                    ld = [None, None]
                    qc0 = b.tok("pe")
                    cons = [qc0, qc0]
                    q0 = [b.tok("act"), b.tok("dve")]
                    ev = [q0, q0, q0, q0]

                    def loadQ(og):
                        p = og % 2
                        b.wt("sp", cons[p])
                        ld[p] = load_wt(w_inT[p], "w_in", og,
                                        "qs" if p == 0 else "qa")
                    loadQ(0)
                    loadQ(1)
                    for oc in range(24):
                        og, ol = divmod(oc, 4)
                        p = og % 2
                        par = oc % 2
                        b.wt("pe", ld[p])
                        b.wt("pe", ct)
                        b.wt("pe", ev[par])
                        mmt = None
                        for fc in range(FC):
                            mmt = b.pe(lambda e, ol=ol, fc=fc, p=p, par=par:
                                       e.matmul(
                                bk(par, 0, TT),
                                w_inT[p][:, fc * 512 + ol * 128:
                                         fc * 512 + ol * 128 + 128],
                                x_bf[:, fc * TT:(fc + 1) * TT],
                                start=(fc == 0), stop=(fc == FC - 1)),
                                inc=(fc == FC - 1))
                        b.wt("act", mmt)
                        b.wait("act", "pe", b.c["pe"])
                        if oc < 8:
                            ev[par] = b.act(lambda e, oc=oc, qid=qid, par=par:
                                            e.activation(
                                qsl(qid, oc, 0, TT), bk(par, 0, TT),
                                AF.Identity,
                                bias=cols["b_inq"][:, oc:oc + 1], scale=QS))
                        elif oc < 16:
                            ev[par] = b.act(lambda e, oc=oc, sl=sl, par=par:
                                            e.activation(
                                ksl(sl, oc - 8, 0, TT), bk(par, 0, TT),
                                AF.Identity, bias=cols["b_in"][:, oc:oc + 1]))
                        else:
                            c = oc - 16
                            vpar = c % 2
                            evv = b.act(lambda e, oc=oc, par=par:
                                        e.activation(
                                vstg, bk(par, 0, TT), AF.Identity,
                                bias=cols["b_in"][:, oc:oc + 1]))
                            ev[par] = evv
                            for th in range(2):
                                b.wt("pe", evv)
                                b.wt("pe", ev[2 + vpar])
                                tr = b.pe(lambda e, th=th, vpar=vpar:
                                          e.transpose(
                                    bkb(2 + vpar, th * 128, (th + 1) * 128),
                                    vstg[:, th * 128:(th + 1) * 128],
                                    ident_b), inc=True)
                                b.wt("dve", tr)
                                b.wait("dve", "pe", b.c["pe"])
                                ev[2 + vpar] = b.dve(
                                    lambda e, th=th, sl=sl, c=c, vpar=vpar:
                                    e.tensor_copy(
                                        v_ring[:, sl * 2048 + th * 1024
                                               + c * 128:
                                               sl * 2048 + th * 1024
                                               + c * 128 + 128],
                                        bkb(2 + vpar, th * 128,
                                            (th + 1) * 128)))
                        if ol == 3 and og + 2 < 6:
                            cons[p] = b.tok("pe")
                            loadQ(og + 2)

                def attn(i, s=s):
                    slp, slc, sln = (i - 1) % 3, i % 3, (i + 1) % 3
                    qid = i % 2
                    b.wait("pe", "act", b.c["act"])
                    b.wait("pe", "dve", b.c["dve"])
                    for qb in range(2):
                        qg = i * 2 + qb
                        mv = 0 if qg == 0 else (2 if qg == 2 * TPS - 1 else 1)
                        for h in range(NHEAD):
                            fcq = h // 2
                            r0 = 64 * (h % 2)
                            if qb == 0:
                                kp = [(slp, 128, 128), (slc, 0, 256)]
                                vp = [(slp, 1), (slc, 0), (slc, 1)]
                            else:
                                kp = [(slc, 0, 128), (slc, 128, 256 - 128),
                                      (sln, 0, 128)]
                                vp = [(slc, 0), (slc, 1), (sln, 0)]
                            qAP = q_ring[r0:r0 + 64,
                                         qid * 2048 + fcq * TT + qb * 128:
                                         qid * 2048 + fcq * TT + qb * 128 + 128]
                            co = 0
                            mmt = None
                            for (ksl_, klo, kn) in kp:
                                mmt = b.pe(lambda e, ksl_=ksl_, klo=klo, kn=kn,
                                           co=co, qAP=qAP, r0=r0, fcq=fcq:
                                           e.matmul(
                                    bk(0, co, co + kn), qAP,
                                    k_ring[r0:r0 + 64,
                                           ksl_ * 2048 + fcq * TT + klo:
                                           ksl_ * 2048 + fcq * TT + klo + kn],
                                    start=True, stop=True),
                                    inc=(co + kn == 384))
                                co += kn
                            b.wt("dve", mmt)
                            if mv == 0:
                                b.dve(lambda e: e.memset(s_sb[:, 0:128], -1e9))
                                ma = b.dve(lambda e: e.tensor_add(
                                    s_sb[:, 128:384], bk(0, 128, 384),
                                    msk_b[:, 128:384]))
                            elif mv == 2:
                                b.dve(lambda e: e.memset(s_sb[:, 256:384],
                                                         -1e9))
                                ma = b.dve(lambda e: e.tensor_add(
                                    s_sb[:, 0:256], bk(0, 0, 256),
                                    msk_b[:, 0:256]))
                            else:
                                ma = b.dve(lambda e: e.tensor_add(
                                    s_sb, bk(0, 0, 384), msk_b))
                            b.wt("pe", ma)
                            b.wt("act", ma)
                            ex = b.act(lambda e: e.activation(
                                s_sb, s_sb, AF.Exp, accum_out=sumc))
                            b.wt("dve", ex)
                            b.dve(lambda e: e.reciprocal(rinv, sumc))
                            pb = b.dve(lambda e: e.tensor_scalar_mul(
                                p_bf, s_sb, rinv[:, 0:1]))
                            b.wt("pe", pb)
                            trs = None
                            for j in range(3):
                                trs = b.pe(lambda e, j=j: e.transpose(
                                    bkb(1, j * 128, (j + 1) * 128),
                                    p_bf[:, j * 128:(j + 1) * 128], ident_b),
                                    inc=(j == 2))
                            b.wt("dve", trs)
                            pt = b.dve(lambda e: e.tensor_copy(pT_bf,
                                                               bkb(1, 0, 384)))
                            b.wt("pe", pt)
                            cm = None
                            for j, (vsl_, vth) in enumerate(vp):
                                cm = b.pe(lambda e, vsl_=vsl_, vth=vth, h=h,
                                          j=j: e.matmul(
                                    bk(2, 0, 128)[0:64, :], vsl(vsl_, vth, h),
                                    pT_bf[:, j * 128:(j + 1) * 128],
                                    start=(j == 0), stop=(j == 2)),
                                    inc=(j == 2))
                            b.wt("dve", cm)
                            cev = b.dve(lambda e, fcq=fcq, r0=r0, qb=qb:
                                        e.tensor_copy(
                                ctx_bf[r0:r0 + 64, fcq * TT + qb * 128:
                                       fcq * TT + qb * 128 + 128],
                                bk(2, 0, 128)[0:64, :]))
                            b.wt("pe", cev)

                def outproj(i, s=s):
                    t0 = s * SEQ + i * TT
                    b.wait("pe", "dve", b.c["dve"])
                    b.wt("pe", wo_ld)
                    b.wait("pe", "qs", min(b.c["qs"], wo_ld[1] + 16 * 200))
                    o0 = [b.tok("act"), b.tok("dve")]
                    evo = [o0, o0]
                    for oc in range(FC):
                        par = oc % 2
                        b.wt("pe", evo[par])
                        mmt = None
                        for fc in range(FC):
                            mmt = b.pe(lambda e, oc=oc, fc=fc, par=par:
                                       e.matmul(
                                bk(par, 0, TT),
                                w_outT[:, fc * 1024 + oc * 128:
                                       fc * 1024 + oc * 128 + 128],
                                ctx_bf[:, fc * TT:(fc + 1) * TT],
                                start=(fc == 0), stop=(fc == FC - 1)),
                                inc=(fc == FC - 1))
                        b.wt("dve", mmt)
                        b.dve(lambda e, oc=oc, t0=t0, par=par:
                              e.scalar_tensor_tensor(
                            otmp, bk(par, 0, TT),
                            cols["b_out"][:, oc:oc + 1],
                            ST(oc, t0, TT), op0=ALU.add, op1=ALU.add))
                        evo[par] = b.dve(lambda e, oc=oc, t0=t0, par=par:
                                         e.tensor_add(
                            ST(oc, t0, TT), ST(oc, t0, TT), otmp))

                qkv(0)
                qkv(1)
                for i in range(TPS):
                    attn(i)
                    outproj(i)
                    if i + 2 < TPS:
                        qkv(i + 2)
            A.pop()
            b.barrier()
            ln_sweep()

        def _ph_l0f():
            A.push()
            h_buf = A.bf16(32 * TT)
            u_buf = A.bf16(32 * TT)
            x_bf = A.bf16(FC * TT)
            wb = [A.bf16(8192), A.bf16(8192)]
            silu_t = [A.f32(TT), A.f32(TT)]
            for ti in range(NTILE):
                t0 = ti * TT
                b.wait("dve", "pe", b.c["pe"])
                ct = cast_tile(x_bf, t0)

                ld = [None, None]
                c0 = b.tok("pe")
                cons = [c0, c0]
                e0 = [b.tok("act"), b.tok("dve")]
                ev = [e0, e0]

                def loadA(og):
                    p = og % 2
                    b.wt("sp", cons[p])
                    q = "qs" if p == 0 else "qa"
                    load_wt(wb[p][:, 0:4096], "w1", og, q)
                    ld[p] = load_wt(wb[p][:, 4096:8192], "w2", og, q)
                loadA(0)
                loadA(1)
                for og in range(8):
                    p = og % 2
                    for ol in range(4):
                        oc = og * 4 + ol
                        par = oc % 2
                        bA, bB = (0, 1) if par == 0 else (2, 3)
                        b.wt("pe", ld[p])
                        b.wt("pe", ct)
                        b.wt("pe", ev[par])
                        ta = None
                        for fc in range(FC):
                            ta = b.pe(lambda e, ol=ol, fc=fc, p=p, bA=bA:
                                      e.matmul(
                                bk(bA, 0, TT),
                                wb[p][:, fc * 512 + ol * 128:
                                      fc * 512 + ol * 128 + 128],
                                x_bf[:, fc * TT:(fc + 1) * TT],
                                start=(fc == 0), stop=(fc == FC - 1)),
                                inc=(fc == FC - 1))
                        tb = None
                        for fc in range(FC):
                            tb = b.pe(lambda e, ol=ol, fc=fc, p=p, bB=bB:
                                      e.matmul(
                                bk(bB, 0, TT),
                                wb[p][:, 4096 + fc * 512 + ol * 128:
                                      4096 + fc * 512 + ol * 128 + 128],
                                x_bf[:, fc * TT:(fc + 1) * TT],
                                start=(fc == 0), stop=(fc == FC - 1)),
                                inc=(fc == FC - 1))
                        b.wt("act", ta)
                        sa = b.act(lambda e, oc=oc, par=par, bA=bA:
                                   e.activation(
                            silu_t[par], bk(bA, 0, TT), AF.Silu,
                            scale=cols["w1s"][:, oc:oc + 1]))
                        b.wt("dve", tb)
                        b.wt("dve", sa)
                        hv = b.dve(lambda e, oc=oc, par=par, bB=bB:
                                   e.scalar_tensor_tensor(
                            h_buf[:, oc * TT:(oc + 1) * TT], bk(bB, 0, TT),
                            cols["w2s"][:, oc:oc + 1], silu_t[par],
                            op0=ALU.mult, op1=ALU.mult))
                        ev[par] = [sa, hv]
                    cons[p] = b.tok("pe")
                    if og + 2 < 8:
                        loadA(og + 2)

                def hphase(mat, src_buf, dst_buf, scol, n_og=16):
                    ld = [None, None]
                    c0 = b.tok("pe")
                    cons = [c0, c0]
                    e0 = [b.tok("act"), b.tok("dve")]
                    ev = [e0, e0]

                    def loadH(og):
                        p = og % 2
                        b.wt("sp", cons[p])
                        ld[p] = load_wt(wb[p], mat, og,
                                        "qs" if p == 0 else "qa")
                    loadH(0)
                    loadH(1)
                    for og in range(n_og):
                        p = og % 2
                        for ol in range(2):
                            oc = og * 2 + ol
                            par = oc % 2
                            b.wt("pe", ld[p])
                            b.wt("pe", ev[par])
                            tu = None
                            for ic in range(32):
                                tu = b.pe(lambda e, ol=ol, ic=ic, p=p, par=par:
                                          e.matmul(
                                    bk(par, 0, TT),
                                    wb[p][:, ic * 256 + ol * 128:
                                          ic * 256 + ol * 128 + 128],
                                    src_buf[:, ic * TT:(ic + 1) * TT],
                                    start=(ic == 0), stop=(ic == 31)),
                                    inc=(ic == 31))
                            b.wt("act", tu)
                            ug = b.act(lambda e, oc=oc, par=par: e.activation(
                                dst_buf[:, oc * TT:(oc + 1) * TT],
                                bk(par, 0, TT),
                                AF.Gelu, scale=cols[scol][:, oc:oc + 1]))
                            ev[par] = ug
                        cons[p] = b.tok("pe")
                        if og + 2 < n_og:
                            loadH(og + 2)

                hphase("h1", h_buf, u_buf, "h1s")
                hphase("h2", u_buf, h_buf, "h2s")

                ld = [None, None]
                c0 = b.tok("pe")
                cons = [c0, c0]
                e0 = [b.tok("act"), b.tok("dve")]
                ev = [e0, e0]

                def loadW3(og):
                    p = og % 2
                    b.wt("sp", cons[p])
                    ld[p] = load_wt(wb[p], "w3", og, "qs" if p == 0 else "qa")
                loadW3(0)
                loadW3(1)
                for og in range(4):
                    p = og % 2
                    for ol in range(2):
                        oc = og * 2 + ol
                        par = oc % 2
                        b.wt("pe", ld[p])
                        b.wt("pe", ev[par])
                        ty = None
                        for ic in range(32):
                            ty = b.pe(lambda e, ol=ol, ic=ic, p=p, par=par:
                                      e.matmul(
                                bk(par, 0, TT),
                                wb[p][:, ic * 256 + ol * 128:
                                      ic * 256 + ol * 128 + 128],
                                h_buf[:, ic * TT:(ic + 1) * TT],
                                start=(ic == 0), stop=(ic == 31)),
                                inc=(ic == 31))
                        b.wt("dve", ty)
                        yd = b.dve(lambda e, oc=oc, t0=t0, par=par:
                                   e.scalar_tensor_tensor(
                            ST(oc, t0, TT), bk(par, 0, TT),
                            cols["w3s"][:, oc:oc + 1], ST(oc, t0, TT),
                            op0=ALU.mult, op1=ALU.add))
                        ev[par] = yd
                    cons[p] = b.tok("pe")
                    if og + 2 < 4:
                        loadW3(og + 2)
            A.pop()
            b.barrier()
            ln_sweep()


        def _ph_l1a():
            A.push()
            pwc_f = A.f32(3 * 128)
            G_tm = A.bf16(4 * 1024)
            GhatT = A.bf16(8 * 512)
            Gh3 = GhatT.rearrange("p (fc g) -> p fc g", fc=FC)
            scr4 = A.f32(1024)
            gh_tmp = A.bf16(1024)
            d_bf = A.bf16(8 * 128)
            s_sb = A.f32(512)
            gw_bf = A.bf16(512)
            gwT_sb = A.bf16(512)
            infl_bf = A.bf16(8 * 128)
            nrm_t = A.bf16(128)
            bl_ring = A.bf16(2 * 1024)
            C_cur = A.bf16(128)
            C_prev = A.bf16(128)
            stg = A.f32(128)
            traj = A.f32(1024)
            cb = A.f32(16)
            (C_MAG, C_E4, C_T, C_MW0, C_MW1, C_MX, C_REC, C_I2, C_I03,
             C_SUM, C_RI, C_W, C_IW, C_S2, C_GN) = range(15)

            def col(i):
                return cb[:, i:i + 1]

            pld = b.dma("pool", lambda e: e.dma_start(out=pwc_f[:, :],
                                                      in_=inp["pwc"]))
            b.pool(lambda e: e.memset(stg, 0.0))
            b.wt("sp", cast_toks["g"])
            gld = b.dma("sp", lambda e: e.dma_start(
                out=G_tm.rearrange("p (gc f) -> p gc f", gc=4),
                in_=cast_bf["g"].rearrange("(gc p) f -> p gc f", p=128)))
            b.wt("act", gld)
            for gc in range(4):
                sq = b.act(lambda e, gc=gc: e.activation(
                    scr4, G_tm[:, gc * 1024:(gc + 1) * 1024], AF.Square,
                    accum_out=col(C_S2)))
                b.wt("act", sq)
                b.act(lambda e: e.activation(col(C_GN), col(C_S2), AF.Ln))
                gn = b.act(lambda e: e.activation(col(C_GN), col(C_GN),
                                                  AF.Exp, scale=0.5))
                b.wt("dve", gn)
                b.dve(lambda e: e.tensor_scalar_max(col(C_MX), col(C_GN), 1e-8))
                rc = b.dve(lambda e: e.reciprocal(col(C_REC), col(C_MX)))
                gh = b.dve(lambda e, gc=gc: e.tensor_scalar_mul(
                    gh_tmp, G_tm[:, gc * 1024:(gc + 1) * 1024],
                    col(C_REC)[:, 0:1]))
                b.wt("pe", gh)
                tk = None
                for fc in range(FC):
                    tk = b.pe(lambda e, fc=fc: e.transpose(
                        bkb(0, fc * 128, (fc + 1) * 128),
                        gh_tmp[:, fc * 128:(fc + 1) * 128], ident_b),
                        inc=(fc == FC - 1))
                b.wt("dve", tk)
                ev = b.dve(lambda e, gc=gc: e.tensor_copy(
                    Gh3[:, :, gc * 128:(gc + 1) * 128],
                    bkb(0).rearrange("p (f g) -> p f g", f=FC)))
                b.wt("pe", ev)
                b.wt("act", ev)
            b.barrier()

            for s in range(NSEQ):
                for ci in range(16):
                    t0 = s * SEQ + ci * 128
                    par = ci % 2
                    nd = 127 if ci == 15 else 128
                    b.wait("dve", "pe", b.c["pe"])
                    dts = []
                    for fc in range(FC):
                        dts.append(b.dve(lambda e, fc=fc, t0=t0, nd=nd:
                                         e.tensor_sub(
                            d_bf[:, fc * 128:fc * 128 + nd],
                            ST(fc, t0 + 1, nd), ST(fc, t0, nd))))
                    if ci == 15:
                        b.wait("pool", "dve", b.c["dve"])
                        for fc in range(FC):
                            b.pool(lambda e, fc=fc: e.memset(
                                d_bf[:, fc * 128 + 127:fc * 128 + 128], 0.0))
                    t_pool = b.tok("pool")
                    b.wt("dve", t_pool)
                    mt = None
                    for fc in range(FC):
                        if mt is not None:
                            b.wt("dve", mt)
                        sq = b.dve(lambda e, fc=fc: e.tensor_mul(
                            scr4[:, 0:128], d_bf[:, fc * 128:(fc + 1) * 128],
                            d_bf[:, fc * 128:(fc + 1) * 128]))
                        b.wt("pe", sq)
                        mt = b.pe(lambda e, fc=fc: e.matmul(
                            bk(0, 0, 128)[0:1, :], ones_cf, scr4[:, 0:128],
                            start=(fc == 0), stop=(fc == FC - 1)), inc=True)
                    b.wt("dve", mt)
                    cp = b.dve(lambda e: e.tensor_copy(stg[0:1, :],
                                                       bk(0, 0, 128)[0:1, :]))
                    b.wt("pe", cp)
                    tr1 = b.pe(lambda e: e.transpose(bk(1, 0, 128), stg,
                                                     ident_f), inc=True)
                    b.wt("act", tr1)
                    b.act(lambda e: e.activation(col(C_MAG), bk(1, 0, 128)[:, 0:1],
                                                 AF.Ln))
                    mg = b.act(lambda e: e.activation(col(C_MAG), col(C_MAG),
                                                      AF.Exp, scale=0.5))
                    e4 = b.act(lambda e: e.activation(col(C_E4), col(C_MAG),
                                                      AF.Exp, scale=4.0))
                    b.wt("dve", e4)
                    b.dve(lambda e: e.tensor_scalar_add(col(C_T), col(C_E4), 1.0))
                    b.dve(lambda e: e.reciprocal(col(C_T), col(C_T)))
                    mwc = col(C_MW0) if par == 0 else col(C_MW1)
                    b.dve(lambda e, mwc=mwc: e.tensor_scalar(
                        mwc, col(C_T), -2.0, 1.0, op0=ALU.mult, op1=ALU.add))
                    b.dve(lambda e: e.tensor_scalar_max(col(C_MX), col(C_MAG),
                                                        1e-6))
                    b.dve(lambda e: e.reciprocal(col(C_REC), col(C_MX)))
                    b.dve(lambda e: e.tensor_scalar_mul(col(C_I2), col(C_REC),
                                                        2.0))
                    iv3 = b.dve(lambda e: e.tensor_scalar_mul(
                        col(C_I03), col(C_REC), 0.6))
                    b.wt("pe", iv3)
                    b.wt("pe", t_pool)
                    dd = None
                    for fc in range(FC):
                        dd = b.pe(lambda e, fc=fc: e.matmul(
                            bk(2, 0, 512), d_bf[:, fc * 128:(fc + 1) * 128],
                            GhatT[:, fc * 512:(fc + 1) * 512],
                            start=(fc == 0), stop=(fc == FC - 1)),
                            inc=(fc == FC - 1))
                    b.wt("dve", dd)
                    sm = b.dve(lambda e: e.tensor_scalar_mul(
                        s_sb, bk(2, 0, 512), col(C_I2)[:, 0:1]))
                    b.wt("act", sm)
                    ex = b.act(lambda e: e.activation(s_sb, s_sb, AF.Exp,
                                                      accum_out=col(C_SUM)))
                    b.wt("dve", ex)
                    b.dve(lambda e: e.reciprocal(col(C_RI), col(C_SUM)))
                    gw = b.dve(lambda e: e.tensor_scalar_mul(
                        gw_bf, s_sb, col(C_RI)[:, 0:1]))
                    b.wt("pe", gw)
                    tk = None
                    for j in range(4):
                        tk = b.pe(lambda e, j=j: e.transpose(
                            bkb(3, j * 128, (j + 1) * 128),
                            gw_bf[:, j * 128:(j + 1) * 128], ident_b),
                            inc=(j == 3))
                    b.wt("dve", tk)
                    gt = b.dve(lambda e: e.tensor_copy(gwT_sb, bkb(3, 0, 512)))
                    b.wt("pe", gt)
                    ift = None
                    for fc in range(FC):
                        for gc in range(4):
                            ift = b.pe(lambda e, fc=fc, gc=gc: e.matmul(
                                bk(4, 0, 128),
                                G_tm[:, gc * 1024 + fc * 128:
                                     gc * 1024 + fc * 128 + 128],
                                gwT_sb[:, gc * 128:(gc + 1) * 128],
                                start=(gc == 0), stop=(gc == 3)),
                                inc=(gc == 3))
                        b.wt("act", ift)
                        fe = b.act(lambda e, fc=fc: e.activation(
                            infl_bf[:, fc * 128:(fc + 1) * 128], bk(4, 0, 128),
                            AF.Identity, scale=0.4))
                        b.wt("pe", fe)
                    for fc in range(FC):
                        trd = b.pe(lambda e, fc=fc: e.transpose(
                            bkb(5, 0, 128), d_bf[:, fc * 128:(fc + 1) * 128],
                            ident_b), inc=True)
                        b.wt("dve", trd)
                        nt = b.dve(lambda e: e.tensor_scalar_mul(
                            nrm_t, bkb(5, 0, 128), col(C_I03)[:, 0:1]))
                        b.wt("pe", nt)
                        tri = b.pe(lambda e, fc=fc: e.transpose(
                            bkb(5, 128, 256), infl_bf[:, fc * 128:(fc + 1) * 128],
                            ident_b), inc=True)
                        b.wt("dve", tri)
                        bl = b.dve(lambda e, fc=fc, par=par: e.tensor_add(
                            bl_ring[:, par * 1024 + fc * 128:
                                    par * 1024 + fc * 128 + 128],
                            nrm_t, bkb(5, 128, 256)))
                        b.wt("pe", bl)
                    b.wt("dve", pld)
                    pv = 0 if ci == 0 else 1
                    cc = b.dve(lambda e, pv=pv, mwc=mwc: e.tensor_scalar_mul(
                        C_cur, pwc_f[:, pv * 128:(pv + 1) * 128], mwc[:, 0:1]))
                    cpv = None
                    if ci > 0:
                        mwp = col(C_MW1) if par == 0 else col(C_MW0)
                        cpv = b.dve(lambda e, mwp=mwp: e.tensor_scalar_mul(
                            C_prev, pwc_f[:, 256:384], mwp[:, 0:1]))
                    b.wt("pe", cc)
                    b.wt("pe", cpv)
                    ws = b.pe(lambda e, ci=ci: e.matmul(
                        bk(0, 0, 128)[0:1, :], ones_cb[:, 0:1], C_cur,
                        start=True, stop=(ci == 0)), inc=(ci == 0))
                    if ci > 0:
                        ws = b.pe(lambda e: e.matmul(
                            bk(0, 0, 128)[0:1, :], ones_cb[:, 0:1], C_prev,
                            start=False, stop=True), inc=True)
                    b.wt("dve", ws)
                    cp2 = b.dve(lambda e: e.tensor_copy(stg[0:1, :],
                                                        bk(0, 0, 128)[0:1, :]))
                    b.wt("pe", cp2)
                    tr2 = b.pe(lambda e: e.transpose(bk(1, 0, 128), stg,
                                                     ident_f), inc=True)
                    b.wt("dve", tr2)
                    b.dve(lambda e: e.tensor_scalar_max(col(C_W),
                                                        bk(1, 0, 128)[:, 0:1],
                                                        1e-8))
                    iw = b.dve(lambda e: e.reciprocal(col(C_IW), col(C_W)))
                    for fh in range(2):
                        tj = None
                        if ci > 0:
                            tj = b.pe(lambda e, fh=fh, par=par: e.matmul(
                                bk(6, 0, 512), C_prev,
                                bl_ring[:, (1 - par) * 1024 + fh * 512:
                                        (1 - par) * 1024 + fh * 512 + 512],
                                start=True, stop=False))
                        tj = b.pe(lambda e, fh=fh, par=par, ci=ci: e.matmul(
                            bk(6, 0, 512), C_cur,
                            bl_ring[:, par * 1024 + fh * 512:
                                    par * 1024 + fh * 512 + 512],
                            start=(ci == 0), stop=True), inc=True)
                        b.wt("dve", tj)
                        b.wt("dve", iw)
                        tv = b.dve(lambda e, fh=fh: e.tensor_scalar_mul(
                            traj[:, fh * 512:(fh + 1) * 512], bk(6, 0, 512),
                            col(C_IW)[:, 0:1]))
                        b.wt("pe", tv)
                    for fc in range(FC):
                        tt = b.pe(lambda e, fc=fc: e.transpose(
                            bk(7, 0, 128), traj[:, fc * 128:(fc + 1) * 128],
                            ident_f), inc=True)
                        b.wt("dve", tt)
                        ad = b.dve(lambda e, fc=fc, t0=t0: e.tensor_add(
                            ST(fc, t0, 128), ST(fc, t0, 128), bk(7, 0, 128)))
                        b.wt("pe", ad)
            A.pop()
            b.barrier()
            ln_sweep()

        def _ph_l1f():
            A.push()
            oneh = A.f32(1024)
            gwTv = A.f32(64)
            gbias = A.f32(8)
            A.push()
            gb_row = A.f32(8)
            gw_nat = A.f32(1024)
            A.pop()
            x_bf = A.bf16(FC * TT)
            wp = [A.bf16(8192), A.bf16(8192)]
            h_bf = A.bf16(16 * TT)
            g_bc = A.bf16(8 * TT)
            y_sb = A.f32(8 * TT)
            gat = A.f32(NTILE * 2 * 8)
            gT_sb = A.f32(128)
            silu_t = [A.f32(TT), A.f32(TT)]
            t2 = A.f32(TT)
            s8 = A.f32(8)
            sb8 = A.f32(8)
            i1 = A.f32(8)
            i2 = A.f32(8)
            mk8 = A.f32(8)
            c8 = A.f32(4)
            z_tm = A.f32(1024)
            obuf = A.f32(1024)

            old = b.dma("pool", lambda e: e.dma_start(out=oneh[0:8, :],
                                                      in_=inp["onehot"]))
            l1 = b.dma("sp", lambda e: e.dma_start(
                out=gb_row[0:1, :],
                in_=inp["L1_ffn_gate_bias"].rearrange("(a b) -> a b", a=1)))
            l2 = b.dma("sp", lambda e: e.dma_start(out=gw_nat[0:8, :],
                                                   in_=inp["L1_ffn_gate_w"]))
            b.wt("pe", l1)
            b.wt("pe", l2)
            tk = b.pe(lambda e: e.matmul(bk(1, 0, 8), ones_row[0:1, :],
                                         gb_row[0:1, :], start=True, stop=True),
                      inc=True)
            b.wt("dve", tk)
            cp = b.dve(lambda e: e.tensor_copy(gbias, bk(1, 0, 8)))
            b.wt("pe", l2)
            b.wt("pe", cp)
            for fc in range(FC):
                tk = b.pe(lambda e, fc=fc: e.transpose(
                    bk(2, fc * 8, fc * 8 + 8),
                    gw_nat[0:8, fc * 128:(fc + 1) * 128], ident_f[0:8, 0:8]),
                    inc=(fc == FC - 1))
            b.wt("dve", tk)
            cp2 = b.dve(lambda e: e.tensor_copy(gwTv, bk(2, 0, 64)))
            b.wt("pe", cp2)
            b.barrier()

            for tch in range(NTILE * 2):
                t0 = tch * 128
                lg = None
                for fc in range(FC):
                    lg = b.pe(lambda e, fc=fc, t0=t0: e.matmul(
                        bk(0, 0, 8), ST(fc, t0, 128),
                        gwTv[:, fc * 8:(fc + 1) * 8],
                        start=(fc == 0), stop=(fc == FC - 1)),
                        inc=(fc == FC - 1))
                b.wt("act", lg)
                b.wait("act", "dve", b.c["dve"])
                ex = b.act(lambda e: e.activation(s8, bk(0, 0, 8), AF.Exp,
                                                  accum_out=c8[:, 0:1]))
                b.wt("pe", ex)
                b.wt("dve", ex)
                b.dve(lambda e: e.reciprocal(c8[:, 1:2], c8[:, 0:1]))
                b.dve(lambda e: e.tensor_scalar_mul(s8, s8, c8[:, 1:2]))
                b.dve(lambda e: e.tensor_add(sb8, s8, gbias))
                b.dve(lambda e: e.reduce_max(c8[:, 2:3], sb8, axis=AX.X))
                b.dve(lambda e: e.tensor_scalar(i1, sb8, c8[:, 2:3], None,
                                                op0=ALU.is_ge))
                b.dve(lambda e: e.scalar_tensor_tensor(
                    mk8, i1, -1e9, sb8, op0=ALU.mult, op1=ALU.add))
                b.dve(lambda e: e.reduce_max(c8[:, 3:4], mk8, axis=AX.X))
                b.dve(lambda e: e.tensor_scalar(i2, mk8, c8[:, 3:4], None,
                                                op0=ALU.is_ge))
                b.dve(lambda e: e.tensor_add(i1, i1, i2))
                b.dve(lambda e, tch=tch: e.tensor_mul(
                    gat[:, tch * 8:(tch + 1) * 8], s8, i1))

            prev_store = None
            for ti in range(NTILE):
                t0 = ti * TT
                b.wait("dve", "pe", b.c["pe"])
                ct = cast_tile(x_bf, t0)
                b.wait("pool", "dve", b.c["dve"])
                ym = b.pool(lambda e: e.memset(y_sb, 0.0))
                for tc in range(2):
                    tch = ti * 2 + tc
                    b.wait("pe", "dve", b.c["dve"])
                    tk = b.pe(lambda e, tch=tch: e.transpose(
                        bk(1, 0, 128)[0:8, :], gat[:, tch * 8:(tch + 1) * 8],
                        ident_f), inc=True)
                    b.wt("dve", tk)
                    cp = b.dve(lambda e: e.tensor_copy(gT_sb[0:8, :],
                                                       bk(1, 0, 128)[0:8, :]))
                    b.wt("pe", cp)
                    for e_ in range(8):
                        bc = b.pe(lambda e, e_=e_: e.matmul(
                            bk(2, 0, 128), oneh[0:8, e_ * 128:(e_ + 1) * 128],
                            gT_sb[0:8, :], start=True, stop=True), inc=True)
                        b.wt("dve", bc)
                        ev = b.dve(lambda e, e_=e_, tc=tc: e.tensor_copy(
                            g_bc[:, e_ * TT + tc * 128:e_ * TT + tc * 128 + 128],
                            bk(2, 0, 128)))
                        b.wt("pe", ev)
                gen_loads = []
                for e_ in range(8):
                    for og in (2 * e_, 2 * e_ + 1):
                        gen_loads.append(lambda buf, q, og=og: (
                            load_wt(buf[:, 0:4096], "ew1", og, q),
                            load_wt(buf[:, 4096:8192], "ew3", og, q))[-1])
                    gen_loads.append(lambda buf, q, e_=e_: (
                        load_wt(buf[:, 0:4096], "ew2", 2 * e_, q),
                        load_wt(buf[:, 4096:8192], "ew2", 2 * e_ + 1, q))[-1])
                for og in range(4):
                    gen_loads.append(lambda buf, q, og=og: (
                        load_wt(buf[:, 0:4096], "sw1", og, q),
                        load_wt(buf[:, 4096:8192], "sw3", og, q))[-1])
                for og in range(2):
                    gen_loads.append(lambda buf, q, og=og:
                                     load_wt(buf, "sw2", og, q))

                pc0 = b.tok("pe")
                pstate = {"ld": [None, None], "cons": [pc0, pc0]}

                def pissue(i):
                    p = i % 2
                    b.wt("sp", pstate["cons"][p])
                    q = "qs" if p == 0 else "qa"
                    pstate["ld"][p] = gen_loads[i](wp[p], q)

                def pconsume(i):
                    b.wt("pe", pstate["ld"][i % 2])
                    return i % 2

                def pdone(i):
                    pstate["cons"][i % 2] = b.tok("pe")
                    if i + 2 < len(gen_loads):
                        pissue(i + 2)

                pissue(0)
                pissue(1)
                gi = 0
                eh0 = [b.tok("act"), b.tok("dve")]
                evh = [eh0, eh0]
                evy = [eh0, eh0]

                def silu_block(p, oc, hslot, gate_e):
                    par = oc % 2
                    bA, bB = (0, 1) if par == 0 else (2, 3)
                    b.wt("pe", ct)
                    b.wt("pe", evh[par])
                    ta = None
                    for fc in range(FC):
                        ta = b.pe(lambda e, oc=oc, fc=fc, p=p, bA=bA: e.matmul(
                            bk(bA, 0, TT),
                            wp[p][:, fc * 512 + (oc % 4) * 128:
                                  fc * 512 + (oc % 4) * 128 + 128],
                            x_bf[:, fc * TT:(fc + 1) * TT],
                            start=(fc == 0), stop=(fc == FC - 1)),
                            inc=(fc == FC - 1))
                    tb = None
                    for fc in range(FC):
                        tb = b.pe(lambda e, oc=oc, fc=fc, p=p, bB=bB: e.matmul(
                            bk(bB, 0, TT),
                            wp[p][:, 4096 + fc * 512 + (oc % 4) * 128:
                                  4096 + fc * 512 + (oc % 4) * 128 + 128],
                            x_bf[:, fc * TT:(fc + 1) * TT],
                            start=(fc == 0), stop=(fc == FC - 1)),
                            inc=(fc == FC - 1))
                    b.wt("act", ta)
                    sa = b.act(lambda e, par=par, bA=bA: e.activation(
                        silu_t[par], bk(bA, 0, TT), AF.Silu))
                    b.wt("dve", tb)
                    b.wt("dve", sa)
                    if gate_e is not None:
                        b.dve(lambda e, par=par, bB=bB: e.tensor_mul(
                            t2, silu_t[par], bk(bB, 0, TT)))
                        hv = b.dve(lambda e, hslot=hslot, gate_e=gate_e:
                                   e.tensor_mul(
                            h_bf[:, hslot * TT:(hslot + 1) * TT], t2,
                            g_bc[:, gate_e * TT:(gate_e + 1) * TT]))
                    else:
                        hv = b.dve(lambda e, hslot=hslot, par=par, bB=bB:
                                   e.tensor_mul(
                            h_bf[:, hslot * TT:(hslot + 1) * TT],
                            silu_t[par], bk(bB, 0, TT)))
                    evh[par] = [sa, hv]

                def y_block(p, oc, wbase, n_ic):
                    par = oc % 2
                    b.wt("pe", evy[par])
                    ty = None
                    for ic in range(n_ic):
                        ty = b.pe(lambda e, oc=oc, ic=ic, p=p, par=par,
                                  wbase=wbase, n_ic=n_ic: e.matmul(
                            bk(4 + par, 0, TT),
                            wp[p][:, wbase + ic * 512 + (oc % 4) * 128:
                                  wbase + ic * 512 + (oc % 4) * 128 + 128],
                            h_bf[:, ic * TT:(ic + 1) * TT],
                            start=(ic == 0), stop=(ic == n_ic - 1)),
                            inc=(ic == n_ic - 1))
                    b.wt("dve", ty)
                    b.wt("dve", ym)
                    ya = b.dve(lambda e, oc=oc, par=par: e.tensor_add(
                        y_sb[:, oc * TT:(oc + 1) * TT],
                        y_sb[:, oc * TT:(oc + 1) * TT], bk(4 + par, 0, TT)))
                    evy[par] = ya

                for e_ in range(8):
                    for half in range(2):
                        p = pconsume(gi)
                        for ol in range(4):
                            silu_block(p, half * 4 + ol, half * 4 + ol, e_)
                        pdone(gi)
                        gi += 1
                    p = pconsume(gi)
                    b.wait("pe", "dve", b.c["dve"])
                    for oc in range(8):
                        y_block(p, oc, (oc // 4) * 4096, FC)
                    pdone(gi)
                    gi += 1
                for og in range(4):
                    p = pconsume(gi)
                    for ol in range(4):
                        silu_block(p, og * 4 + ol, og * 4 + ol, None)
                    pdone(gi)
                    gi += 1
                b.wait("pe", "dve", b.c["dve"])
                for og in range(2):
                    p = pconsume(gi)
                    for ol in range(4):
                        y_block(p, og * 4 + ol, 0, 16)
                    pdone(gi)
                    gi += 1
                zt = None
                for fc in range(FC):
                    zt = b.dve(lambda e, fc=fc, t0=t0: e.tensor_add(
                        ST(fc, t0, TT), ST(fc, t0, TT),
                        y_sb[:, fc * TT:(fc + 1) * TT]))
                for tc in range(2):
                    tt0 = t0 + tc * 128
                    bb, pos = divmod(tt0, SEQ)
                    b.wait("pe", "dve", b.c["dve"])
                    b.wt("pe", prev_store)
                    for half in range(2):
                        tk = None
                        for j in range(4):
                            fc = half * 4 + j
                            tk = b.pe(lambda e, fc=fc, tt0=tt0, half=half, j=j:
                                      e.transpose(
                                bk(half, j * 128, (j + 1) * 128),
                                ST(fc, tt0, 128), ident_f), inc=(j == 3))
                        b.wt("dve", tk)
                        ev = b.dve(lambda e, half=half: e.tensor_copy(
                            z_tm.rearrange("p (f t) -> p f t", f=FC)
                            [:, half * 4:(half + 1) * 4, :],
                            bk(half).rearrange("p (f t) -> p f t", f=4)))
                        b.wt("pe", ev)
                    b.dve(lambda e: e.reduce_sum(c8[:, 0:1], z_tm, axis=AX.X))
                    b.dve(lambda e: e.tensor_scalar_mul(c8[:, 0:1], c8[:, 0:1],
                                                        1.0 / D))
                    b.dve(lambda e: e.tensor_scalar_sub(z_tm, z_tm,
                                                        c8[:, 0:1]))
                    b.wt("dve", prev_store)
                    b.dve(lambda e: e.tensor_mul(obuf, z_tm, z_tm))
                    b.dve(lambda e: e.reduce_sum(c8[:, 1:2], obuf, axis=AX.X))
                    vr = b.dve(lambda e: e.tensor_scalar_mul(
                        c8[:, 1:2], c8[:, 1:2], 1.0 / D))
                    b.wt("act", vr)
                    b.act(lambda e: e.activation(c8[:, 1:2], c8[:, 1:2],
                                                 AF.Ln, bias=1e-5))
                    rs = b.act(lambda e: e.activation(c8[:, 1:2], c8[:, 1:2],
                                                      AF.Exp, scale=-0.5))
                    b.wt("dve", rs)
                    b.wt("dve", prev_store)
                    ob = b.dve(lambda e: e.tensor_scalar_mul(obuf, z_tm,
                                                             c8[:, 1:2]))
                    b.wait("pool", "dve", b.c["dve"])
                    prev_store = b.dma("pool", lambda e, bb=bb, pos=pos:
                                       e.dma_start(out=out[bb, pos:pos + 128, :],
                                                   in_=obuf))
            A.pop()

        def _ph_store():
            A.push()
            obuf = A.f32(1024)
            prev_store = None
            for ch in range(NTOK // 128):
                bb, pos = divmod(ch * 128, SEQ)
                b.wt("pe", prev_store)
                for half in range(2):
                    tk = None
                    for j in range(4):
                        fc = half * 4 + j
                        tk = b.pe(lambda e, fc=fc, ch=ch, half=half, j=j:
                                  e.transpose(
                            bk(half, j * 128, (j + 1) * 128),
                            ST(fc, ch * 128, 128), ident_f), inc=(j == 3))
                    b.wt("dve", tk)
                    b.wt("dve", prev_store)
                    ev = b.dve(lambda e, half=half: e.tensor_copy(
                        obuf.rearrange("p (f t) -> p f t", f=FC)
                        [:, half * 4:(half + 1) * 4, :],
                        bk(half).rearrange("p (f t) -> p f t", f=4)))
                    b.wt("pe", ev)
                b.wait("pool", "dve", b.c["dve"])
                prev_store = b.dma("pool", lambda e, bb=bb, pos=pos:
                                   e.dma_start(out=out[bb, pos:pos + 128, :],
                                               in_=obuf))
            A.pop()

        if "l0a" in phases:
            _ph_l0a()
        if "l0f" in phases:
            _ph_l0f()
        if "l1a" in phases:
            _ph_l1a()
        if "l1f" in phases:
            _ph_l1f()
        else:
            _ph_store()

        final_qp = b.c["qp"]
        b.raw("pool", lambda e: e.wait_ge(sems["qp"], final_qp))
        b.finalize()
    return nc, A.peak


def _host_consts():
    msk = np.full((128, 384), -1e9, np.float32)
    qi = np.arange(128)[:, None]
    j = np.arange(384)[None, :]
    msk[(j >= qi) & (j <= qi + 256)] = 0.0

    S, W = SEQ, 8
    pos = np.arange(S)[:, None]
    k = np.arange(W)[None, :]
    idx = pos - W + k
    valid = (idx >= 0) & (pos >= 1)
    w = np.minimum(pos, W)
    jj = k - (W - w)
    lin = np.where(w > 1, 0.1 + 0.9 * jj / np.maximum(w - 1, 1), 0.1)
    pw = np.where(valid, lin, 0.0).astype(np.float32)

    pwc = np.zeros((128, 3, 128), np.float32)
    i_ = np.arange(128)[:, None]
    j_ = np.arange(128)[None, :]
    kk = i_ - j_ + 8
    m = (kk >= 0) & (kk <= 7)
    kkc = np.clip(kk, 0, 7)
    jb = np.broadcast_to(j_, (128, 128))
    pwc[:, 0] = np.where(m, pw[jb, kkc], 0.0)
    pwc[:, 1] = np.where(m, pw[128 + jb, kkc], 0.0)
    kp = i_ - j_ - 120
    mp = (kp >= 0) & (kp <= 7)
    kpc = np.clip(kp, 0, 7)
    pwc[:, 2] = np.where(mp, pw[128 + jb, kpc], 0.0)

    oneh = np.zeros((8, 8, 128), np.float32)
    for e in range(8):
        oneh[e, e, :] = 1.0
    return msk, pwc.reshape(128, -1), oneh.reshape(8, -1)


KPHASES = ("l0a", "l0f", "l1a", "l1f")
NCORES = 8
_CACHE = {}


def kernel(**inputs):
    key = tuple(KPHASES)
    if key not in _CACHE:
        _CACHE[key] = build_program(KPHASES)[0]
    nc = _CACHE[key]
    msk, pwc, oneh = _host_consts()
    in_maps = []
    for c in range(NCORES):
        m = {"x": np.ascontiguousarray(inputs["L0_attn_x"][2 * c:2 * c + 2])}
        for k in ("L0_attn_w_in", "L0_attn_b_in", "L0_attn_w_out",
                  "L0_attn_b_out", "L0_ffn_w1_w", "L0_ffn_w1_s",
                  "L0_ffn_w2_w", "L0_ffn_w2_s", "L0_ffn_h1_w", "L0_ffn_h1_s",
                  "L0_ffn_h2_w", "L0_ffn_h2_s", "L0_ffn_w3_w", "L0_ffn_w3_s",
                  "L1_ffn_gate_w", "L1_ffn_gate_bias",
                  "L1_ffn_sw1", "L1_ffn_sw3", "L1_ffn_sw2"):
            m[k] = np.ascontiguousarray(inputs[k])
        m["L1_attn_g"] = np.ascontiguousarray(
            inputs["L1_attn_guidance_trajectories"])
        m["L1_ffn_w1"] = np.ascontiguousarray(
            inputs["L1_ffn_w1"].reshape(8192, 1024))
        m["L1_ffn_w3"] = np.ascontiguousarray(
            inputs["L1_ffn_w3"].reshape(8192, 1024))
        m["L1_ffn_w2"] = np.ascontiguousarray(
            inputs["L1_ffn_w2"].reshape(8192, 1024))
        m["msk"] = msk
        m["pwc"] = pwc
        m["onehot"] = oneh
        in_maps.append(m)
    res = run_bass_kernel_spmd(nc, in_maps, core_ids=list(range(NCORES)))
    return np.concatenate([np.asarray(r["out"], np.float32)[None]
                           for r in res.results], axis=0).reshape(
                               2 * NCORES, SEQ, D)


# version 03
# speedup vs baseline: 1.7087x; 1.1270x over previous
import numpy as np
import ml_dtypes
from contextlib import ExitStack

import concourse.bass as bass
import concourse.mybir as mybir
from concourse.bass_utils import run_bass_kernel_spmd

F32 = mybir.dt.float32
BF16 = mybir.dt.bfloat16
AF = mybir.ActivationFunctionType
ALU = mybir.AluOpType
AX = mybir.AxisListType

D = 1024
FC = 8
SEQ = 2048
NSEQ = 2
NTOK = SEQ * NSEQ
TT = 256
NTILE = NTOK // TT
TPS = SEQ // TT
NHEAD = 16
HD = 64
QS = 0.125

ENGS = ("pe", "dve", "act", "pool", "sp")
CHAINS = ("pe", "dve", "act", "pool", "qs", "qp", "qa", "qb")

WTS = {
    "w_in": ("L0_attn_w_in", 3072, 1024, 512),
    "w_out": ("L0_attn_w_out", 1024, 1024, 512),
    "w1": ("L0_ffn_w1_w", 4096, 1024, 512),
    "w2": ("L0_ffn_w2_w", 4096, 1024, 512),
    "h1": ("L0_ffn_h1_w", 4096, 4096, 256),
    "h2": ("L0_ffn_h2_w", 4096, 4096, 256),
    "w3": ("L0_ffn_w3_w", 1024, 4096, 256),
    "ew1": ("L1_ffn_w1", 8192, 1024, 512),
    "ew3": ("L1_ffn_w3", 8192, 1024, 512),
    "ew2": ("L1_ffn_w2", 8192, 1024, 512),
    "sw1": ("L1_ffn_sw1", 2048, 1024, 512),
    "sw3": ("L1_ffn_sw3", 2048, 1024, 512),
    "sw2": ("L1_ffn_sw2", 1024, 2048, 512),
}

COLVECS = {
    "b_in": ("L0_attn_b_in", 3072),
    "b_out": ("L0_attn_b_out", 1024),
    "w1s": ("L0_ffn_w1_s", 4096),
    "w2s": ("L0_ffn_w2_s", 4096),
    "h1s": ("L0_ffn_h1_s", 4096),
    "h2s": ("L0_ffn_h2_s", 4096),
    "w3s": ("L0_ffn_w3_s", 1024),
}


class B:

    def __init__(self, nc, block, sems):
        self.nc = nc
        self.block = block
        self.h = sems
        self.c = {k: 0 for k in CHAINS}
        self.ops = {e: [] for e in ENGS}
        self.pend = {e: [] for e in ENGS}

    def wait(self, eng, sem, val):
        if val > 0:
            self.pend[eng].append((sem, val))

    def wt(self, eng, tok):
        if tok is None:
            return
        if isinstance(tok, (list, tuple)) and tok and \
                not isinstance(tok[0], str):
            for t in tok:
                self.wt(eng, t)
            return
        self.wait(eng, tok[0], tok[1])

    def _emit(self, eng, fn, inc=None, by=1):
        waits = self.pend[eng]
        self.pend[eng] = []
        tok = None
        if inc is not None:
            self.c[inc] += by
            tok = (inc, self.c[inc])
        self.ops[eng].append((waits, fn, inc, by))
        return tok

    def pe(self, fn, inc=False):
        return self._emit("pe", fn, "pe" if inc else None, 1)

    def dve(self, fn):
        self.wait("dve", "dve", self.c["dve"])
        return self._emit("dve", fn, "dve", 1)

    def act(self, fn):
        self.wait("act", "act", self.c["act"])
        return self._emit("act", fn, "act", 1)

    def pool(self, fn):
        self.wait("pool", "pool", self.c["pool"])
        return self._emit("pool", fn, "pool", 1)

    def dma(self, eng, fn, sem=None):
        if sem is None:
            sem = {"sp": "qs", "pool": "qp", "act": "qa"}[eng]
        return self._emit(eng, fn, sem, 16)

    def dma_on(self, eng, sem, fn):
        return self._emit(eng, fn, sem, 16)

    def raw(self, eng, fn):
        return self._emit(eng, fn, None)

    def tok(self, sem):
        return (sem, self.c[sem])

    def barrier(self, engines=ENGS, sems=CHAINS):
        snap = [(s, self.c[s]) for s in sems if self.c[s] > 0]
        for e in engines:
            for s, v in snap:
                self.wait(e, s, v)

    def n_ops(self):
        return sum(len(v) for v in self.ops.values())

    def finalize(self):
        h = self.h

        def mk(eng):
            def body(e):
                for waits, fn, inc, by in self.ops[eng]:
                    for s, v in waits:
                        e.wait_ge(h[s], v)
                    inst = fn(e)
                    if inc is not None:
                        inst.then_inc(h[inc], by)
            return body
        self.block.tensor(mk("pe"))
        self.block.vector(mk("dve"))
        self.block.scalar(mk("act"))
        self.block.gpsimd(mk("pool"))
        self.block.sync(mk("sp"))


class Arena:
    def __init__(self, f32_ap, bf16_ap, nbytes):
        self.vf = f32_ap
        self.vb = bf16_ap
        self.n = nbytes
        self.off = 0
        self.stack = []
        self.peak = 0

    def push(self):
        self.stack.append(self.off)

    def pop(self):
        self.off = self.stack.pop()

    def _take(self, nbytes):
        off = (self.off + 31) & ~31
        self.off = off + nbytes
        self.peak = max(self.peak, self.off)
        assert self.off <= self.n, f"SBUF arena overflow: {self.off} > {self.n}"
        return off

    def f32(self, cols):
        off = self._take(cols * 4)
        return self.vf[:, off // 4: off // 4 + cols]

    def bf16(self, cols):
        off = self._take(cols * 2)
        return self.vb[:, off // 2: off // 2 + cols]


def build_program(phases=("l0a", "l0f", "l1a", "l1f")):
    nc = bass.Bass("TRN2", target_bir_lowering=False)

    ishapes = {
        "x": [NSEQ, SEQ, D],
        "L0_attn_w_in": [3072, 1024], "L0_attn_b_in": [3072],
        "L0_attn_w_out": [1024, 1024], "L0_attn_b_out": [1024],
        "L0_ffn_w1_w": [4096, 1024], "L0_ffn_w1_s": [4096],
        "L0_ffn_w2_w": [4096, 1024], "L0_ffn_w2_s": [4096],
        "L0_ffn_h1_w": [4096, 4096], "L0_ffn_h1_s": [4096],
        "L0_ffn_h2_w": [4096, 4096], "L0_ffn_h2_s": [4096],
        "L0_ffn_w3_w": [1024, 4096], "L0_ffn_w3_s": [1024],
        "L1_attn_g": [512, 1024],
        "L1_ffn_gate_w": [8, 1024], "L1_ffn_gate_bias": [8],
        "L1_ffn_w1": [8192, 1024], "L1_ffn_w3": [8192, 1024],
        "L1_ffn_w2": [8192, 1024],
        "L1_ffn_sw1": [2048, 1024], "L1_ffn_sw3": [2048, 1024],
        "L1_ffn_sw2": [1024, 2048],
        "msk": [128, 384],
        "pwc": [128, 3 * 128],
        "onehot": [8, 8 * 128],
    }
    inp = {k: nc.dram_tensor(k, s, F32, kind="ExternalInput").ap()
           for k, s in ishapes.items()}
    out = nc.dram_tensor("out", [NSEQ, SEQ, D], F32, kind="ExternalOutput").ap()

    cast_bf = {w: nc.dram_tensor("cb_" + w, [v[1], v[2]], BF16,
                                 kind="Internal").ap()
               for w, v in WTS.items()}
    cast_bf["g"] = nc.dram_tensor("cb_g", [512, 1024], BF16, kind="Internal").ap()
    wT = {w: nc.dram_tensor("wT_" + w, [v[1] // v[3], 128, v[2] // 128, v[3]],
                            BF16, kind="Internal").ap()
          for w, v in WTS.items()}

    es = ExitStack()
    ARENA_F32 = 53100
    arena_t = es.enter_context(nc.sbuf_tensor("arena", [128, ARENA_F32], F32))
    A = Arena(arena_t.ap(), arena_t.bitcast(BF16).ap(), ARENA_F32 * 4)
    banks = [es.enter_context(nc.psum_tensor(f"pbank{i}", [128, 512], F32))
             for i in range(8)]

    def bk(i, lo=0, hi=512):
        return banks[i][:, lo:hi]

    def bkb(i, lo=0, hi=1024):
        return banks[i].bitcast(BF16)[:, lo:hi]

    sems = {s: es.enter_context(nc.semaphore("sem_" + s)) for s in CHAINS}
    block = es.enter_context(nc.Block())
    b = B(nc, block, sems)

    with es:
        ident_f = A.f32(128)
        ident_b = A.bf16(128)
        ones_cf = A.f32(1)
        ones_cb = A.bf16(16)
        ones_row = A.f32(128)
        ccols = A.f32(8)
        cols = {"b_in": A.f32(24), "b_inq": A.f32(8), "b_out": A.f32(8),
                "w1s": A.f32(32), "w2s": A.f32(32), "h1s": A.f32(32),
                "h2s": A.f32(32), "w3s": A.f32(8)}
        stream = A.f32(FC * NTOK)
        st3 = stream.rearrange("p (fc t) -> p fc t", fc=FC)

        def ST(fc, t0, n):
            return st3[:, fc, t0:t0 + n]

        nc.const_aps.register(F32, 1e-5, ccols[:, 5:6])

        b.pool(lambda e: e.memset(ident_f, 0.0))
        b.pool(lambda e: e.affine_select(
            out=ident_f, in_=ident_f, compare_op=ALU.not_equal, fill=1.0,
            base=0, pattern=[[-1, 128]], channel_multiplier=1))
        b.pool(lambda e: e.memset(ident_b, 0.0))
        b.pool(lambda e: e.affine_select(
            out=ident_b, in_=ident_b, compare_op=ALU.not_equal, fill=1.0,
            base=0, pattern=[[-1, 128]], channel_multiplier=1))
        b.pool(lambda e: e.memset(ones_cf, 1.0))
        b.pool(lambda e: e.memset(ones_cb[:, 0:1], 1.0))
        b.pool(lambda e: e.memset(ones_row, 0.0))
        b.pool(lambda e: e.memset(ones_row[0:1, :], 1.0))
        for i, v in enumerate([QS, 0.5, 4.0, 0.4, -0.5, 1e-5, 1.0, 0.0]):
            b.pool(lambda e, i=i, v=v: e.memset(ccols[:, i:i + 1], v))
        t_consts = b.tok("pool")

        cast_toks = {}
        for w, (src, O, I, OW) in WTS.items():
            cast_toks[w] = b.dma(
                "pool", lambda e, w=w, src=src:
                e.dma_start(out=cast_bf[w], in_=inp[src]))
        cast_toks["g"] = b.dma(
            "pool", lambda e: e.dma_start(out=cast_bf["g"],
                                          in_=inp["L1_attn_g"]))

        A.push()
        tstage = A.bf16(32 * 256)
        wT_tok = {}
        for w, (src, O, I, OW) in WTS.items():
            IC = I // 128
            b.wt("sp", cast_toks[w])
            for og in range(O // OW):
                last = None
                for ic in range(IC):
                    last = b.dma("sp", lambda e, w=w, og=og, ic=ic, OW=OW:
                                 e.dma_start(
                                     out=tstage[:, ic * OW:(ic + 1) * OW],
                                     in_=cast_bf[w][og * OW:(og + 1) * OW,
                                                    ic * 128:(ic + 1) * 128],
                                     transpose=True))
                b.wt("sp", last)
                st_t = b.dma("sp", lambda e, w=w, og=og, IC=IC, OW=OW:
                             e.dma_start(
                                 out=wT[w][og].rearrange("p ic ow -> p (ic ow)"),
                                 in_=tstage[:, 0:IC * OW]))
                b.wt("sp", st_t)
            wT_tok[w] = b.tok("qs")
        A.pop()

        def load_wt(dst, w, og, sem=None, eng="sp"):
            _, O, I, OW = WTS[w]
            IC = I // 128
            b.wt(eng, wT_tok[w])
            return b.dma(eng, lambda e, w=w, og=og, IC=IC, OW=OW, dst=dst:
                         e.dma_start(
                             out=dst[:, 0:IC * OW],
                             in_=wT[w][og].rearrange("p ic ow -> p (ic ow)")),
                         sem=sem)

        A.push()
        vstage = A.f32(128)
        for name, (src, n) in COLVECS.items():
            nch = n // 128
            ld = b.dma("sp", lambda e, src=src, nch=nch:
                       e.dma_start(out=vstage[0:nch, :],
                                   in_=inp[src].rearrange("(c p) -> c p", p=128)))
            b.wt("pe", ld)
            b.wt("pe", t_consts)
            tk = b.pe(lambda e, nch=nch: e.transpose(
                bk(0, 0, nch), vstage[0:nch, :], ident_f[0:nch, 0:nch]),
                inc=True)
            b.wt("dve", tk)
            cp = b.dve(lambda e, name=name, nch=nch:
                       e.tensor_copy(cols[name][:, 0:nch], bk(0, 0, nch)))
            b.wt("sp", tk)
            b.wt("pe", cp)
        b.dve(lambda e: e.tensor_scalar_mul(cols["b_inq"], cols["b_in"][:, 0:8],
                                            QS))
        A.pop()

        A.push()
        xst = [A.f32(1024), A.f32(1024)]
        last_rd = [None, None]
        NCH = NTOK // 128
        for ch in range(NCH):
            bb, t0 = divmod(ch * 128, SEQ)
            buf = ch % 2
            b.wt("sp", last_rd[buf])
            ld = b.dma("sp", lambda e, bb=bb, t0=t0, buf=buf:
                       e.dma_start(out=xst[buf], in_=inp["x"][bb, t0:t0 + 128, :]))
            b.wt("pe", ld)
            b.wt("pe", t_consts)
            for half in range(2):
                bkid = half
                tk = None
                for j in range(4):
                    fc = half * 4 + j
                    tk = b.pe(lambda e, buf=buf, fc=fc, bkid=bkid, j=j:
                              e.transpose(bk(bkid, j * 128, (j + 1) * 128),
                                          xst[buf][:, fc * 128:(fc + 1) * 128],
                                          ident_f), inc=(j == 3))
                b.wt("dve", tk)
                ev = b.dve(lambda e, ch=ch, half=half, bkid=bkid: e.tensor_copy(
                    st3[:, half * 4:(half + 1) * 4, ch * 128:ch * 128 + 128],
                    bk(bkid).rearrange("p (f t) -> p f t", f=4)))
                b.wt("pe", ev)
            last_rd[buf] = b.tok("pe")
        A.pop()
        b.barrier()

        def cast_tile(dst_bf, t0, n=TT):
            src = st3[:, :, t0:t0 + n]
            d3 = dst_bf.rearrange("p (fc t) -> p fc t", fc=FC)[:, :, 0:n]
            return b.dve(lambda e: e.tensor_copy(d3, src))

        def ln_sweep():
            A.push()
            sqt = A.f32(TT)
            rows = A.f32(512)
            mt = None
            for ti in range(NTILE):
                t0 = ti * TT
                for fc in range(FC):
                    if mt is not None:
                        b.wt("dve", mt)
                    sq = b.dve(lambda e, fc=fc, t0=t0: e.tensor_mul(
                        sqt, ST(fc, t0, TT), ST(fc, t0, TT)))
                    b.wt("pe", sq)
                    b.pe(lambda e, fc=fc, t0=t0: e.matmul(
                        bk(0, 0, TT)[0:1, :], ones_cf, ST(fc, t0, TT),
                        start=(fc == 0), stop=(fc == FC - 1)))
                    mt = b.pe(lambda e, fc=fc: e.matmul(
                        bk(1, 0, TT)[0:1, :], ones_cf, sqt,
                        start=(fc == 0), stop=(fc == FC - 1)), inc=True)
                b.wt("dve", mt)
                b.dve(lambda e: e.tensor_scalar_mul(
                    rows[0:1, 0:TT], bk(0, 0, TT)[0:1, :], 1.0 / D))
                b.dve(lambda e: e.tensor_mul(
                    rows[0:1, 256:512], rows[0:1, 0:TT], rows[0:1, 0:TT]))
                vv = b.dve(lambda e: e.scalar_tensor_tensor(
                    rows[0:1, 256:512], bk(1, 0, TT)[0:1, :], 1.0 / D,
                    rows[0:1, 256:512], op0=ALU.mult, op1=ALU.subtract))
                b.wt("act", vv)
                b.act(lambda e: e.activation(rows[0:1, 256:512],
                                             rows[0:1, 256:512], AF.Ln,
                                             bias=1e-5))
                l2 = b.act(lambda e: e.activation(rows[0:1, 256:512],
                                                  rows[0:1, 256:512], AF.Exp,
                                                  scale=-0.5))
                b.wt("pe", l2)
                bc = b.pe(lambda e: e.matmul(bk(2), ones_row[0:1, :],
                                             rows[0:1, 0:512],
                                             start=True, stop=True), inc=True)
                b.wt("dve", bc)
                lt = None
                for fc in range(FC):
                    b.dve(lambda e, fc=fc, t0=t0: e.tensor_sub(
                        sqt, ST(fc, t0, TT), bk(2, 0, TT)))
                    lt = b.dve(lambda e, fc=fc, t0=t0: e.tensor_mul(
                        ST(fc, t0, TT), sqt, bk(2, 256, 512)))
                b.wt("pe", lt)
            A.pop()
            b.barrier()

        def _ph_l0a():
            A.push()
            msk_b = A.bf16(384)
            w_inT = [A.bf16(8 * 512), A.bf16(8 * 512)]
            k_ring = A.bf16(3 * FC * TT)
            v_ring = A.bf16(3 * 2 * NHEAD * HD)
            q_ring = A.bf16(2 * FC * TT)
            x_bf = A.bf16(FC * TT)
            ctx_bf = A.bf16(FC * TT)
            s_sb = [A.f32(384), A.f32(384)]
            p_bf = [A.bf16(384), A.bf16(384)]
            pT_bf = [A.bf16(384), A.bf16(384)]
            vstg = A.bf16(TT)
            sumc = A.f32(2)
            rinv = A.f32(2)

            b.dma("pool", lambda e: e.dma_start(out=msk_b[:, :], in_=inp["msk"]))
            b.pool(lambda e: e.memset(k_ring, 0.0))
            b.pool(lambda e: e.memset(v_ring, 0.0))
            b.pool(lambda e: e.memset(q_ring, 0.0))
            b.barrier()

            def qsl(sl, fc, lo, n):
                base = sl * 2048 + fc * TT + lo
                return q_ring[:, base:base + n]

            def ksl(sl, fc, lo, n):
                base = sl * 2048 + fc * TT + lo
                return k_ring[:, base:base + n]

            def vsl(sl, th, h):
                base = sl * 2048 + th * 1024 + h * HD
                return v_ring[:, base:base + HD]

            for s in range(NSEQ):
                def qkv(i, s=s):
                    sl = i % 3
                    qid = i % 2
                    t0 = s * SEQ + i * TT
                    b.wait("dve", "pe", b.c["pe"])
                    ct = cast_tile(x_bf, t0)
                    ld = [None, None]
                    qc0 = b.tok("pe")
                    cons = [qc0, qc0]
                    q0 = [b.tok("act"), b.tok("dve")]
                    ev = [q0, q0, q0, q0]

                    def loadQ(og):
                        p = og % 2
                        b.wt("act", cons[p])
                        ld[p] = load_wt(w_inT[p], "w_in", og,
                                        "qa" if p == 0 else "qb", eng="act")
                    loadQ(0)
                    loadQ(1)
                    for oc in range(24):
                        og, ol = divmod(oc, 4)
                        p = og % 2
                        par = oc % 2
                        b.wt("pe", ld[p])
                        b.wt("pe", ct)
                        b.wt("pe", ev[par])
                        mmt = None
                        for fc in range(FC):
                            mmt = b.pe(lambda e, ol=ol, fc=fc, p=p, par=par:
                                       e.matmul(
                                bk(par, 0, TT),
                                w_inT[p][:, fc * 512 + ol * 128:
                                         fc * 512 + ol * 128 + 128],
                                x_bf[:, fc * TT:(fc + 1) * TT],
                                start=(fc == 0), stop=(fc == FC - 1)),
                                inc=(fc == FC - 1))
                        b.wt("act", mmt)
                        b.wait("act", "pe", b.c["pe"])
                        if oc < 8:
                            ev[par] = b.act(lambda e, oc=oc, qid=qid, par=par:
                                            e.activation(
                                qsl(qid, oc, 0, TT), bk(par, 0, TT),
                                AF.Identity,
                                bias=cols["b_inq"][:, oc:oc + 1], scale=QS))
                        elif oc < 16:
                            ev[par] = b.act(lambda e, oc=oc, sl=sl, par=par:
                                            e.activation(
                                ksl(sl, oc - 8, 0, TT), bk(par, 0, TT),
                                AF.Identity, bias=cols["b_in"][:, oc:oc + 1]))
                        else:
                            c = oc - 16
                            vpar = c % 2
                            evv = b.act(lambda e, oc=oc, par=par:
                                        e.activation(
                                vstg, bk(par, 0, TT), AF.Identity,
                                bias=cols["b_in"][:, oc:oc + 1]))
                            ev[par] = evv
                            for th in range(2):
                                b.wt("pe", evv)
                                b.wt("pe", ev[2 + vpar])
                                tr = b.pe(lambda e, th=th, vpar=vpar:
                                          e.transpose(
                                    bkb(2 + vpar, th * 128, (th + 1) * 128),
                                    vstg[:, th * 128:(th + 1) * 128],
                                    ident_b), inc=True)
                                b.wt("dve", tr)
                                b.wait("dve", "pe", b.c["pe"])
                                ev[2 + vpar] = b.dve(
                                    lambda e, th=th, sl=sl, c=c, vpar=vpar:
                                    e.tensor_copy(
                                        v_ring[:, sl * 2048 + th * 1024
                                               + c * 128:
                                               sl * 2048 + th * 1024
                                               + c * 128 + 128],
                                        bkb(2 + vpar, th * 128,
                                            (th + 1) * 128)))
                        if ol == 3 and og + 2 < 6:
                            cons[p] = b.tok("pe")
                            loadQ(og + 2)

                def attn(i, s=s):
                    slp, slc, sln = (i - 1) % 3, i % 3, (i + 1) % 3
                    qid = i % 2
                    a0 = [b.tok("act"), b.tok("dve")]
                    tok_ma = [a0, a0]
                    tok_ex = [None, None]
                    tok_pb = [a0, a0]
                    tok_pt = [a0, a0]
                    tok_cev = [a0, a0]
                    tok_trs = [b.tok("pe"), b.tok("pe")]
                    tok_cm = [b.tok("pe"), b.tok("pe")]
                    b.wait("pe", "act", b.c["act"])
                    b.wait("pe", "dve", b.c["dve"])
                    for qb in range(2):
                        qg = i * 2 + qb
                        mv = 0 if qg == 0 else (2 if qg == 2 * TPS - 1 else 1)
                        for h in range(NHEAD):
                            hp = (qb * NHEAD + h) % 2
                            bS, bT, bC = (0, 1, 2) if hp == 0 else (3, 4, 5)
                            fcq = h // 2
                            r0 = 64 * (h % 2)
                            if qb == 0:
                                kp = [(slp, 128, 128), (slc, 0, 256)]
                                vp = [(slp, 1), (slc, 0), (slc, 1)]
                            else:
                                kp = [(slc, 0, 128), (slc, 128, 256 - 128),
                                      (sln, 0, 128)]
                                vp = [(slc, 0), (slc, 1), (sln, 0)]
                            qAP = q_ring[r0:r0 + 64,
                                         qid * 2048 + fcq * TT + qb * 128:
                                         qid * 2048 + fcq * TT + qb * 128 + 128]
                            b.wt("pe", tok_ma[hp])
                            co = 0
                            mmt = None
                            for (ksl_, klo, kn) in kp:
                                mmt = b.pe(lambda e, ksl_=ksl_, klo=klo, kn=kn,
                                           co=co, qAP=qAP, r0=r0, fcq=fcq,
                                           bS=bS: e.matmul(
                                    bk(bS, co, co + kn), qAP,
                                    k_ring[r0:r0 + 64,
                                           ksl_ * 2048 + fcq * TT + klo:
                                           ksl_ * 2048 + fcq * TT + klo + kn],
                                    start=True, stop=True),
                                    inc=(co + kn == 384))
                                co += kn
                            b.wt("dve", mmt)
                            if mv == 0:
                                b.dve(lambda e, hp=hp: e.memset(
                                    s_sb[hp][:, 0:128], -1e9))
                                ma = b.dve(lambda e, hp=hp, bS=bS: e.tensor_add(
                                    s_sb[hp][:, 128:384], bk(bS, 128, 384),
                                    msk_b[:, 128:384]))
                            elif mv == 2:
                                b.dve(lambda e, hp=hp: e.memset(
                                    s_sb[hp][:, 256:384], -1e9))
                                ma = b.dve(lambda e, hp=hp, bS=bS: e.tensor_add(
                                    s_sb[hp][:, 0:256], bk(bS, 0, 256),
                                    msk_b[:, 0:256]))
                            else:
                                ma = b.dve(lambda e, hp=hp, bS=bS: e.tensor_add(
                                    s_sb[hp], bk(bS, 0, 384), msk_b))
                            tok_ma[hp] = ma
                            b.wt("act", ma)
                            b.wt("act", tok_pb[hp])
                            ex = b.act(lambda e, hp=hp: e.activation(
                                s_sb[hp], s_sb[hp], AF.Exp,
                                accum_out=sumc[:, hp:hp + 1]))
                            tok_ex[hp] = ex
                            b.wt("dve", ex)
                            b.dve(lambda e, hp=hp: e.reciprocal(
                                rinv[:, hp:hp + 1], sumc[:, hp:hp + 1]))
                            b.wt("dve", tok_trs[hp])
                            pb = b.dve(lambda e, hp=hp: e.tensor_scalar_mul(
                                p_bf[hp], s_sb[hp], rinv[:, hp:hp + 1]))
                            tok_pb[hp] = pb
                            b.wt("pe", pb)
                            b.wt("pe", tok_pt[hp])
                            trs = None
                            for j in range(3):
                                trs = b.pe(lambda e, j=j, hp=hp, bT=bT:
                                           e.transpose(
                                    bkb(bT, j * 128, (j + 1) * 128),
                                    p_bf[hp][:, j * 128:(j + 1) * 128],
                                    ident_b), inc=(j == 2))
                            tok_trs[hp] = trs
                            b.wt("dve", trs)
                            b.wt("dve", tok_cm[hp])
                            pt = b.dve(lambda e, hp=hp, bT=bT: e.tensor_copy(
                                pT_bf[hp], bkb(bT, 0, 384)))
                            tok_pt[hp] = pt
                            b.wt("pe", pt)
                            b.wt("pe", tok_cev[hp])
                            cm = None
                            for j, (vsl_, vth) in enumerate(vp):
                                cm = b.pe(lambda e, vsl_=vsl_, vth=vth, h=h,
                                          j=j, hp=hp, bC=bC: e.matmul(
                                    bk(bC, 0, 128)[0:64, :], vsl(vsl_, vth, h),
                                    pT_bf[hp][:, j * 128:(j + 1) * 128],
                                    start=(j == 0), stop=(j == 2)),
                                    inc=(j == 2))
                            tok_cm[hp] = cm
                            b.wt("dve", cm)
                            cev = b.dve(lambda e, fcq=fcq, r0=r0, qb=qb,
                                        hp=hp, bC=bC: e.tensor_copy(
                                ctx_bf[r0:r0 + 64, fcq * TT + qb * 128:
                                       fcq * TT + qb * 128 + 128],
                                bk(bC, 0, 128)[0:64, :]))
                            tok_cev[hp] = cev

                def outproj(i, s=s):
                    t0 = s * SEQ + i * TT
                    b.wait("pe", "dve", b.c["dve"])
                    b.wait("act", "pe", b.c["pe"])
                    wo1 = load_wt(w_inT[0], "w_out", 0, "qa", eng="act")
                    wo2 = load_wt(w_inT[1], "w_out", 1, "qb", eng="act")
                    b.wt("pe", wo1)
                    b.wt("pe", wo2)
                    o0 = [b.tok("act"), b.tok("dve")]
                    evo = [o0, o0]
                    for oc in range(FC):
                        par = oc % 2
                        b.wt("pe", evo[par])
                        mmt = None
                        for fc in range(FC):
                            mmt = b.pe(lambda e, oc=oc, fc=fc, par=par:
                                       e.matmul(
                                bk(par, 0, TT),
                                w_inT[oc // 4][:, fc * 512 + (oc % 4) * 128:
                                               fc * 512 + (oc % 4) * 128 + 128],
                                ctx_bf[:, fc * TT:(fc + 1) * TT],
                                start=(fc == 0), stop=(fc == FC - 1)),
                                inc=(fc == FC - 1))
                        b.wt("dve", mmt)
                        b.dve(lambda e, oc=oc, t0=t0, par=par:
                              e.scalar_tensor_tensor(
                            s_sb[0][:, 0:TT], bk(par, 0, TT),
                            cols["b_out"][:, oc:oc + 1],
                            ST(oc, t0, TT), op0=ALU.add, op1=ALU.add))
                        evo[par] = b.dve(lambda e, oc=oc, t0=t0, par=par:
                                         e.tensor_add(
                            ST(oc, t0, TT), ST(oc, t0, TT), s_sb[0][:, 0:TT]))

                qkv(0)
                qkv(1)
                for i in range(TPS):
                    attn(i)
                    outproj(i)
                    if i + 2 < TPS:
                        qkv(i + 2)
            A.pop()
            b.barrier()
            ln_sweep()

        def _ph_l0f():
            A.push()
            h_buf = A.bf16(32 * TT)
            u_buf = A.bf16(32 * TT)
            x_bf = A.bf16(FC * TT)
            wb = [A.bf16(8192), A.bf16(8192)]
            silu_t = [A.f32(TT), A.f32(TT)]
            for ti in range(NTILE):
                t0 = ti * TT
                b.wait("dve", "pe", b.c["pe"])
                ct = cast_tile(x_bf, t0)

                ld = [None, None]
                c0 = b.tok("pe")
                cons = [c0, c0]
                e0 = [b.tok("act"), b.tok("dve")]
                ev = [e0, e0]

                def loadA(og):
                    p = og % 2
                    b.wt("sp", cons[p])
                    q = "qs" if p == 0 else "qa"
                    load_wt(wb[p][:, 0:4096], "w1", og, q)
                    ld[p] = load_wt(wb[p][:, 4096:8192], "w2", og, q)
                loadA(0)
                loadA(1)
                for og in range(8):
                    p = og % 2
                    for ol in range(4):
                        oc = og * 4 + ol
                        par = oc % 2
                        bA, bB = (0, 1) if par == 0 else (2, 3)
                        b.wt("pe", ld[p])
                        b.wt("pe", ct)
                        b.wt("pe", ev[par])
                        ta = None
                        for fc in range(FC):
                            ta = b.pe(lambda e, ol=ol, fc=fc, p=p, bA=bA:
                                      e.matmul(
                                bk(bA, 0, TT),
                                wb[p][:, fc * 512 + ol * 128:
                                      fc * 512 + ol * 128 + 128],
                                x_bf[:, fc * TT:(fc + 1) * TT],
                                start=(fc == 0), stop=(fc == FC - 1)),
                                inc=(fc == FC - 1))
                        tb = None
                        for fc in range(FC):
                            tb = b.pe(lambda e, ol=ol, fc=fc, p=p, bB=bB:
                                      e.matmul(
                                bk(bB, 0, TT),
                                wb[p][:, 4096 + fc * 512 + ol * 128:
                                      4096 + fc * 512 + ol * 128 + 128],
                                x_bf[:, fc * TT:(fc + 1) * TT],
                                start=(fc == 0), stop=(fc == FC - 1)),
                                inc=(fc == FC - 1))
                        b.wt("act", ta)
                        sa = b.act(lambda e, oc=oc, par=par, bA=bA:
                                   e.activation(
                            silu_t[par], bk(bA, 0, TT), AF.Silu,
                            scale=cols["w1s"][:, oc:oc + 1]))
                        b.wt("dve", tb)
                        b.wt("dve", sa)
                        hv = b.dve(lambda e, oc=oc, par=par, bB=bB:
                                   e.scalar_tensor_tensor(
                            h_buf[:, oc * TT:(oc + 1) * TT], bk(bB, 0, TT),
                            cols["w2s"][:, oc:oc + 1], silu_t[par],
                            op0=ALU.mult, op1=ALU.mult))
                        ev[par] = [sa, hv]
                    cons[p] = b.tok("pe")
                    if og + 2 < 8:
                        loadA(og + 2)

                def hphase(mat, src_buf, dst_buf, scol, n_og=16):
                    ld = [None, None]
                    c0 = b.tok("pe")
                    cons = [c0, c0]
                    e0 = [b.tok("act"), b.tok("dve")]
                    ev = [e0, e0]

                    def loadH(og):
                        p = og % 2
                        b.wt("sp", cons[p])
                        ld[p] = load_wt(wb[p], mat, og,
                                        "qs" if p == 0 else "qa")
                    loadH(0)
                    loadH(1)
                    for og in range(n_og):
                        p = og % 2
                        for ol in range(2):
                            oc = og * 2 + ol
                            par = oc % 2
                            b.wt("pe", ld[p])
                            b.wt("pe", ev[par])
                            tu = None
                            for ic in range(32):
                                tu = b.pe(lambda e, ol=ol, ic=ic, p=p, par=par:
                                          e.matmul(
                                    bk(par, 0, TT),
                                    wb[p][:, ic * 256 + ol * 128:
                                          ic * 256 + ol * 128 + 128],
                                    src_buf[:, ic * TT:(ic + 1) * TT],
                                    start=(ic == 0), stop=(ic == 31)),
                                    inc=(ic == 31))
                            b.wt("act", tu)
                            ug = b.act(lambda e, oc=oc, par=par: e.activation(
                                dst_buf[:, oc * TT:(oc + 1) * TT],
                                bk(par, 0, TT),
                                AF.Gelu, scale=cols[scol][:, oc:oc + 1]))
                            ev[par] = ug
                        cons[p] = b.tok("pe")
                        if og + 2 < n_og:
                            loadH(og + 2)

                hphase("h1", h_buf, u_buf, "h1s")
                hphase("h2", u_buf, h_buf, "h2s")

                ld = [None, None]
                c0 = b.tok("pe")
                cons = [c0, c0]
                e0 = [b.tok("act"), b.tok("dve")]
                ev = [e0, e0]

                def loadW3(og):
                    p = og % 2
                    b.wt("sp", cons[p])
                    ld[p] = load_wt(wb[p], "w3", og, "qs" if p == 0 else "qa")
                loadW3(0)
                loadW3(1)
                for og in range(4):
                    p = og % 2
                    for ol in range(2):
                        oc = og * 2 + ol
                        par = oc % 2
                        b.wt("pe", ld[p])
                        b.wt("pe", ev[par])
                        ty = None
                        for ic in range(32):
                            ty = b.pe(lambda e, ol=ol, ic=ic, p=p, par=par:
                                      e.matmul(
                                bk(par, 0, TT),
                                wb[p][:, ic * 256 + ol * 128:
                                      ic * 256 + ol * 128 + 128],
                                h_buf[:, ic * TT:(ic + 1) * TT],
                                start=(ic == 0), stop=(ic == 31)),
                                inc=(ic == 31))
                        b.wt("dve", ty)
                        yd = b.dve(lambda e, oc=oc, t0=t0, par=par:
                                   e.scalar_tensor_tensor(
                            ST(oc, t0, TT), bk(par, 0, TT),
                            cols["w3s"][:, oc:oc + 1], ST(oc, t0, TT),
                            op0=ALU.mult, op1=ALU.add))
                        ev[par] = yd
                    cons[p] = b.tok("pe")
                    if og + 2 < 4:
                        loadW3(og + 2)
            A.pop()
            b.barrier()
            ln_sweep()


        def _ph_l1a():
            A.push()
            pwc_f = A.f32(3 * 128)
            G_tm = A.bf16(4 * 1024)
            GhatT = A.bf16(8 * 512)
            Gh3 = GhatT.rearrange("p (fc g) -> p fc g", fc=FC)
            scr4 = A.f32(1024)
            gh_tmp = A.bf16(1024)
            d_bf = A.bf16(8 * 128)
            s_sb = A.f32(512)
            gw_bf = A.bf16(512)
            gwT_sb = A.bf16(512)
            infl_bf = A.bf16(8 * 128)
            nrm_t = A.bf16(128)
            bl_ring = A.bf16(2 * 1024)
            C_cur = A.bf16(128)
            C_prev = A.bf16(128)
            stg = A.f32(128)
            traj = A.f32(1024)
            cb = A.f32(16)
            (C_MAG, C_E4, C_T, C_MW0, C_MW1, C_MX, C_REC, C_I2, C_I03,
             C_SUM, C_RI, C_W, C_IW, C_S2, C_GN) = range(15)

            def col(i):
                return cb[:, i:i + 1]

            pld = b.dma("pool", lambda e: e.dma_start(out=pwc_f[:, :],
                                                      in_=inp["pwc"]))
            b.pool(lambda e: e.memset(stg, 0.0))
            b.wt("sp", cast_toks["g"])
            gld = b.dma("sp", lambda e: e.dma_start(
                out=G_tm.rearrange("p (gc f) -> p gc f", gc=4),
                in_=cast_bf["g"].rearrange("(gc p) f -> p gc f", p=128)))
            b.wt("act", gld)
            for gc in range(4):
                sq = b.act(lambda e, gc=gc: e.activation(
                    scr4, G_tm[:, gc * 1024:(gc + 1) * 1024], AF.Square,
                    accum_out=col(C_S2)))
                b.wt("act", sq)
                b.act(lambda e: e.activation(col(C_GN), col(C_S2), AF.Ln))
                gn = b.act(lambda e: e.activation(col(C_GN), col(C_GN),
                                                  AF.Exp, scale=0.5))
                b.wt("dve", gn)
                b.dve(lambda e: e.tensor_scalar_max(col(C_MX), col(C_GN), 1e-8))
                rc = b.dve(lambda e: e.reciprocal(col(C_REC), col(C_MX)))
                gh = b.dve(lambda e, gc=gc: e.tensor_scalar_mul(
                    gh_tmp, G_tm[:, gc * 1024:(gc + 1) * 1024],
                    col(C_REC)[:, 0:1]))
                b.wt("pe", gh)
                tk = None
                for fc in range(FC):
                    tk = b.pe(lambda e, fc=fc: e.transpose(
                        bkb(0, fc * 128, (fc + 1) * 128),
                        gh_tmp[:, fc * 128:(fc + 1) * 128], ident_b),
                        inc=(fc == FC - 1))
                b.wt("dve", tk)
                ev = b.dve(lambda e, gc=gc: e.tensor_copy(
                    Gh3[:, :, gc * 128:(gc + 1) * 128],
                    bkb(0).rearrange("p (f g) -> p f g", f=FC)))
                b.wt("pe", ev)
                b.wt("act", ev)
            b.barrier()

            for s in range(NSEQ):
                for ci in range(16):
                    t0 = s * SEQ + ci * 128
                    par = ci % 2
                    nd = 127 if ci == 15 else 128
                    b.wait("dve", "pe", b.c["pe"])
                    dts = []
                    for fc in range(FC):
                        dts.append(b.dve(lambda e, fc=fc, t0=t0, nd=nd:
                                         e.tensor_sub(
                            d_bf[:, fc * 128:fc * 128 + nd],
                            ST(fc, t0 + 1, nd), ST(fc, t0, nd))))
                    if ci == 15:
                        b.wait("pool", "dve", b.c["dve"])
                        for fc in range(FC):
                            b.pool(lambda e, fc=fc: e.memset(
                                d_bf[:, fc * 128 + 127:fc * 128 + 128], 0.0))
                    t_pool = b.tok("pool")
                    b.wt("dve", t_pool)
                    mt = None
                    for fc in range(FC):
                        if mt is not None:
                            b.wt("dve", mt)
                        sq = b.dve(lambda e, fc=fc: e.tensor_mul(
                            scr4[:, 0:128], d_bf[:, fc * 128:(fc + 1) * 128],
                            d_bf[:, fc * 128:(fc + 1) * 128]))
                        b.wt("pe", sq)
                        mt = b.pe(lambda e, fc=fc: e.matmul(
                            bk(0, 0, 128)[0:1, :], ones_cf, scr4[:, 0:128],
                            start=(fc == 0), stop=(fc == FC - 1)), inc=True)
                    b.wt("dve", mt)
                    cp = b.dve(lambda e: e.tensor_copy(stg[0:1, :],
                                                       bk(0, 0, 128)[0:1, :]))
                    b.wt("pe", cp)
                    tr1 = b.pe(lambda e: e.transpose(bk(1, 0, 128), stg,
                                                     ident_f), inc=True)
                    b.wt("act", tr1)
                    b.act(lambda e: e.activation(col(C_MAG), bk(1, 0, 128)[:, 0:1],
                                                 AF.Ln))
                    mg = b.act(lambda e: e.activation(col(C_MAG), col(C_MAG),
                                                      AF.Exp, scale=0.5))
                    e4 = b.act(lambda e: e.activation(col(C_E4), col(C_MAG),
                                                      AF.Exp, scale=4.0))
                    b.wt("dve", e4)
                    b.dve(lambda e: e.tensor_scalar_add(col(C_T), col(C_E4), 1.0))
                    b.dve(lambda e: e.reciprocal(col(C_T), col(C_T)))
                    mwc = col(C_MW0) if par == 0 else col(C_MW1)
                    b.dve(lambda e, mwc=mwc: e.tensor_scalar(
                        mwc, col(C_T), -2.0, 1.0, op0=ALU.mult, op1=ALU.add))
                    b.dve(lambda e: e.tensor_scalar_max(col(C_MX), col(C_MAG),
                                                        1e-6))
                    b.dve(lambda e: e.reciprocal(col(C_REC), col(C_MX)))
                    b.dve(lambda e: e.tensor_scalar_mul(col(C_I2), col(C_REC),
                                                        2.0))
                    iv3 = b.dve(lambda e: e.tensor_scalar_mul(
                        col(C_I03), col(C_REC), 0.6))
                    b.wt("pe", iv3)
                    b.wt("pe", t_pool)
                    dd = None
                    for fc in range(FC):
                        dd = b.pe(lambda e, fc=fc: e.matmul(
                            bk(2, 0, 512), d_bf[:, fc * 128:(fc + 1) * 128],
                            GhatT[:, fc * 512:(fc + 1) * 512],
                            start=(fc == 0), stop=(fc == FC - 1)),
                            inc=(fc == FC - 1))
                    b.wt("dve", dd)
                    sm = b.dve(lambda e: e.tensor_scalar_mul(
                        s_sb, bk(2, 0, 512), col(C_I2)[:, 0:1]))
                    b.wt("act", sm)
                    ex = b.act(lambda e: e.activation(s_sb, s_sb, AF.Exp,
                                                      accum_out=col(C_SUM)))
                    b.wt("dve", ex)
                    b.dve(lambda e: e.reciprocal(col(C_RI), col(C_SUM)))
                    gw = b.dve(lambda e: e.tensor_scalar_mul(
                        gw_bf, s_sb, col(C_RI)[:, 0:1]))
                    b.wt("pe", gw)
                    tk = None
                    for j in range(4):
                        tk = b.pe(lambda e, j=j: e.transpose(
                            bkb(3, j * 128, (j + 1) * 128),
                            gw_bf[:, j * 128:(j + 1) * 128], ident_b),
                            inc=(j == 3))
                    b.wt("dve", tk)
                    gt = b.dve(lambda e: e.tensor_copy(gwT_sb, bkb(3, 0, 512)))
                    b.wt("pe", gt)
                    ift = None
                    for fc in range(FC):
                        for gc in range(4):
                            ift = b.pe(lambda e, fc=fc, gc=gc: e.matmul(
                                bk(4, 0, 128),
                                G_tm[:, gc * 1024 + fc * 128:
                                     gc * 1024 + fc * 128 + 128],
                                gwT_sb[:, gc * 128:(gc + 1) * 128],
                                start=(gc == 0), stop=(gc == 3)),
                                inc=(gc == 3))
                        b.wt("act", ift)
                        fe = b.act(lambda e, fc=fc: e.activation(
                            infl_bf[:, fc * 128:(fc + 1) * 128], bk(4, 0, 128),
                            AF.Identity, scale=0.4))
                        b.wt("pe", fe)
                    for fc in range(FC):
                        trd = b.pe(lambda e, fc=fc: e.transpose(
                            bkb(5, 0, 128), d_bf[:, fc * 128:(fc + 1) * 128],
                            ident_b), inc=True)
                        b.wt("dve", trd)
                        nt = b.dve(lambda e: e.tensor_scalar_mul(
                            nrm_t, bkb(5, 0, 128), col(C_I03)[:, 0:1]))
                        b.wt("pe", nt)
                        tri = b.pe(lambda e, fc=fc: e.transpose(
                            bkb(5, 128, 256), infl_bf[:, fc * 128:(fc + 1) * 128],
                            ident_b), inc=True)
                        b.wt("dve", tri)
                        bl = b.dve(lambda e, fc=fc, par=par: e.tensor_add(
                            bl_ring[:, par * 1024 + fc * 128:
                                    par * 1024 + fc * 128 + 128],
                            nrm_t, bkb(5, 128, 256)))
                        b.wt("pe", bl)
                    b.wt("dve", pld)
                    pv = 0 if ci == 0 else 1
                    cc = b.dve(lambda e, pv=pv, mwc=mwc: e.tensor_scalar_mul(
                        C_cur, pwc_f[:, pv * 128:(pv + 1) * 128], mwc[:, 0:1]))
                    cpv = None
                    if ci > 0:
                        mwp = col(C_MW1) if par == 0 else col(C_MW0)
                        cpv = b.dve(lambda e, mwp=mwp: e.tensor_scalar_mul(
                            C_prev, pwc_f[:, 256:384], mwp[:, 0:1]))
                    b.wt("pe", cc)
                    b.wt("pe", cpv)
                    ws = b.pe(lambda e, ci=ci: e.matmul(
                        bk(0, 0, 128)[0:1, :], ones_cb[:, 0:1], C_cur,
                        start=True, stop=(ci == 0)), inc=(ci == 0))
                    if ci > 0:
                        ws = b.pe(lambda e: e.matmul(
                            bk(0, 0, 128)[0:1, :], ones_cb[:, 0:1], C_prev,
                            start=False, stop=True), inc=True)
                    b.wt("dve", ws)
                    cp2 = b.dve(lambda e: e.tensor_copy(stg[0:1, :],
                                                        bk(0, 0, 128)[0:1, :]))
                    b.wt("pe", cp2)
                    tr2 = b.pe(lambda e: e.transpose(bk(1, 0, 128), stg,
                                                     ident_f), inc=True)
                    b.wt("dve", tr2)
                    b.dve(lambda e: e.tensor_scalar_max(col(C_W),
                                                        bk(1, 0, 128)[:, 0:1],
                                                        1e-8))
                    iw = b.dve(lambda e: e.reciprocal(col(C_IW), col(C_W)))
                    for fh in range(2):
                        tj = None
                        if ci > 0:
                            tj = b.pe(lambda e, fh=fh, par=par: e.matmul(
                                bk(6, 0, 512), C_prev,
                                bl_ring[:, (1 - par) * 1024 + fh * 512:
                                        (1 - par) * 1024 + fh * 512 + 512],
                                start=True, stop=False))
                        tj = b.pe(lambda e, fh=fh, par=par, ci=ci: e.matmul(
                            bk(6, 0, 512), C_cur,
                            bl_ring[:, par * 1024 + fh * 512:
                                    par * 1024 + fh * 512 + 512],
                            start=(ci == 0), stop=True), inc=True)
                        b.wt("dve", tj)
                        b.wt("dve", iw)
                        tv = b.dve(lambda e, fh=fh: e.tensor_scalar_mul(
                            traj[:, fh * 512:(fh + 1) * 512], bk(6, 0, 512),
                            col(C_IW)[:, 0:1]))
                        b.wt("pe", tv)
                    for fc in range(FC):
                        tt = b.pe(lambda e, fc=fc: e.transpose(
                            bk(7, 0, 128), traj[:, fc * 128:(fc + 1) * 128],
                            ident_f), inc=True)
                        b.wt("dve", tt)
                        ad = b.dve(lambda e, fc=fc, t0=t0: e.tensor_add(
                            ST(fc, t0, 128), ST(fc, t0, 128), bk(7, 0, 128)))
                        b.wt("pe", ad)
            A.pop()
            b.barrier()
            ln_sweep()

        def _ph_l1f():
            A.push()
            oneh = A.f32(1024)
            gwTv = A.f32(64)
            gbias = A.f32(8)
            A.push()
            gb_row = A.f32(8)
            gw_nat = A.f32(1024)
            A.pop()
            x_bf = A.bf16(FC * TT)
            wp = [A.bf16(8192), A.bf16(8192)]
            h_bf = A.bf16(16 * TT)
            g_bc = A.bf16(8 * TT)
            y_sb = A.f32(8 * TT)
            gat = A.f32(NTILE * 2 * 8)
            gT_sb = A.f32(128)
            silu_t = [A.f32(TT), A.f32(TT)]
            t2 = A.f32(TT)
            s8 = A.f32(8)
            sb8 = A.f32(8)
            i1 = A.f32(8)
            i2 = A.f32(8)
            mk8 = A.f32(8)
            c8 = A.f32(4)
            z_tm = A.f32(1024)
            obuf = A.f32(1024)

            old = b.dma("pool", lambda e: e.dma_start(out=oneh[0:8, :],
                                                      in_=inp["onehot"]))
            l1 = b.dma("sp", lambda e: e.dma_start(
                out=gb_row[0:1, :],
                in_=inp["L1_ffn_gate_bias"].rearrange("(a b) -> a b", a=1)))
            l2 = b.dma("sp", lambda e: e.dma_start(out=gw_nat[0:8, :],
                                                   in_=inp["L1_ffn_gate_w"]))
            b.wt("pe", l1)
            b.wt("pe", l2)
            tk = b.pe(lambda e: e.matmul(bk(1, 0, 8), ones_row[0:1, :],
                                         gb_row[0:1, :], start=True, stop=True),
                      inc=True)
            b.wt("dve", tk)
            cp = b.dve(lambda e: e.tensor_copy(gbias, bk(1, 0, 8)))
            b.wt("pe", l2)
            b.wt("pe", cp)
            for fc in range(FC):
                tk = b.pe(lambda e, fc=fc: e.transpose(
                    bk(2, fc * 8, fc * 8 + 8),
                    gw_nat[0:8, fc * 128:(fc + 1) * 128], ident_f[0:8, 0:8]),
                    inc=(fc == FC - 1))
            b.wt("dve", tk)
            cp2 = b.dve(lambda e: e.tensor_copy(gwTv, bk(2, 0, 64)))
            b.wt("pe", cp2)
            b.barrier()

            for tch in range(NTILE * 2):
                t0 = tch * 128
                lg = None
                for fc in range(FC):
                    lg = b.pe(lambda e, fc=fc, t0=t0: e.matmul(
                        bk(0, 0, 8), ST(fc, t0, 128),
                        gwTv[:, fc * 8:(fc + 1) * 8],
                        start=(fc == 0), stop=(fc == FC - 1)),
                        inc=(fc == FC - 1))
                b.wt("act", lg)
                b.wait("act", "dve", b.c["dve"])
                ex = b.act(lambda e: e.activation(s8, bk(0, 0, 8), AF.Exp,
                                                  accum_out=c8[:, 0:1]))
                b.wt("pe", ex)
                b.wt("dve", ex)
                b.dve(lambda e: e.reciprocal(c8[:, 1:2], c8[:, 0:1]))
                b.dve(lambda e: e.tensor_scalar_mul(s8, s8, c8[:, 1:2]))
                b.dve(lambda e: e.tensor_add(sb8, s8, gbias))
                b.dve(lambda e: e.reduce_max(c8[:, 2:3], sb8, axis=AX.X))
                b.dve(lambda e: e.tensor_scalar(i1, sb8, c8[:, 2:3], None,
                                                op0=ALU.is_ge))
                b.dve(lambda e: e.scalar_tensor_tensor(
                    mk8, i1, -1e9, sb8, op0=ALU.mult, op1=ALU.add))
                b.dve(lambda e: e.reduce_max(c8[:, 3:4], mk8, axis=AX.X))
                b.dve(lambda e: e.tensor_scalar(i2, mk8, c8[:, 3:4], None,
                                                op0=ALU.is_ge))
                b.dve(lambda e: e.tensor_add(i1, i1, i2))
                b.dve(lambda e, tch=tch: e.tensor_mul(
                    gat[:, tch * 8:(tch + 1) * 8], s8, i1))

            prev_store = None
            for ti in range(NTILE):
                t0 = ti * TT
                b.wait("dve", "pe", b.c["pe"])
                ct = cast_tile(x_bf, t0)
                b.wait("pool", "dve", b.c["dve"])
                ym = b.pool(lambda e: e.memset(y_sb, 0.0))
                for tc in range(2):
                    tch = ti * 2 + tc
                    b.wait("pe", "dve", b.c["dve"])
                    tk = b.pe(lambda e, tch=tch: e.transpose(
                        bk(1, 0, 128)[0:8, :], gat[:, tch * 8:(tch + 1) * 8],
                        ident_f), inc=True)
                    b.wt("dve", tk)
                    cp = b.dve(lambda e: e.tensor_copy(gT_sb[0:8, :],
                                                       bk(1, 0, 128)[0:8, :]))
                    b.wt("pe", cp)
                    for e_ in range(8):
                        bc = b.pe(lambda e, e_=e_: e.matmul(
                            bk(2, 0, 128), oneh[0:8, e_ * 128:(e_ + 1) * 128],
                            gT_sb[0:8, :], start=True, stop=True), inc=True)
                        b.wt("dve", bc)
                        ev = b.dve(lambda e, e_=e_, tc=tc: e.tensor_copy(
                            g_bc[:, e_ * TT + tc * 128:e_ * TT + tc * 128 + 128],
                            bk(2, 0, 128)))
                        b.wt("pe", ev)
                gen_loads = []
                for e_ in range(8):
                    for og in (2 * e_, 2 * e_ + 1):
                        gen_loads.append(lambda buf, q, og=og: (
                            load_wt(buf[:, 0:4096], "ew1", og, q),
                            load_wt(buf[:, 4096:8192], "ew3", og, q))[-1])
                    gen_loads.append(lambda buf, q, e_=e_: (
                        load_wt(buf[:, 0:4096], "ew2", 2 * e_, q),
                        load_wt(buf[:, 4096:8192], "ew2", 2 * e_ + 1, q))[-1])
                for og in range(4):
                    gen_loads.append(lambda buf, q, og=og: (
                        load_wt(buf[:, 0:4096], "sw1", og, q),
                        load_wt(buf[:, 4096:8192], "sw3", og, q))[-1])
                for og in range(2):
                    gen_loads.append(lambda buf, q, og=og:
                                     load_wt(buf, "sw2", og, q))

                pc0 = b.tok("pe")
                pstate = {"ld": [None, None], "cons": [pc0, pc0]}

                def pissue(i):
                    p = i % 2
                    b.wt("sp", pstate["cons"][p])
                    q = "qs" if p == 0 else "qa"
                    pstate["ld"][p] = gen_loads[i](wp[p], q)

                def pconsume(i):
                    b.wt("pe", pstate["ld"][i % 2])
                    return i % 2

                def pdone(i):
                    pstate["cons"][i % 2] = b.tok("pe")
                    if i + 2 < len(gen_loads):
                        pissue(i + 2)

                pissue(0)
                pissue(1)
                gi = 0
                eh0 = [b.tok("act"), b.tok("dve")]
                evh = [eh0, eh0]
                evy = [eh0, eh0]

                def silu_block(p, oc, hslot, gate_e):
                    par = oc % 2
                    bA, bB = (0, 1) if par == 0 else (2, 3)
                    b.wt("pe", ct)
                    b.wt("pe", evh[par])
                    ta = None
                    for fc in range(FC):
                        ta = b.pe(lambda e, oc=oc, fc=fc, p=p, bA=bA: e.matmul(
                            bk(bA, 0, TT),
                            wp[p][:, fc * 512 + (oc % 4) * 128:
                                  fc * 512 + (oc % 4) * 128 + 128],
                            x_bf[:, fc * TT:(fc + 1) * TT],
                            start=(fc == 0), stop=(fc == FC - 1)),
                            inc=(fc == FC - 1))
                    tb = None
                    for fc in range(FC):
                        tb = b.pe(lambda e, oc=oc, fc=fc, p=p, bB=bB: e.matmul(
                            bk(bB, 0, TT),
                            wp[p][:, 4096 + fc * 512 + (oc % 4) * 128:
                                  4096 + fc * 512 + (oc % 4) * 128 + 128],
                            x_bf[:, fc * TT:(fc + 1) * TT],
                            start=(fc == 0), stop=(fc == FC - 1)),
                            inc=(fc == FC - 1))
                    b.wt("act", ta)
                    sa = b.act(lambda e, par=par, bA=bA: e.activation(
                        silu_t[par], bk(bA, 0, TT), AF.Silu))
                    b.wt("dve", tb)
                    b.wt("dve", sa)
                    if gate_e is not None:
                        b.dve(lambda e, par=par, bB=bB: e.tensor_mul(
                            t2, silu_t[par], bk(bB, 0, TT)))
                        hv = b.dve(lambda e, hslot=hslot, gate_e=gate_e:
                                   e.tensor_mul(
                            h_bf[:, hslot * TT:(hslot + 1) * TT], t2,
                            g_bc[:, gate_e * TT:(gate_e + 1) * TT]))
                    else:
                        hv = b.dve(lambda e, hslot=hslot, par=par, bB=bB:
                                   e.tensor_mul(
                            h_bf[:, hslot * TT:(hslot + 1) * TT],
                            silu_t[par], bk(bB, 0, TT)))
                    evh[par] = [sa, hv]

                def y_block(p, oc, wbase, n_ic):
                    par = oc % 2
                    b.wt("pe", evy[par])
                    ty = None
                    for ic in range(n_ic):
                        ty = b.pe(lambda e, oc=oc, ic=ic, p=p, par=par,
                                  wbase=wbase, n_ic=n_ic: e.matmul(
                            bk(4 + par, 0, TT),
                            wp[p][:, wbase + ic * 512 + (oc % 4) * 128:
                                  wbase + ic * 512 + (oc % 4) * 128 + 128],
                            h_bf[:, ic * TT:(ic + 1) * TT],
                            start=(ic == 0), stop=(ic == n_ic - 1)),
                            inc=(ic == n_ic - 1))
                    b.wt("dve", ty)
                    b.wt("dve", ym)
                    ya = b.dve(lambda e, oc=oc, par=par: e.tensor_add(
                        y_sb[:, oc * TT:(oc + 1) * TT],
                        y_sb[:, oc * TT:(oc + 1) * TT], bk(4 + par, 0, TT)))
                    evy[par] = ya

                for e_ in range(8):
                    for half in range(2):
                        p = pconsume(gi)
                        for ol in range(4):
                            silu_block(p, half * 4 + ol, half * 4 + ol, e_)
                        pdone(gi)
                        gi += 1
                    p = pconsume(gi)
                    b.wait("pe", "dve", b.c["dve"])
                    for oc in range(8):
                        y_block(p, oc, (oc // 4) * 4096, FC)
                    pdone(gi)
                    gi += 1
                for og in range(4):
                    p = pconsume(gi)
                    for ol in range(4):
                        silu_block(p, og * 4 + ol, og * 4 + ol, None)
                    pdone(gi)
                    gi += 1
                b.wait("pe", "dve", b.c["dve"])
                for og in range(2):
                    p = pconsume(gi)
                    for ol in range(4):
                        y_block(p, og * 4 + ol, 0, 16)
                    pdone(gi)
                    gi += 1
                zt = None
                for fc in range(FC):
                    zt = b.dve(lambda e, fc=fc, t0=t0: e.tensor_add(
                        ST(fc, t0, TT), ST(fc, t0, TT),
                        y_sb[:, fc * TT:(fc + 1) * TT]))
                for tc in range(2):
                    tt0 = t0 + tc * 128
                    bb, pos = divmod(tt0, SEQ)
                    b.wait("pe", "dve", b.c["dve"])
                    b.wt("pe", prev_store)
                    for half in range(2):
                        tk = None
                        for j in range(4):
                            fc = half * 4 + j
                            tk = b.pe(lambda e, fc=fc, tt0=tt0, half=half, j=j:
                                      e.transpose(
                                bk(half, j * 128, (j + 1) * 128),
                                ST(fc, tt0, 128), ident_f), inc=(j == 3))
                        b.wt("dve", tk)
                        ev = b.dve(lambda e, half=half: e.tensor_copy(
                            z_tm.rearrange("p (f t) -> p f t", f=FC)
                            [:, half * 4:(half + 1) * 4, :],
                            bk(half).rearrange("p (f t) -> p f t", f=4)))
                        b.wt("pe", ev)
                    b.dve(lambda e: e.reduce_sum(c8[:, 0:1], z_tm, axis=AX.X))
                    b.dve(lambda e: e.tensor_scalar_mul(c8[:, 0:1], c8[:, 0:1],
                                                        1.0 / D))
                    b.dve(lambda e: e.tensor_scalar_sub(z_tm, z_tm,
                                                        c8[:, 0:1]))
                    b.wt("dve", prev_store)
                    b.dve(lambda e: e.tensor_mul(obuf, z_tm, z_tm))
                    b.dve(lambda e: e.reduce_sum(c8[:, 1:2], obuf, axis=AX.X))
                    vr = b.dve(lambda e: e.tensor_scalar_mul(
                        c8[:, 1:2], c8[:, 1:2], 1.0 / D))
                    b.wt("act", vr)
                    b.act(lambda e: e.activation(c8[:, 1:2], c8[:, 1:2],
                                                 AF.Ln, bias=1e-5))
                    rs = b.act(lambda e: e.activation(c8[:, 1:2], c8[:, 1:2],
                                                      AF.Exp, scale=-0.5))
                    b.wt("dve", rs)
                    b.wt("dve", prev_store)
                    ob = b.dve(lambda e: e.tensor_scalar_mul(obuf, z_tm,
                                                             c8[:, 1:2]))
                    b.wait("pool", "dve", b.c["dve"])
                    prev_store = b.dma("pool", lambda e, bb=bb, pos=pos:
                                       e.dma_start(out=out[bb, pos:pos + 128, :],
                                                   in_=obuf))
            A.pop()

        def _ph_store():
            A.push()
            obuf = A.f32(1024)
            prev_store = None
            for ch in range(NTOK // 128):
                bb, pos = divmod(ch * 128, SEQ)
                b.wt("pe", prev_store)
                for half in range(2):
                    tk = None
                    for j in range(4):
                        fc = half * 4 + j
                        tk = b.pe(lambda e, fc=fc, ch=ch, half=half, j=j:
                                  e.transpose(
                            bk(half, j * 128, (j + 1) * 128),
                            ST(fc, ch * 128, 128), ident_f), inc=(j == 3))
                    b.wt("dve", tk)
                    b.wt("dve", prev_store)
                    ev = b.dve(lambda e, half=half: e.tensor_copy(
                        obuf.rearrange("p (f t) -> p f t", f=FC)
                        [:, half * 4:(half + 1) * 4, :],
                        bk(half).rearrange("p (f t) -> p f t", f=4)))
                    b.wt("pe", ev)
                b.wait("pool", "dve", b.c["dve"])
                prev_store = b.dma("pool", lambda e, bb=bb, pos=pos:
                                   e.dma_start(out=out[bb, pos:pos + 128, :],
                                               in_=obuf))
            A.pop()

        if "l0a" in phases:
            _ph_l0a()
        if "l0f" in phases:
            _ph_l0f()
        if "l1a" in phases:
            _ph_l1a()
        if "l1f" in phases:
            _ph_l1f()
        else:
            _ph_store()

        final_qp = b.c["qp"]
        b.raw("pool", lambda e: e.wait_ge(sems["qp"], final_qp))
        b.finalize()
    return nc, A.peak


def _host_consts():
    msk = np.full((128, 384), -1e9, np.float32)
    qi = np.arange(128)[:, None]
    j = np.arange(384)[None, :]
    msk[(j >= qi) & (j <= qi + 256)] = 0.0

    S, W = SEQ, 8
    pos = np.arange(S)[:, None]
    k = np.arange(W)[None, :]
    idx = pos - W + k
    valid = (idx >= 0) & (pos >= 1)
    w = np.minimum(pos, W)
    jj = k - (W - w)
    lin = np.where(w > 1, 0.1 + 0.9 * jj / np.maximum(w - 1, 1), 0.1)
    pw = np.where(valid, lin, 0.0).astype(np.float32)

    pwc = np.zeros((128, 3, 128), np.float32)
    i_ = np.arange(128)[:, None]
    j_ = np.arange(128)[None, :]
    kk = i_ - j_ + 8
    m = (kk >= 0) & (kk <= 7)
    kkc = np.clip(kk, 0, 7)
    jb = np.broadcast_to(j_, (128, 128))
    pwc[:, 0] = np.where(m, pw[jb, kkc], 0.0)
    pwc[:, 1] = np.where(m, pw[128 + jb, kkc], 0.0)
    kp = i_ - j_ - 120
    mp = (kp >= 0) & (kp <= 7)
    kpc = np.clip(kp, 0, 7)
    pwc[:, 2] = np.where(mp, pw[128 + jb, kpc], 0.0)

    oneh = np.zeros((8, 8, 128), np.float32)
    for e in range(8):
        oneh[e, e, :] = 1.0
    return msk, pwc.reshape(128, -1), oneh.reshape(8, -1)


KPHASES = ("l0a", "l0f", "l1a", "l1f")
NCORES = 8
_CACHE = {}


def kernel(**inputs):
    key = tuple(KPHASES)
    if key not in _CACHE:
        _CACHE[key] = build_program(KPHASES)[0]
    nc = _CACHE[key]
    msk, pwc, oneh = _host_consts()
    in_maps = []
    for c in range(NCORES):
        m = {"x": np.ascontiguousarray(inputs["L0_attn_x"][2 * c:2 * c + 2])}
        for k in ("L0_attn_w_in", "L0_attn_b_in", "L0_attn_w_out",
                  "L0_attn_b_out", "L0_ffn_w1_w", "L0_ffn_w1_s",
                  "L0_ffn_w2_w", "L0_ffn_w2_s", "L0_ffn_h1_w", "L0_ffn_h1_s",
                  "L0_ffn_h2_w", "L0_ffn_h2_s", "L0_ffn_w3_w", "L0_ffn_w3_s",
                  "L1_ffn_gate_w", "L1_ffn_gate_bias",
                  "L1_ffn_sw1", "L1_ffn_sw3", "L1_ffn_sw2"):
            m[k] = np.ascontiguousarray(inputs[k])
        m["L1_attn_g"] = np.ascontiguousarray(
            inputs["L1_attn_guidance_trajectories"])
        m["L1_ffn_w1"] = np.ascontiguousarray(
            inputs["L1_ffn_w1"].reshape(8192, 1024))
        m["L1_ffn_w3"] = np.ascontiguousarray(
            inputs["L1_ffn_w3"].reshape(8192, 1024))
        m["L1_ffn_w2"] = np.ascontiguousarray(
            inputs["L1_ffn_w2"].reshape(8192, 1024))
        m["msk"] = msk
        m["pwc"] = pwc
        m["onehot"] = oneh
        in_maps.append(m)
    res = run_bass_kernel_spmd(nc, in_maps, core_ids=list(range(NCORES)))
    return np.concatenate([np.asarray(r["out"], np.float32)[None]
                           for r in res.results], axis=0).reshape(
                               2 * NCORES, SEQ, D)


# version 04 (final)
# speedup vs baseline: 1.9133x; 1.1197x over previous
import numpy as np
import ml_dtypes
from contextlib import ExitStack

import concourse.bass as bass
import concourse.mybir as mybir
from concourse.bass_utils import run_bass_kernel_spmd

F32 = mybir.dt.float32
BF16 = mybir.dt.bfloat16
AF = mybir.ActivationFunctionType
ALU = mybir.AluOpType
AX = mybir.AxisListType

D = 1024
FC = 8
SEQ = 2048
NSEQ = 2
NTOK = SEQ * NSEQ
TT = 256
NTILE = NTOK // TT
TPS = SEQ // TT
NHEAD = 16
HD = 64
QS = 0.125

ENGS = ("pe", "dve", "act", "pool", "sp")
CHAINS = ("pe", "dve", "act", "pool", "qs", "qp", "qa", "qb")

WTS = {
    "w_in": ("L0_attn_w_in", 3072, 1024, 512),
    "w_out": ("L0_attn_w_out", 1024, 1024, 512),
    "w1": ("L0_ffn_w1_w", 4096, 1024, 512),
    "w2": ("L0_ffn_w2_w", 4096, 1024, 512),
    "h1": ("L0_ffn_h1_w", 4096, 4096, 256),
    "h2": ("L0_ffn_h2_w", 4096, 4096, 256),
    "w3": ("L0_ffn_w3_w", 1024, 4096, 256),
    "ew1": ("L1_ffn_w1", 8192, 1024, 512),
    "ew3": ("L1_ffn_w3", 8192, 1024, 512),
    "ew2": ("L1_ffn_w2", 8192, 1024, 512),
    "sw1": ("L1_ffn_sw1", 2048, 1024, 512),
    "sw3": ("L1_ffn_sw3", 2048, 1024, 512),
    "sw2": ("L1_ffn_sw2", 1024, 2048, 512),
}

COLVECS = {
    "b_in": ("L0_attn_b_in", 3072),
    "b_out": ("L0_attn_b_out", 1024),
    "w1s": ("L0_ffn_w1_s", 4096),
    "w2s": ("L0_ffn_w2_s", 4096),
    "h1s": ("L0_ffn_h1_s", 4096),
    "h2s": ("L0_ffn_h2_s", 4096),
    "w3s": ("L0_ffn_w3_s", 1024),
}


class B:

    def __init__(self, nc, block, sems):
        self.nc = nc
        self.block = block
        self.h = sems
        self.c = {k: 0 for k in CHAINS}
        self.ops = {e: [] for e in ENGS}
        self.pend = {e: [] for e in ENGS}

    def wait(self, eng, sem, val):
        if val > 0:
            self.pend[eng].append((sem, val))

    def wt(self, eng, tok):
        if tok is None:
            return
        if isinstance(tok, (list, tuple)) and tok and \
                not isinstance(tok[0], str):
            for t in tok:
                self.wt(eng, t)
            return
        self.wait(eng, tok[0], tok[1])

    def _emit(self, eng, fn, inc=None, by=1):
        waits = self.pend[eng]
        self.pend[eng] = []
        tok = None
        if inc is not None:
            self.c[inc] += by
            tok = (inc, self.c[inc])
        self.ops[eng].append((waits, fn, inc, by))
        return tok

    def pe(self, fn, inc=False):
        return self._emit("pe", fn, "pe" if inc else None, 1)

    def dve(self, fn):
        self.wait("dve", "dve", self.c["dve"])
        return self._emit("dve", fn, "dve", 1)

    def act(self, fn):
        self.wait("act", "act", self.c["act"])
        return self._emit("act", fn, "act", 1)

    def pool(self, fn):
        self.wait("pool", "pool", self.c["pool"])
        return self._emit("pool", fn, "pool", 1)

    def dma(self, eng, fn, sem=None):
        if sem is None:
            sem = {"sp": "qs", "pool": "qp", "act": "qa"}[eng]
        return self._emit(eng, fn, sem, 16)

    def dma_on(self, eng, sem, fn):
        return self._emit(eng, fn, sem, 16)

    def raw(self, eng, fn):
        return self._emit(eng, fn, None)

    def tok(self, sem):
        return (sem, self.c[sem])

    def barrier(self, engines=ENGS, sems=CHAINS):
        snap = [(s, self.c[s]) for s in sems if self.c[s] > 0]
        for e in engines:
            for s, v in snap:
                self.wait(e, s, v)

    def n_ops(self):
        return sum(len(v) for v in self.ops.values())

    def finalize(self):
        h = self.h

        def mk(eng):
            def body(e):
                for waits, fn, inc, by in self.ops[eng]:
                    for s, v in waits:
                        e.wait_ge(h[s], v)
                    inst = fn(e)
                    if inc is not None:
                        inst.then_inc(h[inc], by)
            return body
        self.block.tensor(mk("pe"))
        self.block.vector(mk("dve"))
        self.block.scalar(mk("act"))
        self.block.gpsimd(mk("pool"))
        self.block.sync(mk("sp"))


class Arena:
    def __init__(self, f32_ap, bf16_ap, nbytes):
        self.vf = f32_ap
        self.vb = bf16_ap
        self.n = nbytes
        self.off = 0
        self.stack = []
        self.peak = 0

    def push(self):
        self.stack.append(self.off)

    def pop(self):
        self.off = self.stack.pop()

    def _take(self, nbytes):
        off = (self.off + 31) & ~31
        self.off = off + nbytes
        self.peak = max(self.peak, self.off)
        assert self.off <= self.n, f"SBUF arena overflow: {self.off} > {self.n}"
        return off

    def f32(self, cols):
        off = self._take(cols * 4)
        return self.vf[:, off // 4: off // 4 + cols]

    def bf16(self, cols):
        off = self._take(cols * 2)
        return self.vb[:, off // 2: off // 2 + cols]


def build_program(phases=("l0a", "l0f", "l1a", "l1f")):
    nc = bass.Bass("TRN2", target_bir_lowering=False)

    ishapes = {
        "x": [NSEQ, SEQ, D],
        "L0_attn_w_in": [3072, 1024], "L0_attn_b_in": [3072],
        "L0_attn_w_out": [1024, 1024], "L0_attn_b_out": [1024],
        "L0_ffn_w1_w": [4096, 1024], "L0_ffn_w1_s": [4096],
        "L0_ffn_w2_w": [4096, 1024], "L0_ffn_w2_s": [4096],
        "L0_ffn_h1_w": [4096, 4096], "L0_ffn_h1_s": [4096],
        "L0_ffn_h2_w": [4096, 4096], "L0_ffn_h2_s": [4096],
        "L0_ffn_w3_w": [1024, 4096], "L0_ffn_w3_s": [1024],
        "L1_attn_g": [512, 1024],
        "L1_ffn_gate_w": [8, 1024], "L1_ffn_gate_bias": [8],
        "L1_ffn_w1": [8192, 1024], "L1_ffn_w3": [8192, 1024],
        "L1_ffn_w2": [8192, 1024],
        "L1_ffn_sw1": [2048, 1024], "L1_ffn_sw3": [2048, 1024],
        "L1_ffn_sw2": [1024, 2048],
        "msk": [128, 384],
        "pwc": [128, 3 * 128],
        "onehot": [8, 8 * 128],
    }
    inp = {k: nc.dram_tensor(k, s, F32, kind="ExternalInput").ap()
           for k, s in ishapes.items()}
    out = nc.dram_tensor("out", [NSEQ, SEQ, D], F32, kind="ExternalOutput").ap()

    cast_bf = {w: nc.dram_tensor("cb_" + w, [v[1], v[2]], BF16,
                                 kind="Internal").ap()
               for w, v in WTS.items()}
    cast_bf["g"] = nc.dram_tensor("cb_g", [512, 1024], BF16, kind="Internal").ap()
    wT = {w: nc.dram_tensor("wT_" + w, [v[1] // v[3], 128, v[2] // 128, v[3]],
                            BF16, kind="Internal").ap()
          for w, v in WTS.items()}

    es = ExitStack()
    ARENA_F32 = 53100
    arena_t = es.enter_context(nc.sbuf_tensor("arena", [128, ARENA_F32], F32))
    A = Arena(arena_t.ap(), arena_t.bitcast(BF16).ap(), ARENA_F32 * 4)
    banks = [es.enter_context(nc.psum_tensor(f"pbank{i}", [128, 512], F32))
             for i in range(8)]

    def bk(i, lo=0, hi=512):
        return banks[i][:, lo:hi]

    def bkb(i, lo=0, hi=1024):
        return banks[i].bitcast(BF16)[:, lo:hi]

    sems = {s: es.enter_context(nc.semaphore("sem_" + s)) for s in CHAINS}
    block = es.enter_context(nc.Block())
    b = B(nc, block, sems)

    with es:
        ident_f = A.f32(128)
        ident_b = A.bf16(128)
        ones_cf = A.f32(1)
        ones_cb = A.bf16(16)
        ones_row = A.f32(128)
        ccols = A.f32(8)
        cols = {"b_in": A.f32(24), "b_inq": A.f32(8), "b_out": A.f32(8),
                "w1s": A.f32(32), "w2s": A.f32(32), "h1s": A.f32(32),
                "h2s": A.f32(32), "w3s": A.f32(8)}
        stream = A.bf16(FC * NTOK)
        zt32 = A.f32(FC * 512)
        rows_ln = A.f32(1024)
        sq_bf = A.bf16(8 * 512)
        ltmp_ln = A.f32(512)
        st3 = stream.rearrange("p (fc t) -> p fc t", fc=FC)

        def ST(fc, t0, n):
            return st3[:, fc, t0:t0 + n]

        nc.const_aps.register(F32, 1e-5, ccols[:, 5:6])

        b.pool(lambda e: e.memset(ident_f, 0.0))
        b.pool(lambda e: e.affine_select(
            out=ident_f, in_=ident_f, compare_op=ALU.not_equal, fill=1.0,
            base=0, pattern=[[-1, 128]], channel_multiplier=1))
        b.pool(lambda e: e.memset(ident_b, 0.0))
        b.pool(lambda e: e.affine_select(
            out=ident_b, in_=ident_b, compare_op=ALU.not_equal, fill=1.0,
            base=0, pattern=[[-1, 128]], channel_multiplier=1))
        b.pool(lambda e: e.memset(ones_cf, 1.0))
        b.pool(lambda e: e.memset(ones_cb[:, 0:1], 1.0))
        b.pool(lambda e: e.memset(ones_row, 0.0))
        b.pool(lambda e: e.memset(ones_row[0:1, :], 1.0))
        for i, v in enumerate([QS, 0.5, 4.0, 0.4, -0.5, 1e-5, 1.0, 0.0]):
            b.pool(lambda e, i=i, v=v: e.memset(ccols[:, i:i + 1], v))
        t_consts = b.tok("pool")

        cast_toks = {}
        for w, (src, O, I, OW) in WTS.items():
            cast_toks[w] = b.dma(
                "pool", lambda e, w=w, src=src:
                e.dma_start(out=cast_bf[w], in_=inp[src]))
        cast_toks["g"] = b.dma(
            "pool", lambda e: e.dma_start(out=cast_bf["g"],
                                          in_=inp["L1_attn_g"]))

        A.push()
        tstage = A.bf16(32 * 256)
        wT_tok = {}
        for w, (src, O, I, OW) in WTS.items():
            IC = I // 128
            b.wt("sp", cast_toks[w])
            for og in range(O // OW):
                last = None
                for ic in range(IC):
                    last = b.dma("sp", lambda e, w=w, og=og, ic=ic, OW=OW:
                                 e.dma_start(
                                     out=tstage[:, ic * OW:(ic + 1) * OW],
                                     in_=cast_bf[w][og * OW:(og + 1) * OW,
                                                    ic * 128:(ic + 1) * 128],
                                     transpose=True))
                b.wt("sp", last)
                st_t = b.dma("sp", lambda e, w=w, og=og, IC=IC, OW=OW:
                             e.dma_start(
                                 out=wT[w][og].rearrange("p ic ow -> p (ic ow)"),
                                 in_=tstage[:, 0:IC * OW]))
                b.wt("sp", st_t)
            wT_tok[w] = b.tok("qs")
        A.pop()

        def load_wt(dst, w, og, sem=None, eng="sp"):
            _, O, I, OW = WTS[w]
            IC = I // 128
            b.wt(eng, wT_tok[w])
            return b.dma(eng, lambda e, w=w, og=og, IC=IC, OW=OW, dst=dst:
                         e.dma_start(
                             out=dst[:, 0:IC * OW],
                             in_=wT[w][og].rearrange("p ic ow -> p (ic ow)")),
                         sem=sem)

        A.push()
        vstage = A.f32(128)
        for name, (src, n) in COLVECS.items():
            nch = n // 128
            ld = b.dma("sp", lambda e, src=src, nch=nch:
                       e.dma_start(out=vstage[0:nch, :],
                                   in_=inp[src].rearrange("(c p) -> c p", p=128)))
            b.wt("pe", ld)
            b.wt("pe", t_consts)
            tk = b.pe(lambda e, nch=nch: e.transpose(
                bk(0, 0, nch), vstage[0:nch, :], ident_f[0:nch, 0:nch]),
                inc=True)
            b.wt("dve", tk)
            cp = b.dve(lambda e, name=name, nch=nch:
                       e.tensor_copy(cols[name][:, 0:nch], bk(0, 0, nch)))
            b.wt("sp", tk)
            b.wt("pe", cp)
        b.dve(lambda e: e.tensor_scalar_mul(cols["b_inq"], cols["b_in"][:, 0:8],
                                            QS))
        A.pop()

        A.push()
        xst = [A.f32(1024), A.f32(1024)]
        last_rd = [None, None]
        NCH = NTOK // 128
        for ch in range(NCH):
            bb, t0 = divmod(ch * 128, SEQ)
            buf = ch % 2
            b.wt("sp", last_rd[buf])
            ld = b.dma("sp", lambda e, bb=bb, t0=t0, buf=buf:
                       e.dma_start(out=xst[buf], in_=inp["x"][bb, t0:t0 + 128, :]))
            b.wt("pe", ld)
            b.wt("pe", t_consts)
            for half in range(2):
                bkid = half
                tk = None
                for j in range(4):
                    fc = half * 4 + j
                    tk = b.pe(lambda e, buf=buf, fc=fc, bkid=bkid, j=j:
                              e.transpose(bk(bkid, j * 128, (j + 1) * 128),
                                          xst[buf][:, fc * 128:(fc + 1) * 128],
                                          ident_f), inc=(j == 3))
                b.wt("dve", tk)
                ev = b.dve(lambda e, ch=ch, half=half, bkid=bkid: e.tensor_copy(
                    st3[:, half * 4:(half + 1) * 4, ch * 128:ch * 128 + 128],
                    bk(bkid).rearrange("p (f t) -> p f t", f=4)))
                b.wt("pe", ev)
            last_rd[buf] = b.tok("pe")
        A.pop()
        b.barrier()

        def cast_tile(dst_bf, t0, n=TT):
            src = st3[:, :, t0:t0 + n]
            d3 = dst_bf.rearrange("p (fc t) -> p fc t", fc=FC)[:, :, 0:n]
            return b.dve(lambda e: e.tensor_copy(d3, src))

        def tile_ln(T, t0, bkM=6, bkS=7):
            def Z(fc):
                return zt32[:, fc * T:(fc + 1) * T]
            sq = None
            for fc in range(FC):
                sq = b.dve(lambda e, fc=fc, T=T: e.tensor_mul(
                    sq_bf[:, fc * T:(fc + 1) * T], Z(fc), Z(fc)))
            for fc in range(FC):
                b.pe(lambda e, fc=fc: e.matmul(
                    bk(bkM, 0, T)[0:1, :], ones_cf, Z(fc),
                    start=(fc == 0), stop=(fc == FC - 1)))
            b.wt("pe", sq)
            mt = None
            for fc in range(FC):
                mt = b.pe(lambda e, fc=fc, T=T: e.matmul(
                    bk(bkS, 0, T)[0:1, :], ones_cb[:, 0:1],
                    sq_bf[:, fc * T:(fc + 1) * T],
                    start=(fc == 0), stop=(fc == FC - 1)), inc=True)
            b.wt("dve", mt)
            b.dve(lambda e: e.tensor_scalar_mul(
                rows_ln[0:1, 0:T], bk(bkM, 0, T)[0:1, :], 1.0 / D))
            b.dve(lambda e: e.tensor_mul(
                rows_ln[0:1, 512:512 + T], rows_ln[0:1, 0:T],
                rows_ln[0:1, 0:T]))
            vv = b.dve(lambda e: e.scalar_tensor_tensor(
                rows_ln[0:1, 512:512 + T], bk(bkS, 0, T)[0:1, :], 1.0 / D,
                rows_ln[0:1, 512:512 + T], op0=ALU.mult, op1=ALU.subtract))
            b.wt("act", vv)
            b.act(lambda e: e.activation(rows_ln[0:1, 512:512 + T],
                                         rows_ln[0:1, 512:512 + T],
                                         AF.Ln, bias=1e-5))
            rs = b.act(lambda e: e.activation(rows_ln[0:1, 512:512 + T],
                                              rows_ln[0:1, 512:512 + T],
                                              AF.Exp, scale=-0.5))
            b.wait("pe", "dve", b.c["dve"])
            b.pe(lambda e: e.matmul(bk(bkM, 0, T), ones_row[0:1, :],
                                    rows_ln[0:1, 0:T], start=True, stop=True))
            b.wt("pe", rs)
            bcs = b.pe(lambda e: e.matmul(bk(bkS, 0, T), ones_row[0:1, :],
                                          rows_ln[0:1, 512:512 + T],
                                          start=True, stop=True), inc=True)
            b.wt("dve", bcs)
            lt = None
            for fc in range(FC):
                b.dve(lambda e, fc=fc: e.tensor_sub(
                    ltmp_ln[:, 0:T], Z(fc), bk(bkM, 0, T)))
                lt = b.dve(lambda e, fc=fc, t0=t0, T=T: e.tensor_mul(
                    ST(fc, t0, T), ltmp_ln[:, 0:T], bk(bkS, 0, T)))
            return lt

        def ln_sweep():
            A.push()
            sqt = A.bf16(TT)
            ltmp = A.f32(TT)
            rows = A.f32(512)
            mt = None
            for ti in range(NTILE):
                t0 = ti * TT
                for fc in range(FC):
                    if mt is not None:
                        b.wt("dve", mt)
                    sq = b.dve(lambda e, fc=fc, t0=t0: e.tensor_mul(
                        sqt, ST(fc, t0, TT), ST(fc, t0, TT)))
                    b.wt("pe", sq)
                    b.pe(lambda e, fc=fc, t0=t0: e.matmul(
                        bk(0, 0, TT)[0:1, :], ones_cb[:, 0:1], ST(fc, t0, TT),
                        start=(fc == 0), stop=(fc == FC - 1)))
                    mt = b.pe(lambda e, fc=fc: e.matmul(
                        bk(1, 0, TT)[0:1, :], ones_cb[:, 0:1], sqt,
                        start=(fc == 0), stop=(fc == FC - 1)), inc=True)
                b.wt("dve", mt)
                b.dve(lambda e: e.tensor_scalar_mul(
                    rows[0:1, 0:TT], bk(0, 0, TT)[0:1, :], 1.0 / D))
                b.dve(lambda e: e.tensor_mul(
                    rows[0:1, 256:512], rows[0:1, 0:TT], rows[0:1, 0:TT]))
                vv = b.dve(lambda e: e.scalar_tensor_tensor(
                    rows[0:1, 256:512], bk(1, 0, TT)[0:1, :], 1.0 / D,
                    rows[0:1, 256:512], op0=ALU.mult, op1=ALU.subtract))
                b.wt("act", vv)
                b.act(lambda e: e.activation(rows[0:1, 256:512],
                                             rows[0:1, 256:512], AF.Ln,
                                             bias=1e-5))
                l2 = b.act(lambda e: e.activation(rows[0:1, 256:512],
                                                  rows[0:1, 256:512], AF.Exp,
                                                  scale=-0.5))
                b.wt("pe", l2)
                bc = b.pe(lambda e: e.matmul(bk(2), ones_row[0:1, :],
                                             rows[0:1, 0:512],
                                             start=True, stop=True), inc=True)
                b.wt("dve", bc)
                lt = None
                for fc in range(FC):
                    b.dve(lambda e, fc=fc, t0=t0: e.tensor_sub(
                        ltmp, ST(fc, t0, TT), bk(2, 0, TT)))
                    lt = b.dve(lambda e, fc=fc, t0=t0: e.tensor_mul(
                        ST(fc, t0, TT), ltmp, bk(2, 256, 512)))
                b.wt("pe", lt)
            A.pop()
            b.barrier()

        def _ph_l0a():
            A.push()
            msk_b = A.bf16(384)
            w_inT = [A.bf16(8 * 512), A.bf16(8 * 512)]
            k_ring = A.bf16(3 * FC * TT)
            v_ring = A.bf16(3 * 2 * NHEAD * HD)
            q_ring = A.bf16(2 * FC * TT)
            ctx_bf = A.bf16(FC * TT)
            s_sb = [A.f32(384), A.f32(384)]
            p_bf = [A.bf16(384), A.bf16(384)]
            pT_bf = [A.bf16(384), A.bf16(384)]
            vstg = A.bf16(TT)
            sumc = A.f32(2)
            rinv = A.f32(2)

            b.dma("pool", lambda e: e.dma_start(out=msk_b[:, :], in_=inp["msk"]))
            b.pool(lambda e: e.memset(k_ring, 0.0))
            b.pool(lambda e: e.memset(v_ring, 0.0))
            b.pool(lambda e: e.memset(q_ring, 0.0))
            b.barrier()

            def qsl(sl, fc, lo, n):
                base = sl * 2048 + fc * TT + lo
                return q_ring[:, base:base + n]

            def ksl(sl, fc, lo, n):
                base = sl * 2048 + fc * TT + lo
                return k_ring[:, base:base + n]

            def vsl(sl, th, h):
                base = sl * 2048 + th * 1024 + h * HD
                return v_ring[:, base:base + HD]

            for s in range(NSEQ):
                def qkv(i, s=s):
                    sl = i % 3
                    qid = i % 2
                    t0 = s * SEQ + i * TT
                    ct = None
                    ld = [None, None]
                    qc0 = b.tok("pe")
                    cons = [qc0, qc0]
                    q0 = [b.tok("act"), b.tok("dve")]
                    ev = [q0, q0, q0, q0]

                    def loadQ(og):
                        p = og % 2
                        b.wt("act", cons[p])
                        ld[p] = load_wt(w_inT[p], "w_in", og,
                                        "qa" if p == 0 else "qb", eng="act")
                    loadQ(0)
                    loadQ(1)
                    for oc in range(24):
                        og, ol = divmod(oc, 4)
                        p = og % 2
                        par = oc % 2
                        b.wt("pe", ld[p])
                        b.wt("pe", ct)
                        b.wt("pe", ev[par])
                        mmt = None
                        for fc in range(FC):
                            mmt = b.pe(lambda e, ol=ol, fc=fc, p=p, par=par,
                                       t0=t0: e.matmul(
                                bk(par, 0, TT),
                                w_inT[p][:, fc * 512 + ol * 128:
                                         fc * 512 + ol * 128 + 128],
                                ST(fc, t0, TT),
                                start=(fc == 0), stop=(fc == FC - 1)),
                                inc=(fc == FC - 1))
                        b.wt("act", mmt)
                        b.wait("act", "pe", b.c["pe"])
                        if oc < 8:
                            ev[par] = b.act(lambda e, oc=oc, qid=qid, par=par:
                                            e.activation(
                                qsl(qid, oc, 0, TT), bk(par, 0, TT),
                                AF.Identity,
                                bias=cols["b_inq"][:, oc:oc + 1], scale=QS))
                        elif oc < 16:
                            ev[par] = b.act(lambda e, oc=oc, sl=sl, par=par:
                                            e.activation(
                                ksl(sl, oc - 8, 0, TT), bk(par, 0, TT),
                                AF.Identity, bias=cols["b_in"][:, oc:oc + 1]))
                        else:
                            c = oc - 16
                            vpar = c % 2
                            evv = b.act(lambda e, oc=oc, par=par:
                                        e.activation(
                                vstg, bk(par, 0, TT), AF.Identity,
                                bias=cols["b_in"][:, oc:oc + 1]))
                            ev[par] = evv
                            for th in range(2):
                                b.wt("pe", evv)
                                b.wt("pe", ev[2 + vpar])
                                tr = b.pe(lambda e, th=th, vpar=vpar:
                                          e.transpose(
                                    bkb(2 + vpar, th * 128, (th + 1) * 128),
                                    vstg[:, th * 128:(th + 1) * 128],
                                    ident_b), inc=True)
                                b.wt("dve", tr)
                                b.wait("dve", "pe", b.c["pe"])
                                ev[2 + vpar] = b.dve(
                                    lambda e, th=th, sl=sl, c=c, vpar=vpar:
                                    e.tensor_copy(
                                        v_ring[:, sl * 2048 + th * 1024
                                               + c * 128:
                                               sl * 2048 + th * 1024
                                               + c * 128 + 128],
                                        bkb(2 + vpar, th * 128,
                                            (th + 1) * 128)))
                        if ol == 3 and og + 2 < 6:
                            cons[p] = b.tok("pe")
                            loadQ(og + 2)

                def attn(i, s=s):
                    slp, slc, sln = (i - 1) % 3, i % 3, (i + 1) % 3
                    qid = i % 2
                    a0 = [b.tok("act"), b.tok("dve")]
                    tok_ma = [a0, a0]
                    tok_ex = [None, None]
                    tok_pb = [a0, a0]
                    tok_pt = [a0, a0]
                    tok_cev = [a0, a0]
                    tok_trs = [b.tok("pe"), b.tok("pe")]
                    tok_cm = [b.tok("pe"), b.tok("pe")]
                    b.wait("pe", "act", b.c["act"])
                    b.wait("pe", "dve", b.c["dve"])
                    for qb in range(2):
                        qg = i * 2 + qb
                        mv = 0 if qg == 0 else (2 if qg == 2 * TPS - 1 else 1)
                        for h in range(NHEAD):
                            hp = (qb * NHEAD + h) % 2
                            bS, bT, bC = (0, 1, 2) if hp == 0 else (3, 4, 5)
                            fcq = h // 2
                            r0 = 64 * (h % 2)
                            if qb == 0:
                                kp = [(slp, 128, 128), (slc, 0, 256)]
                                vp = [(slp, 1), (slc, 0), (slc, 1)]
                            else:
                                kp = [(slc, 0, 128), (slc, 128, 256 - 128),
                                      (sln, 0, 128)]
                                vp = [(slc, 0), (slc, 1), (sln, 0)]
                            qAP = q_ring[r0:r0 + 64,
                                         qid * 2048 + fcq * TT + qb * 128:
                                         qid * 2048 + fcq * TT + qb * 128 + 128]
                            b.wt("pe", tok_ma[hp])
                            co = 0
                            mmt = None
                            for (ksl_, klo, kn) in kp:
                                mmt = b.pe(lambda e, ksl_=ksl_, klo=klo, kn=kn,
                                           co=co, qAP=qAP, r0=r0, fcq=fcq,
                                           bS=bS: e.matmul(
                                    bk(bS, co, co + kn), qAP,
                                    k_ring[r0:r0 + 64,
                                           ksl_ * 2048 + fcq * TT + klo:
                                           ksl_ * 2048 + fcq * TT + klo + kn],
                                    start=True, stop=True),
                                    inc=(co + kn == 384))
                                co += kn
                            b.wt("dve", mmt)
                            if mv == 0:
                                b.dve(lambda e, hp=hp: e.memset(
                                    s_sb[hp][:, 0:128], -1e9))
                                ma = b.dve(lambda e, hp=hp, bS=bS: e.tensor_add(
                                    s_sb[hp][:, 128:384], bk(bS, 128, 384),
                                    msk_b[:, 128:384]))
                            elif mv == 2:
                                b.dve(lambda e, hp=hp: e.memset(
                                    s_sb[hp][:, 256:384], -1e9))
                                ma = b.dve(lambda e, hp=hp, bS=bS: e.tensor_add(
                                    s_sb[hp][:, 0:256], bk(bS, 0, 256),
                                    msk_b[:, 0:256]))
                            else:
                                ma = b.dve(lambda e, hp=hp, bS=bS: e.tensor_add(
                                    s_sb[hp], bk(bS, 0, 384), msk_b))
                            tok_ma[hp] = ma
                            b.wt("act", ma)
                            b.wt("act", tok_pb[hp])
                            ex = b.act(lambda e, hp=hp: e.activation(
                                s_sb[hp], s_sb[hp], AF.Exp,
                                accum_out=sumc[:, hp:hp + 1]))
                            tok_ex[hp] = ex
                            b.wt("dve", ex)
                            b.dve(lambda e, hp=hp: e.reciprocal(
                                rinv[:, hp:hp + 1], sumc[:, hp:hp + 1]))
                            b.wt("dve", tok_trs[hp])
                            pb = b.dve(lambda e, hp=hp: e.tensor_scalar_mul(
                                p_bf[hp], s_sb[hp], rinv[:, hp:hp + 1]))
                            tok_pb[hp] = pb
                            b.wt("pe", pb)
                            b.wt("pe", tok_pt[hp])
                            trs = None
                            for j in range(3):
                                trs = b.pe(lambda e, j=j, hp=hp, bT=bT:
                                           e.transpose(
                                    bkb(bT, j * 128, (j + 1) * 128),
                                    p_bf[hp][:, j * 128:(j + 1) * 128],
                                    ident_b), inc=(j == 2))
                            tok_trs[hp] = trs
                            b.wt("dve", trs)
                            b.wt("dve", tok_cm[hp])
                            pt = b.dve(lambda e, hp=hp, bT=bT: e.tensor_copy(
                                pT_bf[hp], bkb(bT, 0, 384)))
                            tok_pt[hp] = pt
                            b.wt("pe", pt)
                            b.wt("pe", tok_cev[hp])
                            cm = None
                            for j, (vsl_, vth) in enumerate(vp):
                                cm = b.pe(lambda e, vsl_=vsl_, vth=vth, h=h,
                                          j=j, hp=hp, bC=bC: e.matmul(
                                    bk(bC, 0, 128)[0:64, :], vsl(vsl_, vth, h),
                                    pT_bf[hp][:, j * 128:(j + 1) * 128],
                                    start=(j == 0), stop=(j == 2)),
                                    inc=(j == 2))
                            tok_cm[hp] = cm
                            b.wt("dve", cm)
                            cev = b.dve(lambda e, fcq=fcq, r0=r0, qb=qb,
                                        hp=hp, bC=bC: e.tensor_copy(
                                ctx_bf[r0:r0 + 64, fcq * TT + qb * 128:
                                       fcq * TT + qb * 128 + 128],
                                bk(bC, 0, 128)[0:64, :]))
                            tok_cev[hp] = cev

                def outproj(i, s=s):
                    t0 = s * SEQ + i * TT
                    b.wait("pe", "dve", b.c["dve"])
                    b.wait("act", "pe", b.c["pe"])
                    wo1 = load_wt(w_inT[0], "w_out", 0, "qa", eng="act")
                    wo2 = load_wt(w_inT[1], "w_out", 1, "qb", eng="act")
                    b.wt("pe", wo1)
                    b.wt("pe", wo2)
                    o0 = [b.tok("act"), b.tok("dve")]
                    evo = [o0, o0]
                    for oc in range(FC):
                        par = oc % 2
                        b.wt("pe", evo[par])
                        mmt = None
                        for fc in range(FC):
                            mmt = b.pe(lambda e, oc=oc, fc=fc, par=par:
                                       e.matmul(
                                bk(par, 0, TT),
                                w_inT[oc // 4][:, fc * 512 + (oc % 4) * 128:
                                               fc * 512 + (oc % 4) * 128 + 128],
                                ctx_bf[:, fc * TT:(fc + 1) * TT],
                                start=(fc == 0), stop=(fc == FC - 1)),
                                inc=(fc == FC - 1))
                        b.wt("dve", mmt)
                        b.dve(lambda e, oc=oc, t0=t0, par=par:
                              e.scalar_tensor_tensor(
                            zt32[:, oc * TT:(oc + 1) * TT], bk(par, 0, TT),
                            cols["b_out"][:, oc:oc + 1],
                            ST(oc, t0, TT), op0=ALU.add, op1=ALU.add))
                        evo[par] = b.dve(lambda e, oc=oc, t0=t0, par=par:
                                         e.tensor_add(
                            zt32[:, oc * TT:(oc + 1) * TT],
                            zt32[:, oc * TT:(oc + 1) * TT], ST(oc, t0, TT)))
                    tile_ln(TT, t0)

                qkv(0)
                qkv(1)
                for i in range(TPS):
                    attn(i)
                    outproj(i)
                    if i + 2 < TPS:
                        qkv(i + 2)
            A.pop()
            b.barrier()

        def _ph_l0f():
            TF = 512
            A.push()
            h_buf = A.bf16(32 * TF)
            u_buf = A.bf16(32 * TF)
            wb = [A.bf16(8192), A.bf16(8192)]
            silu_t = [A.f32(TF), A.f32(TF)]
            for ti in range(NTOK // TF):
                t0 = ti * TF
                ld = [None, None]
                c0 = b.tok("pe")
                cons = [c0, c0]
                e0 = [b.tok("act"), b.tok("dve")]
                ev = [e0, e0]

                def loadA(og):
                    p = og % 2
                    b.wt("sp", cons[p])
                    q = "qs" if p == 0 else "qa"
                    load_wt(wb[p][:, 0:4096], "w1", og, q)
                    ld[p] = load_wt(wb[p][:, 4096:8192], "w2", og, q)
                loadA(0)
                loadA(1)
                for og in range(8):
                    p = og % 2
                    for ol in range(4):
                        oc = og * 4 + ol
                        par = oc % 2
                        bA, bB = (0, 1) if par == 0 else (2, 3)
                        b.wt("pe", ld[p])
                        b.wt("pe", ev[par])
                        ta = None
                        for fc in range(FC):
                            ta = b.pe(lambda e, ol=ol, fc=fc, p=p, bA=bA,
                                      t0=t0: e.matmul(
                                bk(bA),
                                wb[p][:, fc * 512 + ol * 128:
                                      fc * 512 + ol * 128 + 128],
                                ST(fc, t0, TF),
                                start=(fc == 0), stop=(fc == FC - 1)),
                                inc=(fc == FC - 1))
                        tb = None
                        for fc in range(FC):
                            tb = b.pe(lambda e, ol=ol, fc=fc, p=p, bB=bB,
                                      t0=t0: e.matmul(
                                bk(bB),
                                wb[p][:, 4096 + fc * 512 + ol * 128:
                                      4096 + fc * 512 + ol * 128 + 128],
                                ST(fc, t0, TF),
                                start=(fc == 0), stop=(fc == FC - 1)),
                                inc=(fc == FC - 1))
                        b.wt("act", ta)
                        sa = b.act(lambda e, oc=oc, par=par, bA=bA:
                                   e.activation(
                            silu_t[par], bk(bA), AF.Silu,
                            scale=cols["w1s"][:, oc:oc + 1]))
                        b.wt("dve", tb)
                        b.wt("dve", sa)
                        hv = b.dve(lambda e, oc=oc, par=par, bB=bB:
                                   e.scalar_tensor_tensor(
                            h_buf[:, oc * TF:(oc + 1) * TF], bk(bB),
                            cols["w2s"][:, oc:oc + 1], silu_t[par],
                            op0=ALU.mult, op1=ALU.mult))
                        ev[par] = [sa, hv]
                    cons[p] = b.tok("pe")
                    if og + 2 < 8:
                        loadA(og + 2)

                def hphase(mat, src_buf, dst_buf, scol, n_og=16):
                    ld = [None, None]
                    c0 = b.tok("pe")
                    cons = [c0, c0]
                    e0 = [b.tok("act"), b.tok("dve")]
                    ev = [e0, e0]

                    def loadH(og):
                        p = og % 2
                        b.wt("sp", cons[p])
                        ld[p] = load_wt(wb[p], mat, og,
                                        "qs" if p == 0 else "qa")
                    loadH(0)
                    loadH(1)
                    for og in range(n_og):
                        p = og % 2
                        for ol in range(2):
                            oc = og * 2 + ol
                            par = oc % 2
                            b.wt("pe", ld[p])
                            b.wt("pe", ev[par])
                            tu = None
                            for ic in range(32):
                                tu = b.pe(lambda e, ol=ol, ic=ic, p=p,
                                          par=par: e.matmul(
                                    bk(par),
                                    wb[p][:, ic * 256 + ol * 128:
                                          ic * 256 + ol * 128 + 128],
                                    src_buf[:, ic * TF:(ic + 1) * TF],
                                    start=(ic == 0), stop=(ic == 31)),
                                    inc=(ic == 31))
                            b.wt("act", tu)
                            ug = b.act(lambda e, oc=oc, par=par: e.activation(
                                dst_buf[:, oc * TF:(oc + 1) * TF], bk(par),
                                AF.Gelu, scale=cols[scol][:, oc:oc + 1]))
                            ev[par] = ug
                        cons[p] = b.tok("pe")
                        if og + 2 < n_og:
                            loadH(og + 2)

                hphase("h1", h_buf, u_buf, "h1s")
                hphase("h2", u_buf, h_buf, "h2s")

                ld = [None, None]
                c0 = b.tok("pe")
                cons = [c0, c0]
                e0 = [b.tok("act"), b.tok("dve")]
                ev = [e0, e0]

                def loadW3(og):
                    p = og % 2
                    b.wt("sp", cons[p])
                    ld[p] = load_wt(wb[p], "w3", og, "qs" if p == 0 else "qa")
                loadW3(0)
                loadW3(1)
                for og in range(4):
                    p = og % 2
                    for ol in range(2):
                        oc = og * 2 + ol
                        par = oc % 2
                        b.wt("pe", ld[p])
                        b.wt("pe", ev[par])
                        ty = None
                        for ic in range(32):
                            ty = b.pe(lambda e, ol=ol, ic=ic, p=p, par=par:
                                      e.matmul(
                                bk(par),
                                wb[p][:, ic * 256 + ol * 128:
                                      ic * 256 + ol * 128 + 128],
                                h_buf[:, ic * TF:(ic + 1) * TF],
                                start=(ic == 0), stop=(ic == 31)),
                                inc=(ic == 31))
                        b.wt("dve", ty)
                        yd = b.dve(lambda e, oc=oc, t0=t0, par=par:
                                   e.scalar_tensor_tensor(
                            zt32[:, oc * TF:(oc + 1) * TF], bk(par),
                            cols["w3s"][:, oc:oc + 1], ST(oc, t0, TF),
                            op0=ALU.mult, op1=ALU.add))
                        ev[par] = yd
                    cons[p] = b.tok("pe")
                    if og + 2 < 4:
                        loadW3(og + 2)
                tile_ln(TF, t0)
            A.pop()
            b.barrier()


        def _ph_l1a():
            A.push()
            pwc_f = A.f32(3 * 128)
            G_tm = A.bf16(4 * 1024)
            GhatT = A.bf16(8 * 512)
            Gh3 = GhatT.rearrange("p (fc g) -> p fc g", fc=FC)
            scr4 = A.f32(1024)
            gh_tmp = A.bf16(1024)
            d_bf = A.bf16(8 * 128)
            s_sb = A.f32(512)
            gw_bf = A.bf16(512)
            gwT_sb = A.bf16(512)
            infl_bf = A.bf16(8 * 128)
            nrm_t = A.bf16(128)
            bl_ring = A.bf16(2 * 1024)
            C_cur = A.bf16(128)
            C_prev = A.bf16(128)
            stg = A.f32(128)
            traj = A.f32(1024)
            cb = A.f32(16)
            (C_MAG, C_E4, C_T, C_MW0, C_MW1, C_MX, C_REC, C_I2, C_I03,
             C_SUM, C_RI, C_W, C_IW, C_S2, C_GN) = range(15)

            def col(i):
                return cb[:, i:i + 1]

            pld = b.dma("pool", lambda e: e.dma_start(out=pwc_f[:, :],
                                                      in_=inp["pwc"]))
            b.pool(lambda e: e.memset(stg, 0.0))
            b.wt("sp", cast_toks["g"])
            gld = b.dma("sp", lambda e: e.dma_start(
                out=G_tm.rearrange("p (gc f) -> p gc f", gc=4),
                in_=cast_bf["g"].rearrange("(gc p) f -> p gc f", p=128)))
            b.wt("act", gld)
            for gc in range(4):
                sq = b.act(lambda e, gc=gc: e.activation(
                    scr4, G_tm[:, gc * 1024:(gc + 1) * 1024], AF.Square,
                    accum_out=col(C_S2)))
                b.wt("act", sq)
                b.act(lambda e: e.activation(col(C_GN), col(C_S2), AF.Ln))
                gn = b.act(lambda e: e.activation(col(C_GN), col(C_GN),
                                                  AF.Exp, scale=0.5))
                b.wt("dve", gn)
                b.dve(lambda e: e.tensor_scalar_max(col(C_MX), col(C_GN), 1e-8))
                rc = b.dve(lambda e: e.reciprocal(col(C_REC), col(C_MX)))
                gh = b.dve(lambda e, gc=gc: e.tensor_scalar_mul(
                    gh_tmp, G_tm[:, gc * 1024:(gc + 1) * 1024],
                    col(C_REC)[:, 0:1]))
                b.wt("pe", gh)
                tk = None
                for fc in range(FC):
                    tk = b.pe(lambda e, fc=fc: e.transpose(
                        bkb(0, fc * 128, (fc + 1) * 128),
                        gh_tmp[:, fc * 128:(fc + 1) * 128], ident_b),
                        inc=(fc == FC - 1))
                b.wt("dve", tk)
                ev = b.dve(lambda e, gc=gc: e.tensor_copy(
                    Gh3[:, :, gc * 128:(gc + 1) * 128],
                    bkb(0).rearrange("p (f g) -> p f g", f=FC)))
                b.wt("pe", ev)
                b.wt("act", ev)
            b.barrier()

            for s in range(NSEQ):
                for ci in range(16):
                    t0 = s * SEQ + ci * 128
                    par = ci % 2
                    nd = 127 if ci == 15 else 128
                    b.wait("dve", "pe", b.c["pe"])
                    dts = []
                    for fc in range(FC):
                        dts.append(b.dve(lambda e, fc=fc, t0=t0, nd=nd:
                                         e.tensor_sub(
                            d_bf[:, fc * 128:fc * 128 + nd],
                            ST(fc, t0 + 1, nd), ST(fc, t0, nd))))
                    if ci == 15:
                        b.wait("pool", "dve", b.c["dve"])
                        for fc in range(FC):
                            b.pool(lambda e, fc=fc: e.memset(
                                d_bf[:, fc * 128 + 127:fc * 128 + 128], 0.0))
                    t_pool = b.tok("pool")
                    b.wt("dve", t_pool)
                    mt = None
                    for fc in range(FC):
                        if mt is not None:
                            b.wt("dve", mt)
                        sq = b.dve(lambda e, fc=fc: e.tensor_mul(
                            scr4[:, 0:128], d_bf[:, fc * 128:(fc + 1) * 128],
                            d_bf[:, fc * 128:(fc + 1) * 128]))
                        b.wt("pe", sq)
                        mt = b.pe(lambda e, fc=fc: e.matmul(
                            bk(0, 0, 128)[0:1, :], ones_cf, scr4[:, 0:128],
                            start=(fc == 0), stop=(fc == FC - 1)), inc=True)
                    b.wt("dve", mt)
                    cp = b.dve(lambda e: e.tensor_copy(stg[0:1, :],
                                                       bk(0, 0, 128)[0:1, :]))
                    b.wt("pe", cp)
                    tr1 = b.pe(lambda e: e.transpose(bk(1, 0, 128), stg,
                                                     ident_f), inc=True)
                    b.wt("act", tr1)
                    b.act(lambda e: e.activation(col(C_MAG), bk(1, 0, 128)[:, 0:1],
                                                 AF.Ln))
                    mg = b.act(lambda e: e.activation(col(C_MAG), col(C_MAG),
                                                      AF.Exp, scale=0.5))
                    e4 = b.act(lambda e: e.activation(col(C_E4), col(C_MAG),
                                                      AF.Exp, scale=4.0))
                    b.wt("dve", e4)
                    b.dve(lambda e: e.tensor_scalar_add(col(C_T), col(C_E4), 1.0))
                    b.dve(lambda e: e.reciprocal(col(C_T), col(C_T)))
                    mwc = col(C_MW0) if par == 0 else col(C_MW1)
                    b.dve(lambda e, mwc=mwc: e.tensor_scalar(
                        mwc, col(C_T), -2.0, 1.0, op0=ALU.mult, op1=ALU.add))
                    b.dve(lambda e: e.tensor_scalar_max(col(C_MX), col(C_MAG),
                                                        1e-6))
                    b.dve(lambda e: e.reciprocal(col(C_REC), col(C_MX)))
                    b.dve(lambda e: e.tensor_scalar_mul(col(C_I2), col(C_REC),
                                                        2.0))
                    iv3 = b.dve(lambda e: e.tensor_scalar_mul(
                        col(C_I03), col(C_REC), 0.6))
                    b.wt("pe", iv3)
                    b.wt("pe", t_pool)
                    dd = None
                    for fc in range(FC):
                        dd = b.pe(lambda e, fc=fc: e.matmul(
                            bk(2, 0, 512), d_bf[:, fc * 128:(fc + 1) * 128],
                            GhatT[:, fc * 512:(fc + 1) * 512],
                            start=(fc == 0), stop=(fc == FC - 1)),
                            inc=(fc == FC - 1))
                    b.wt("dve", dd)
                    sm = b.dve(lambda e: e.tensor_scalar_mul(
                        s_sb, bk(2, 0, 512), col(C_I2)[:, 0:1]))
                    b.wt("act", sm)
                    ex = b.act(lambda e: e.activation(s_sb, s_sb, AF.Exp,
                                                      accum_out=col(C_SUM)))
                    b.wt("dve", ex)
                    b.dve(lambda e: e.reciprocal(col(C_RI), col(C_SUM)))
                    gw = b.dve(lambda e: e.tensor_scalar_mul(
                        gw_bf, s_sb, col(C_RI)[:, 0:1]))
                    b.wt("pe", gw)
                    tk = None
                    for j in range(4):
                        tk = b.pe(lambda e, j=j: e.transpose(
                            bkb(3, j * 128, (j + 1) * 128),
                            gw_bf[:, j * 128:(j + 1) * 128], ident_b),
                            inc=(j == 3))
                    b.wt("dve", tk)
                    gt = b.dve(lambda e: e.tensor_copy(gwT_sb, bkb(3, 0, 512)))
                    b.wt("pe", gt)
                    ift = None
                    for fc in range(FC):
                        for gc in range(4):
                            ift = b.pe(lambda e, fc=fc, gc=gc: e.matmul(
                                bk(4, 0, 128),
                                G_tm[:, gc * 1024 + fc * 128:
                                     gc * 1024 + fc * 128 + 128],
                                gwT_sb[:, gc * 128:(gc + 1) * 128],
                                start=(gc == 0), stop=(gc == 3)),
                                inc=(gc == 3))
                        b.wt("act", ift)
                        fe = b.act(lambda e, fc=fc: e.activation(
                            infl_bf[:, fc * 128:(fc + 1) * 128], bk(4, 0, 128),
                            AF.Identity, scale=0.4))
                        b.wt("pe", fe)
                    for fc in range(FC):
                        trd = b.pe(lambda e, fc=fc: e.transpose(
                            bkb(5, 0, 128), d_bf[:, fc * 128:(fc + 1) * 128],
                            ident_b), inc=True)
                        b.wt("dve", trd)
                        nt = b.dve(lambda e: e.tensor_scalar_mul(
                            nrm_t, bkb(5, 0, 128), col(C_I03)[:, 0:1]))
                        b.wt("pe", nt)
                        tri = b.pe(lambda e, fc=fc: e.transpose(
                            bkb(5, 128, 256), infl_bf[:, fc * 128:(fc + 1) * 128],
                            ident_b), inc=True)
                        b.wt("dve", tri)
                        bl = b.dve(lambda e, fc=fc, par=par: e.tensor_add(
                            bl_ring[:, par * 1024 + fc * 128:
                                    par * 1024 + fc * 128 + 128],
                            nrm_t, bkb(5, 128, 256)))
                        b.wt("pe", bl)
                    b.wt("dve", pld)
                    pv = 0 if ci == 0 else 1
                    cc = b.dve(lambda e, pv=pv, mwc=mwc: e.tensor_scalar_mul(
                        C_cur, pwc_f[:, pv * 128:(pv + 1) * 128], mwc[:, 0:1]))
                    cpv = None
                    if ci > 0:
                        mwp = col(C_MW1) if par == 0 else col(C_MW0)
                        cpv = b.dve(lambda e, mwp=mwp: e.tensor_scalar_mul(
                            C_prev, pwc_f[:, 256:384], mwp[:, 0:1]))
                    b.wt("pe", cc)
                    b.wt("pe", cpv)
                    ws = b.pe(lambda e, ci=ci: e.matmul(
                        bk(0, 0, 128)[0:1, :], ones_cb[:, 0:1], C_cur,
                        start=True, stop=(ci == 0)), inc=(ci == 0))
                    if ci > 0:
                        ws = b.pe(lambda e: e.matmul(
                            bk(0, 0, 128)[0:1, :], ones_cb[:, 0:1], C_prev,
                            start=False, stop=True), inc=True)
                    b.wt("dve", ws)
                    cp2 = b.dve(lambda e: e.tensor_copy(stg[0:1, :],
                                                        bk(0, 0, 128)[0:1, :]))
                    b.wt("pe", cp2)
                    tr2 = b.pe(lambda e: e.transpose(bk(1, 0, 128), stg,
                                                     ident_f), inc=True)
                    b.wt("dve", tr2)
                    b.dve(lambda e: e.tensor_scalar_max(col(C_W),
                                                        bk(1, 0, 128)[:, 0:1],
                                                        1e-8))
                    iw = b.dve(lambda e: e.reciprocal(col(C_IW), col(C_W)))
                    for fh in range(2):
                        tj = None
                        if ci > 0:
                            tj = b.pe(lambda e, fh=fh, par=par: e.matmul(
                                bk(6, 0, 512), C_prev,
                                bl_ring[:, (1 - par) * 1024 + fh * 512:
                                        (1 - par) * 1024 + fh * 512 + 512],
                                start=True, stop=False))
                        tj = b.pe(lambda e, fh=fh, par=par, ci=ci: e.matmul(
                            bk(6, 0, 512), C_cur,
                            bl_ring[:, par * 1024 + fh * 512:
                                    par * 1024 + fh * 512 + 512],
                            start=(ci == 0), stop=True), inc=True)
                        b.wt("dve", tj)
                        b.wt("dve", iw)
                        tv = b.dve(lambda e, fh=fh: e.tensor_scalar_mul(
                            traj[:, fh * 512:(fh + 1) * 512], bk(6, 0, 512),
                            col(C_IW)[:, 0:1]))
                        b.wt("pe", tv)
                    for fc in range(FC):
                        tt = b.pe(lambda e, fc=fc: e.transpose(
                            bk(7, 0, 128), traj[:, fc * 128:(fc + 1) * 128],
                            ident_f), inc=True)
                        b.wt("dve", tt)
                        ad = b.dve(lambda e, fc=fc, t0=t0: e.tensor_add(
                            zt32[:, fc * 128:(fc + 1) * 128],
                            ST(fc, t0, 128), bk(7, 0, 128)))
                        b.wt("pe", ad)
                    tile_ln(128, t0, bkM=2, bkS=3)
            A.pop()
            b.barrier()

        def _ph_l1f():
            A.push()
            oneh = A.f32(1024)
            gwTv = A.bf16(64)
            gbias = A.f32(8)
            A.push()
            gb_row = A.f32(8)
            gw_nat = A.f32(1024)
            A.pop()
            TF = 512
            wp = [A.bf16(8192), A.bf16(8192)]
            h_bf = A.bf16(16 * TF)
            g_bc = A.bf16(8 * TF)
            y_sb = A.f32(8 * TF)
            gat = A.f32(NTILE * 2 * 8)
            gT_sb = A.f32(128)
            silu_t = [A.f32(TF), A.f32(TF)]
            t2 = A.f32(TF)
            s8 = A.f32(8)
            sb8 = A.f32(8)
            i1 = A.f32(8)
            i2 = A.f32(8)
            mk8 = A.f32(8)
            c8 = A.f32(4)
            z_tm = A.f32(1024)
            obuf = A.f32(1024)

            old = b.dma("pool", lambda e: e.dma_start(out=oneh[0:8, :],
                                                      in_=inp["onehot"]))
            l1 = b.dma("sp", lambda e: e.dma_start(
                out=gb_row[0:1, :],
                in_=inp["L1_ffn_gate_bias"].rearrange("(a b) -> a b", a=1)))
            l2 = b.dma("sp", lambda e: e.dma_start(out=gw_nat[0:8, :],
                                                   in_=inp["L1_ffn_gate_w"]))
            b.wt("pe", l1)
            b.wt("pe", l2)
            tk = b.pe(lambda e: e.matmul(bk(1, 0, 8), ones_row[0:1, :],
                                         gb_row[0:1, :], start=True, stop=True),
                      inc=True)
            b.wt("dve", tk)
            cp = b.dve(lambda e: e.tensor_copy(gbias, bk(1, 0, 8)))
            b.wt("pe", l2)
            b.wt("pe", cp)
            for fc in range(FC):
                tk = b.pe(lambda e, fc=fc: e.transpose(
                    bk(2, fc * 8, fc * 8 + 8),
                    gw_nat[0:8, fc * 128:(fc + 1) * 128], ident_f[0:8, 0:8]),
                    inc=(fc == FC - 1))
            b.wt("dve", tk)
            cp2 = b.dve(lambda e: e.tensor_copy(gwTv, bk(2, 0, 64)))
            b.wt("pe", cp2)
            b.barrier()

            for tch in range(NTILE * 2):
                t0 = tch * 128
                lg = None
                for fc in range(FC):
                    lg = b.pe(lambda e, fc=fc, t0=t0: e.matmul(
                        bk(0, 0, 8), ST(fc, t0, 128),
                        gwTv[:, fc * 8:(fc + 1) * 8],
                        start=(fc == 0), stop=(fc == FC - 1)),
                        inc=(fc == FC - 1))
                b.wt("act", lg)
                b.wait("act", "dve", b.c["dve"])
                ex = b.act(lambda e: e.activation(s8, bk(0, 0, 8), AF.Exp,
                                                  accum_out=c8[:, 0:1]))
                b.wt("pe", ex)
                b.wt("dve", ex)
                b.dve(lambda e: e.reciprocal(c8[:, 1:2], c8[:, 0:1]))
                b.dve(lambda e: e.tensor_scalar_mul(s8, s8, c8[:, 1:2]))
                b.dve(lambda e: e.tensor_add(sb8, s8, gbias))
                b.dve(lambda e: e.reduce_max(c8[:, 2:3], sb8, axis=AX.X))
                b.dve(lambda e: e.tensor_scalar(i1, sb8, c8[:, 2:3], None,
                                                op0=ALU.is_ge))
                b.dve(lambda e: e.scalar_tensor_tensor(
                    mk8, i1, -1e9, sb8, op0=ALU.mult, op1=ALU.add))
                b.dve(lambda e: e.reduce_max(c8[:, 3:4], mk8, axis=AX.X))
                b.dve(lambda e: e.tensor_scalar(i2, mk8, c8[:, 3:4], None,
                                                op0=ALU.is_ge))
                b.dve(lambda e: e.tensor_add(i1, i1, i2))
                b.dve(lambda e, tch=tch: e.tensor_mul(
                    gat[:, tch * 8:(tch + 1) * 8], s8, i1))

            prev_store = None
            for ti in range(NTOK // TF):
                t0 = ti * TF
                ct = None
                b.wait("pool", "dve", b.c["dve"])
                ym = b.pool(lambda e: e.memset(y_sb, 0.0))
                for tc in range(4):
                    tch = ti * 4 + tc
                    b.wait("pe", "dve", b.c["dve"])
                    tk = b.pe(lambda e, tch=tch: e.transpose(
                        bk(1, 0, 128)[0:8, :], gat[:, tch * 8:(tch + 1) * 8],
                        ident_f), inc=True)
                    b.wt("dve", tk)
                    cp = b.dve(lambda e: e.tensor_copy(gT_sb[0:8, :],
                                                       bk(1, 0, 128)[0:8, :]))
                    b.wt("pe", cp)
                    for e_ in range(8):
                        bc = b.pe(lambda e, e_=e_: e.matmul(
                            bk(2, 0, 128), oneh[0:8, e_ * 128:(e_ + 1) * 128],
                            gT_sb[0:8, :], start=True, stop=True), inc=True)
                        b.wt("dve", bc)
                        ev = b.dve(lambda e, e_=e_, tc=tc: e.tensor_copy(
                            g_bc[:, e_ * TF + tc * 128:e_ * TF + tc * 128 + 128],
                            bk(2, 0, 128)))
                        b.wt("pe", ev)
                gen_loads = []
                for e_ in range(8):
                    for og in (2 * e_, 2 * e_ + 1):
                        gen_loads.append(lambda buf, q, og=og: (
                            load_wt(buf[:, 0:4096], "ew1", og, q),
                            load_wt(buf[:, 4096:8192], "ew3", og, q))[-1])
                    gen_loads.append(lambda buf, q, e_=e_: (
                        load_wt(buf[:, 0:4096], "ew2", 2 * e_, q),
                        load_wt(buf[:, 4096:8192], "ew2", 2 * e_ + 1, q))[-1])
                for og in range(4):
                    gen_loads.append(lambda buf, q, og=og: (
                        load_wt(buf[:, 0:4096], "sw1", og, q),
                        load_wt(buf[:, 4096:8192], "sw3", og, q))[-1])
                for og in range(2):
                    gen_loads.append(lambda buf, q, og=og:
                                     load_wt(buf, "sw2", og, q))

                pc0 = b.tok("pe")
                pstate = {"ld": [None, None], "cons": [pc0, pc0]}

                def pissue(i):
                    p = i % 2
                    b.wt("sp", pstate["cons"][p])
                    q = "qs" if p == 0 else "qa"
                    pstate["ld"][p] = gen_loads[i](wp[p], q)

                def pconsume(i):
                    b.wt("pe", pstate["ld"][i % 2])
                    return i % 2

                def pdone(i):
                    pstate["cons"][i % 2] = b.tok("pe")
                    if i + 2 < len(gen_loads):
                        pissue(i + 2)

                pissue(0)
                pissue(1)
                gi = 0
                eh0 = [b.tok("act"), b.tok("dve")]
                evh = [eh0, eh0]
                evy = [eh0, eh0]

                def silu_block(p, oc, hslot, gate_e, t0=None):
                    par = oc % 2
                    bA, bB = (0, 1) if par == 0 else (2, 3)
                    b.wt("pe", evh[par])
                    ta = None
                    for fc in range(FC):
                        ta = b.pe(lambda e, oc=oc, fc=fc, p=p, bA=bA, t0=t0:
                                  e.matmul(
                            bk(bA),
                            wp[p][:, fc * 512 + (oc % 4) * 128:
                                  fc * 512 + (oc % 4) * 128 + 128],
                            ST(fc, t0, TF),
                            start=(fc == 0), stop=(fc == FC - 1)),
                            inc=(fc == FC - 1))
                    tb = None
                    for fc in range(FC):
                        tb = b.pe(lambda e, oc=oc, fc=fc, p=p, bB=bB, t0=t0:
                                  e.matmul(
                            bk(bB),
                            wp[p][:, 4096 + fc * 512 + (oc % 4) * 128:
                                  4096 + fc * 512 + (oc % 4) * 128 + 128],
                            ST(fc, t0, TF),
                            start=(fc == 0), stop=(fc == FC - 1)),
                            inc=(fc == FC - 1))
                    b.wt("act", ta)
                    sa = b.act(lambda e, par=par, bA=bA: e.activation(
                        silu_t[par], bk(bA), AF.Silu))
                    b.wt("dve", tb)
                    b.wt("dve", sa)
                    if gate_e is not None:
                        b.dve(lambda e, par=par, bB=bB: e.tensor_mul(
                            t2, silu_t[par], bk(bB)))
                        hv = b.dve(lambda e, hslot=hslot, gate_e=gate_e:
                                   e.tensor_mul(
                            h_bf[:, hslot * TF:(hslot + 1) * TF], t2,
                            g_bc[:, gate_e * TF:(gate_e + 1) * TF]))
                    else:
                        hv = b.dve(lambda e, hslot=hslot, par=par, bB=bB:
                                   e.tensor_mul(
                            h_bf[:, hslot * TF:(hslot + 1) * TF],
                            silu_t[par], bk(bB)))
                    evh[par] = [sa, hv]

                def y_block(p, oc, wbase, n_ic):
                    par = oc % 2
                    b.wt("pe", evy[par])
                    ty = None
                    for ic in range(n_ic):
                        ty = b.pe(lambda e, oc=oc, ic=ic, p=p, par=par,
                                  wbase=wbase, n_ic=n_ic: e.matmul(
                            bk(4 + par),
                            wp[p][:, wbase + ic * 512 + (oc % 4) * 128:
                                  wbase + ic * 512 + (oc % 4) * 128 + 128],
                            h_bf[:, ic * TF:(ic + 1) * TF],
                            start=(ic == 0), stop=(ic == n_ic - 1)),
                            inc=(ic == n_ic - 1))
                    b.wt("dve", ty)
                    b.wt("dve", ym)
                    ya = b.dve(lambda e, oc=oc, par=par: e.tensor_add(
                        y_sb[:, oc * TF:(oc + 1) * TF],
                        y_sb[:, oc * TF:(oc + 1) * TF], bk(4 + par)))
                    evy[par] = ya

                for e_ in range(8):
                    for half in range(2):
                        p = pconsume(gi)
                        for ol in range(4):
                            silu_block(p, half * 4 + ol, half * 4 + ol, e_,
                                       t0)
                        pdone(gi)
                        gi += 1
                    p = pconsume(gi)
                    b.wait("pe", "dve", b.c["dve"])
                    for oc in range(8):
                        y_block(p, oc, (oc // 4) * 4096, FC)
                    pdone(gi)
                    gi += 1
                for og in range(4):
                    p = pconsume(gi)
                    for ol in range(4):
                        silu_block(p, og * 4 + ol, og * 4 + ol, None, t0)
                    pdone(gi)
                    gi += 1
                b.wait("pe", "dve", b.c["dve"])
                for og in range(2):
                    p = pconsume(gi)
                    for ol in range(4):
                        y_block(p, og * 4 + ol, 0, 16)
                    pdone(gi)
                    gi += 1
                zt = None
                for fc in range(FC):
                    zt = b.dve(lambda e, fc=fc, t0=t0: e.tensor_add(
                        zt32[:, fc * TF:(fc + 1) * TF], ST(fc, t0, TF),
                        y_sb[:, fc * TF:(fc + 1) * TF]))
                for tc in range(4):
                    tt0 = t0 + tc * 128
                    bb, pos = divmod(tt0, SEQ)
                    b.wait("pe", "dve", b.c["dve"])
                    b.wt("pe", prev_store)
                    for half in range(2):
                        tk = None
                        for j in range(4):
                            fc = half * 4 + j
                            tk = b.pe(lambda e, fc=fc, tc=tc, half=half, j=j:
                                      e.transpose(
                                bk(half, j * 128, (j + 1) * 128),
                                zt32[:, fc * TF + tc * 128:
                                     fc * TF + tc * 128 + 128],
                                ident_f), inc=(j == 3))
                        b.wt("dve", tk)
                        ev = b.dve(lambda e, half=half: e.tensor_copy(
                            z_tm.rearrange("p (f t) -> p f t", f=FC)
                            [:, half * 4:(half + 1) * 4, :],
                            bk(half).rearrange("p (f t) -> p f t", f=4)))
                        b.wt("pe", ev)
                    b.dve(lambda e: e.reduce_sum(c8[:, 0:1], z_tm, axis=AX.X))
                    b.dve(lambda e: e.tensor_scalar_mul(c8[:, 0:1], c8[:, 0:1],
                                                        1.0 / D))
                    b.dve(lambda e: e.tensor_scalar_sub(z_tm, z_tm,
                                                        c8[:, 0:1]))
                    b.wt("dve", prev_store)
                    b.dve(lambda e: e.tensor_mul(obuf, z_tm, z_tm))
                    b.dve(lambda e: e.reduce_sum(c8[:, 1:2], obuf, axis=AX.X))
                    vr = b.dve(lambda e: e.tensor_scalar_mul(
                        c8[:, 1:2], c8[:, 1:2], 1.0 / D))
                    b.wt("act", vr)
                    b.act(lambda e: e.activation(c8[:, 1:2], c8[:, 1:2],
                                                 AF.Ln, bias=1e-5))
                    rs = b.act(lambda e: e.activation(c8[:, 1:2], c8[:, 1:2],
                                                      AF.Exp, scale=-0.5))
                    b.wt("dve", rs)
                    b.wt("dve", prev_store)
                    ob = b.dve(lambda e: e.tensor_scalar_mul(obuf, z_tm,
                                                             c8[:, 1:2]))
                    b.wait("pool", "dve", b.c["dve"])
                    prev_store = b.dma("pool", lambda e, bb=bb, pos=pos:
                                       e.dma_start(out=out[bb, pos:pos + 128, :],
                                                   in_=obuf))
            A.pop()

        def _ph_store():
            A.push()
            obuf = A.f32(1024)
            prev_store = None
            for ch in range(NTOK // 128):
                bb, pos = divmod(ch * 128, SEQ)
                b.wt("pe", prev_store)
                for half in range(2):
                    tk = None
                    for j in range(4):
                        fc = half * 4 + j
                        tk = b.pe(lambda e, fc=fc, ch=ch, half=half, j=j:
                                  e.transpose(
                            bkb(half, j * 128, (j + 1) * 128),
                            ST(fc, ch * 128, 128), ident_b), inc=(j == 3))
                    b.wt("dve", tk)
                    b.wt("dve", prev_store)
                    ev = b.dve(lambda e, half=half: e.tensor_copy(
                        obuf.rearrange("p (f t) -> p f t", f=FC)
                        [:, half * 4:(half + 1) * 4, :],
                        bkb(half, 0, 512).rearrange("p (f t) -> p f t", f=4)))
                    b.wt("pe", ev)
                b.wait("pool", "dve", b.c["dve"])
                prev_store = b.dma("pool", lambda e, bb=bb, pos=pos:
                                   e.dma_start(out=out[bb, pos:pos + 128, :],
                                               in_=obuf))
            A.pop()

        if "l0a" in phases:
            _ph_l0a()
        if "l0f" in phases:
            _ph_l0f()
        if "l1a" in phases:
            _ph_l1a()
        if "l1f" in phases:
            _ph_l1f()
        else:
            _ph_store()

        final_qp = b.c["qp"]
        b.raw("pool", lambda e: e.wait_ge(sems["qp"], final_qp))
        b.finalize()
    return nc, A.peak


def _host_consts():
    msk = np.full((128, 384), -1e9, np.float32)
    qi = np.arange(128)[:, None]
    j = np.arange(384)[None, :]
    msk[(j >= qi) & (j <= qi + 256)] = 0.0

    S, W = SEQ, 8
    pos = np.arange(S)[:, None]
    k = np.arange(W)[None, :]
    idx = pos - W + k
    valid = (idx >= 0) & (pos >= 1)
    w = np.minimum(pos, W)
    jj = k - (W - w)
    lin = np.where(w > 1, 0.1 + 0.9 * jj / np.maximum(w - 1, 1), 0.1)
    pw = np.where(valid, lin, 0.0).astype(np.float32)

    pwc = np.zeros((128, 3, 128), np.float32)
    i_ = np.arange(128)[:, None]
    j_ = np.arange(128)[None, :]
    kk = i_ - j_ + 8
    m = (kk >= 0) & (kk <= 7)
    kkc = np.clip(kk, 0, 7)
    jb = np.broadcast_to(j_, (128, 128))
    pwc[:, 0] = np.where(m, pw[jb, kkc], 0.0)
    pwc[:, 1] = np.where(m, pw[128 + jb, kkc], 0.0)
    kp = i_ - j_ - 120
    mp = (kp >= 0) & (kp <= 7)
    kpc = np.clip(kp, 0, 7)
    pwc[:, 2] = np.where(mp, pw[128 + jb, kpc], 0.0)

    oneh = np.zeros((8, 8, 128), np.float32)
    for e in range(8):
        oneh[e, e, :] = 1.0
    return msk, pwc.reshape(128, -1), oneh.reshape(8, -1)


KPHASES = ("l0a", "l0f", "l1a", "l1f")
NCORES = 8
_CACHE = {}


def kernel(**inputs):
    key = tuple(KPHASES)
    if key not in _CACHE:
        _CACHE[key] = build_program(KPHASES)[0]
    nc = _CACHE[key]
    msk, pwc, oneh = _host_consts()
    in_maps = []
    for c in range(NCORES):
        m = {"x": np.ascontiguousarray(inputs["L0_attn_x"][2 * c:2 * c + 2])}
        for k in ("L0_attn_w_in", "L0_attn_b_in", "L0_attn_w_out",
                  "L0_attn_b_out", "L0_ffn_w1_w", "L0_ffn_w1_s",
                  "L0_ffn_w2_w", "L0_ffn_w2_s", "L0_ffn_h1_w", "L0_ffn_h1_s",
                  "L0_ffn_h2_w", "L0_ffn_h2_s", "L0_ffn_w3_w", "L0_ffn_w3_s",
                  "L1_ffn_gate_w", "L1_ffn_gate_bias",
                  "L1_ffn_sw1", "L1_ffn_sw3", "L1_ffn_sw2"):
            m[k] = np.ascontiguousarray(inputs[k])
        m["L1_attn_g"] = np.ascontiguousarray(
            inputs["L1_attn_guidance_trajectories"])
        m["L1_ffn_w1"] = np.ascontiguousarray(
            inputs["L1_ffn_w1"].reshape(8192, 1024))
        m["L1_ffn_w3"] = np.ascontiguousarray(
            inputs["L1_ffn_w3"].reshape(8192, 1024))
        m["L1_ffn_w2"] = np.ascontiguousarray(
            inputs["L1_ffn_w2"].reshape(8192, 1024))
        m["msk"] = msk
        m["pwc"] = pwc
        m["onehot"] = oneh
        in_maps.append(m)
    res = run_bass_kernel_spmd(nc, in_maps, core_ids=list(range(NCORES)))
    return np.concatenate([np.asarray(r["out"], np.float32)[None]
                           for r in res.results], axis=0).reshape(
                               2 * NCORES, SEQ, D)
